# version 01
# baseline (speedup 1.0000x reference)
import contextlib
import math
import numpy as np

import concourse.bass as bass
import concourse.mybir as mybir
from concourse.bass import IndirectOffsetOnAxis
from concourse.bass_utils import run_bass_kernel_spmd

F32 = mybir.dt.float32
BF16 = mybir.dt.bfloat16
U32 = mybir.dt.uint32
AF = mybir.ActivationFunctionType
ALU = mybir.AluOpType
AX = mybir.AxisListType

NCORES = 8
T = 16384
TC = 2048
E = 1024
H = 16
D = 64
HALO = 257
EXT = TC + HALO
W = 256
CHUNK = 128
NFT = 8
DEPTH = 6
DI = 2730
NEG = -1e30
RG = [[0, 1, 2, 3, 4, 5, 6, 7]]
ENG_NAMES = ("tensor", "vector", "scalar", "gpsimd", "sync")

CFG = {"gla": True, "nlayers": DEPTH}


class _Tok:
    __slots__ = ("eng", "pos")
    def __init__(self, eng, pos):
        self.eng = eng
        self.pos = pos


class _DmaTok:
    __slots__ = ("sem", "val")
    def __init__(self, sem, val):
        self.sem = sem
        self.val = val


class _SwTok:
    __slots__ = ("prog", "stream", "slot", "sem", "cell")
    def __init__(self, prog, stream, slot, sem, cell):
        self.prog = prog
        self.stream = stream
        self.slot = slot
        self.sem = sem
        self.cell = cell

    def resolve(self):
        if self.cell[0] is None:
            if self.stream["armed"][self.slot] is self.cell:
                self.prog._resolve_sw(self.stream, self.slot, self.sem)
        return self.cell[0]


class _Eng:
    def __init__(self, name):
        self.name = name
        self.ops = []
        self.incs = []
        self.ninc = 0
        self.waited = {}
        self.last_drain = 0

    def milestone_for(self, pos):
        for p, c in self.incs:
            if p >= pos:
                return c
        lp = len(self.ops) - 1
        assert 0 <= pos <= lp, (self.name, pos, lp)
        if self.ops[lp][2]:
            if self.name in ("vector", "scalar", "gpsimd"):
                self.ops.append([lambda q: q.drain(), [], False])
            else:
                self.ops.append([lambda q: q.nop(), [], False])
            lp = len(self.ops) - 1
        self.ninc += 1
        self.incs.append((lp, self.ninc))
        self.ops[lp][1].append(("self", self.ninc))
        return self.ninc


class Prog:
    DMA_POOL = 4

    def __init__(self, nc):
        self.nc = nc
        self.engs = {n: _Eng(n) for n in ENG_NAMES}
        self.writers = {}
        self.readers = {}
        self.streams = {}
        self.sem_handles = {}
        self.fence_toks = []

    @staticmethod
    def _keys(aps, key):
        if key is not None:
            return list(key) if isinstance(key, (list, tuple)) else [key]
        out = []
        for ap in (aps if isinstance(aps, (list, tuple)) else [aps]):
            if ap is None or not hasattr(ap, "tensor"):
                continue
            out.append(ap.tensor.name)
        return out

    def _emit_wait(self, ename, sem, val):
        e = self.engs[ename]
        if e.waited.get(sem, -1) >= val:
            return
        e.waited[sem] = val
        e.ops.append([lambda q, s=sem, v=val: q.wait_ge(self.sem_handles[s], v), [], False])

    def op(self, eng, fn, outs=(), ins=(), okey=None, ikey=None, dma_sem=None,
           no_track=False):
        e = self.engs[eng]
        ik = self._keys(ins, ikey)
        ok = self._keys(outs, okey)
        deps = []
        if not no_track:
            deps.extend(self.fence_toks)
            for k in ik:
                w = self.writers.get(k)
                if w is not None:
                    deps.append(w)
            for k in ok:
                w = self.writers.get(k)
                if w is not None:
                    deps.append(w)
                deps.extend(self.readers.get(k, ()))
        need_drain = False
        for tok in deps:
            if isinstance(tok, _SwTok):
                tok = tok.resolve()
            if isinstance(tok, _DmaTok):
                self._emit_wait(eng, tok.sem, tok.val)
            elif tok.eng is e:
                if tok.pos >= e.last_drain:
                    need_drain = True
            else:
                cnt = tok.eng.milestone_for(tok.pos)
                self._emit_wait(eng, "M_" + tok.eng.name, cnt)
        if need_drain and eng in ("vector", "scalar", "gpsimd"):
            e.last_drain = len(e.ops)
            e.ops.append([lambda q: q.drain(), [], False])
        if dma_sem is not None:
            name, inc = (dma_sem, 16) if not isinstance(dma_sem, tuple) else dma_sem
            sw = eng == "gpsimd"
            name = name + "@" + eng[0]
            stream = self.streams.setdefault(
                name, {"counts": [0] * self.DMA_POOL, "armed": [None] * self.DMA_POOL,
                       "idx": 0, "inc": inc, "sw": sw})
            assert stream["sw"] == sw
            slot = stream["idx"] % self.DMA_POOL
            stream["idx"] += 1
            slot_sem = f"{name}#{slot}"
            if stream["counts"][slot] > 0:
                self._emit_wait(eng, slot_sem, inc * stream["counts"][slot])
        pos = len(e.ops)
        inc_list = []
        e.ops.append([fn, inc_list, dma_sem is not None])
        if dma_sem is not None:
            stream["counts"][slot] += 1
            inc_list.append((slot_sem, inc))
            tok = _DmaTok(slot_sem, inc * stream["counts"][slot])
        else:
            tok = _Tok(e, pos)
        if not no_track:
            for k in ok:
                self.writers[k] = tok
                self.readers[k] = []
            for k in ik:
                self.readers.setdefault(k, []).append(tok)
        return tok

    def _resolve_sw(self, stream, slot, slot_sem):
        g = self.engs["gpsimd"]
        cell = stream["armed"][slot]
        if cell[0] is None:
            g.ops.append([lambda q, s=slot_sem: q.wait_ge(self.sem_handles[s],
                                                          stream["inc"]), [], False])
            pos = len(g.ops)
            g.ops.append([lambda q, s=slot_sem: q.sem_clear(self.sem_handles[s]), [], False])
            cell[0] = _Tok(g, pos)
        stream["armed"][slot] = None
        return cell[0]

    def fence(self):
        toks = []
        for e in self.engs.values():
            if e.ops:
                toks.append(_Tok(e, len(e.ops) - 1))
        for name, stream in self.streams.items():
            for slot in range(self.DMA_POOL):
                if stream["counts"][slot] > 0:
                    toks.append(_DmaTok(f"{name}#{slot}",
                                        stream["inc"] * stream["counts"][slot]))
        self.fence_toks = toks

    def wait_tok(self, eng, tok):
        if isinstance(tok, _SwTok):
            tok = tok.resolve()
        if isinstance(tok, _DmaTok):
            self._emit_wait(eng, tok.sem, tok.val)
        elif tok.eng is self.engs[eng]:
            e = self.engs[eng]
            if tok.pos >= e.last_drain and eng in ("vector", "scalar", "gpsimd"):
                e.last_drain = len(e.ops)
                e.ops.append([lambda q: q.drain(), [], False])
        else:
            cnt = tok.eng.milestone_for(tok.pos)
            self._emit_wait(eng, "M_" + tok.eng.name, cnt)

    def replay(self, block, stack):
        for n in ENG_NAMES:
            self.sem_handles["M_" + n] = stack.enter_context(
                self.nc.semaphore("M_" + n))
        for name, stream in self.streams.items():
            for slot in range(self.DMA_POOL):
                s = f"{name}#{slot}"
                self.sem_handles[s] = stack.enter_context(
                    self.nc.semaphore(s.replace("#", "_").replace("@", "_")))
        prog = self

        def make_runner(ename):
            def run(q):
                for fn, incs, _dma in prog.engs[ename].ops:
                    inst = fn(q)
                    for kind, val in incs:
                        if kind == "self":
                            inst.then_inc(prog.sem_handles["M_" + ename], 1)
                        else:
                            try:
                                inst.then_inc(prog.sem_handles[kind], val)
                            except BaseException as ex:
                                raise RuntimeError(f"then_inc fail {ename} {kind} {val} {inst}") from ex
            return run
        for ename in ENG_NAMES:
            getattr(block, ename)(make_runner(ename))


class Arena:
    def __init__(self, nc, stack, name, nbytes):
        self.t = stack.enter_context(nc.sbuf_tensor(name, [128, nbytes // 4], F32))
        self.nbytes = nbytes
        self.off = 0
        self.marks = []

    def alloc(self, cols, dtype=F32, parts=128):
        esz = {F32: 4, BF16: 2, U32: 4}[dtype]
        nb = (cols * esz + 3) & ~3
        assert self.off + nb <= self.nbytes, (self.off, nb, self.nbytes)
        off = self.off
        self.off += nb
        base = self.t.ap()
        if dtype == BF16:
            return base.bitcast(BF16)[0:parts, off // 2: off // 2 + cols]
        if dtype == U32:
            return base.bitcast(U32)[0:parts, off // 4: off // 4 + cols]
        return base[0:parts, off // 4: off // 4 + cols]

    def push(self):
        self.marks.append(self.off)

    def pop(self):
        self.off = self.marks.pop()


class PsumPool:
    def __init__(self, nc, stack):
        self.ts = [stack.enter_context(nc.psum_tensor(f"psum{i}", [128, 512], F32))
                   for i in range(8)]
        self.aps = [t.ap() for t in self.ts]
        self.free = list(range(8))
        self.rr = 0

    def acquire(self):
        assert self.free, "psum pool exhausted"
        for _ in range(len(self.free)):
            b = self.free.pop(0)
            break
        return b

    def release(self, b):
        self.free.append(b)


CONST_LAYOUT = [
    ("iden_f", 128, F32),
    ("iden_b", 128, BF16),
    ("RT_b", 128, BF16),
    ("bdall", 128, BF16),
    ("hmall", 1024, BF16),
    ("ones_b", 1, BF16),
    ("ones_row", 128, F32),
    ("triu_b", 128, BF16),
    ("mask0", 384, BF16),
    ("mask1", 384, BF16),
    ("maskf0", 384, BF16),
    ("maskf1", 384, BF16),
]
TAB_LAYOUT = [
    ("cosq", EXT, F32),
    ("sinq", EXT, F32),
    ("cosk", EXT, F32),
    ("sink", EXT, F32),
]
CONST_COLS = sum(c for _, c, _ in CONST_LAYOUT) + sum(c for _, c, _ in TAB_LAYOUT)

def _vc_build():
    vc, c = {}, 0
    def take(name, n):
        nonlocal c
        vc[name] = c
        c += n
    take("delta", NFT)
    take("normw", NFT)
    take("halomul", 1)
    take("qs", DEPTH)
    take("ks", DEPTH)
    take("lng", DEPTH * NFT)
    take("lnb", DEPTH * NFT)
    take("b2", DEPTH * NFT)
    take("rkM", NCORES)
    take("rkI", NCORES)
    take("skb2", 1)
    take("c_eps", 1)
    take("c_one", 1)
    take("c_epsD", 1)
    take("c_hlnD", 1)
    take("b1a", DEPTH * 22)
    take("b1g", DEPTH * 22)
    return vc, c

VC, VC_COLS = _vc_build()


class Build:
    def __init__(self, cfg):
        self.cfg = cfg
        self.nc = bass.Bass("TRN2", target_bir_lowering=False, debug=False)
        self.p = Prog(self.nc)
        self.stack = contextlib.ExitStack()
        self.dram = {}
        self._declare_dram()

    def _declare_dram(self):
        nc, dd = self.nc, self.dram
        dd["xs"] = nc.dram_tensor("xs", [EXT, E], F32, kind="ExternalInput").ap()
        for nm, shp in [
            ("attn_Wq", [E, E]), ("attn_Wk", [E, E]), ("attn_Wv", [E, E]),
            ("attn_Wo", [E, E]), ("attn_G1", [E, D]), ("attn_G2", [D, E]),
            ("ffn_Wqkv", [DEPTH, E, 3 * E]), ("ffn_Wo", [DEPTH, E, E]),
            ("ffn_W1", [DEPTH, E, 2 * DI]), ("ffn_W2", [DEPTH, DI, E]),
        ]:
            dd[nm] = nc.dram_tensor(nm, shp, F32, kind="ExternalInput").ap()
        dd["vecs"] = nc.dram_tensor("vecs", [128, VC_COLS], F32,
                                    kind="ExternalInput").ap()
        dd["consts"] = nc.dram_tensor("consts", [128, CONST_COLS], F32,
                                      kind="ExternalInput").ap()
        dd["hidx"] = nc.dram_tensor("hidx", [128, NFT], U32, kind="ExternalInput").ap()
        dd["y"] = nc.dram_tensor("y", [TC, E], F32, kind="ExternalOutput").ap()
        dd["glapark"] = nc.dram_tensor("glapark", [E, TC], F32).ap()
        dd["cin_s"] = nc.dram_tensor("cin_s", [128, 520], F32).ap()
        dd["cout_s"] = nc.dram_tensor("cout_s", [1024, 520], F32).ap()
        dd["cin_h"] = nc.dram_tensor("cin_h", [E, HALO], F32).ap()
        dd["cout_h"] = nc.dram_tensor("cout_h", [NCORES * E, HALO], F32).ap()
        dd["cin_g"] = nc.dram_tensor("cin_g", [E, 1], F32).ap()
        dd["cout_g"] = nc.dram_tensor("cout_g", [NCORES * E, 1], F32).ap()

    def mm(self, out, lhsT, rhs, start=True, stop=True, okey=None, ikey=None, tp=None):
        return self.p.op("tensor",
                         lambda q: q.matmul(out, lhsT, rhs, start=start, stop=stop,
                                            tile_position=tp),
                         outs=[out], ins=[lhsT, rhs], okey=okey, ikey=ikey)

    def trans(self, out, in_, ident, okey=None, ikey=None):
        return self.p.op("tensor", lambda q: q.transpose(out, in_, ident),
                         outs=[out], ins=[in_, ident], okey=okey, ikey=ikey)

    def tt(self, eng, out, a, b, op, okey=None, ikey=None):
        return self.p.op(eng, lambda q: q.tensor_tensor(out, a, b, op),
                         outs=[out], ins=[a, b], okey=okey, ikey=ikey)

    @staticmethod
    def _addkeys(ikey, aps):
        names = [a.tensor.name for a in aps if hasattr(a, "tensor")]
        if not names:
            return ikey
        if ikey is None:
            return None
        base = list(ikey) if isinstance(ikey, (list, tuple)) else [ikey]
        return base + names

    def ts(self, eng, out, in0, s1, s2, op0, op1=ALU.bypass, accum=None,
           okey=None, ikey=None):
        extra = [x for x in (s1, s2) if hasattr(x, "tensor")]
        outs = [out] + ([accum] if accum is not None else [])
        return self.p.op(eng, lambda q: q.tensor_scalar(out, in0, s1, s2, op0, op1,
                                                        accum_out=accum),
                         outs=outs, ins=[in0] + extra, okey=okey,
                         ikey=self._addkeys(ikey, extra))

    def stt(self, eng, out, in0, s, op0, in1, op1, accum=None, okey=None, ikey=None):
        extra = [s] if hasattr(s, "tensor") else []
        outs = [out] + ([accum] if accum is not None else [])
        return self.p.op(eng,
                         lambda q: q.scalar_tensor_tensor(out, in0, s, in1, op0, op1,
                                                          accum_out=accum),
                         outs=outs, ins=[in0, in1] + extra, okey=okey,
                         ikey=self._addkeys(ikey, extra))

    def act(self, out, in_, func, bias=0.0, scale=1.0, accum=None,
            okey=None, ikey=None):
        extra = [x for x in (bias, scale) if hasattr(x, "tensor")]
        outs = [out] + ([accum] if accum is not None else [])
        return self.p.op("scalar",
                         lambda q: q.activation(out, in_, func, bias=bias, scale=scale,
                                                accum_out=accum),
                         outs=outs, ins=[in_] + extra, okey=okey,
                         ikey=self._addkeys(ikey, extra))

    def copy(self, eng, out, in_, okey=None, ikey=None):
        if eng == "scalar":
            return self.p.op(eng, lambda q: q.copy(out, in_), outs=[out], ins=[in_],
                             okey=okey, ikey=ikey)
        return self.p.op(eng, lambda q: q.tensor_copy(out, in_),
                         outs=[out], ins=[in_], okey=okey, ikey=ikey)

    def red(self, out, in_, op, negate=False, okey=None, ikey=None):
        return self.p.op("vector",
                         lambda q: q.tensor_reduce(out, in_, AX.X, op, negate=negate),
                         outs=[out], ins=[in_], okey=okey, ikey=ikey)

    def scan(self, out, d0, op0=ALU.add, okey=None, ikey=None):
        return self.p.op("vector",
                         lambda q: q.tensor_tensor_scan(out, d0, d0, 0.0, op0,
                                                        ALU.bypass),
                         outs=[out], ins=[d0], okey=okey, ikey=ikey)

    def recip(self, out, in_, okey=None, ikey=None):
        return self.p.op("vector", lambda q: q.reciprocal(out, in_),
                         outs=[out], ins=[in_], okey=okey, ikey=ikey)

    def dma(self, eng, out, in_, sem, okey=None, ikey=None, slow=False):
        return self.p.op(eng,
                         lambda q: q.dma_start(out=out, in_=in_,
                                               allow_slow_non_contiguous=slow),
                         outs=[out], ins=[in_], okey=okey, ikey=ikey, dma_sem=sem)

    def gather(self, out, in_, idx, sem, okey=None, ikey=None):
        return self.p.op("gpsimd",
                         lambda q: q.indirect_dma_start(
                             out=out, out_offset=None, in_=in_,
                             in_offset=IndirectOffsetOnAxis(ap=idx, axis=0)),
                         outs=[out], ins=[in_, idx], okey=okey, ikey=ikey, dma_sem=sem)

    def memset(self, eng, ap, val, okey=None):
        return self.p.op(eng, lambda q: q.memset(ap, val), outs=[ap], okey=okey)

    def cc(self, kind, op, cin, cout, okey=None, ikey=None):
        return self.p.op("gpsimd",
                         lambda q: q.collective_compute(kind, op, RG, ins=[cin.opt()],
                                                        outs=[cout.opt()]),
                         outs=[cout], ins=[cin], okey=okey, ikey=ikey,
                         dma_sem=("d_cc", 1))

    def vcol(self, name, i=0):
        return self.vecs[:, VC[name] + i: VC[name] + i + 1]

    def build(self):
        nc, st = self.nc, self.stack
        cfg = self.cfg
        self.rT = [st.enter_context(nc.sbuf_tensor(f"rT{f}", [128, EXT], F32)).ap()
                   for f in range(NFT)]
        self.pp = PsumPool(nc, st)
        ca = Arena(nc, st, "ca", 4 * VC_COLS + 4 * NFT
                   + sum(c * (2 if d == BF16 else 4) + 4 for _, c, d in CONST_LAYOUT) + 64)
        self.ctab = {}
        for nm, cols, dt in CONST_LAYOUT:
            self.ctab[nm] = ca.alloc(cols, dt)
        self.vecs = st.enter_context(nc.sbuf_tensor("vecs_t", [128, VC_COLS], F32)).ap()
        self.hidx = ca.alloc(NFT, U32)
        self.work = Arena(nc, st, "work", 127744)
        self._load_consts()
        self._phase_top()
        self.p.fence()
        if cfg["gla"]:
            self._phase_gla()
        else:
            self._park_zero()
        self.p.fence()
        for l in range(cfg["nlayers"]):
            self._layer_attn(l)
            self._layer_glu(l)
            self.p.fence()
        self._phase_final()
        with st:
            with nc.Block() as block:
                self.p.replay(block, st)
        return nc

    def _load_consts(self):
        dd = self.dram
        c = dd["consts"]
        off = 0
        for nm, cols, dt in CONST_LAYOUT:
            src = c[:, off:off + cols]
            if dt == BF16:
                self.dma("gpsimd", self.ctab[nm], src, "d_const", okey="CONST",
                         slow=(cols < 2))
            else:
                self.dma("sync", self.ctab[nm], src, "d_const", okey="CONST",
                         slow=(cols < 2))
            off += cols
        self.dma("sync", self.vecs, dd["vecs"], "d_const")
        self.dma("gpsimd", self.hidx, dd["hidx"], "d_const", okey="HIDX")

    def _ffn_setup(self):
        if getattr(self, "_ffn_ready", False):
            return
        self._ffn_ready = True
        wk = self.work
        self.fb = {"halo": wk.alloc(HALO)}
        for nm in ("cosq", "sinq", "cosk", "sink"):
            self.fb["t_" + nm] = wk.alloc(512, BF16)
        off = sum(c for _, c, _ in CONST_LAYOUT)
        self.tab_off = {}
        for nm, cols, _ in TAB_LAYOUT:
            self.tab_off[nm] = off
            off += cols
    def _phase_top(self):
        dd = self.dram
        wk = self.work
        wk.push()
        xt = [wk.alloc(E) for _ in range(2)]
        xsq = [wk.alloc(E) for _ in range(2)]
        xn = [wk.alloc(E) for _ in range(2)]
        stat = [wk.alloc(8) for _ in range(2)]
        ntile = (EXT + 127) // 128
        for i in range(ntile):
            rows = min(128, EXT - i * 128)
            b = i % 2
            kx, ks_, kq, kn = f"xt{b}", f"st{b}", f"xq{b}", f"xn{b}"
            x_t = xt[b][0:rows, :]
            self.dma("sync", x_t, dd["xs"][i * 128:i * 128 + rows, :], "d_x", okey=kx)
            mu = stat[b][0:rows, 0:1]
            sq = stat[b][0:rows, 1:2]
            var = stat[b][0:rows, 2:3]
            rstd = stat[b][0:rows, 3:4]
            self.red(mu, x_t, ALU.add, okey=ks_, ikey=kx)
            self.ts("vector", mu, mu, 1.0 / E, None, ALU.mult, okey=ks_, ikey=ks_)
            self.tt("vector", xsq[b][0:rows, :], x_t, x_t, ALU.mult, okey=kq, ikey=kx)
            self.red(sq, xsq[b][0:rows, :], ALU.add, okey=ks_, ikey=kq)
            self.tt("vector", var, mu, mu, ALU.mult, okey=ks_, ikey=ks_)
            self.stt("vector", var, sq, 1.0 / E, ALU.mult, var, ALU.subtract,
                     okey=ks_, ikey=ks_)
            self.act(var, var, AF.Ln, bias=self.vcol("c_eps")[0:rows, :],
                     okey=ks_, ikey=ks_)
            self.act(rstd, var, AF.Exp, scale=-0.5, okey=ks_, ikey=ks_)
            xn_t = xn[b][0:rows, :]
            self.stt("vector", xn_t, x_t, mu, ALU.subtract,
                     rstd.broadcast_to((rows, E)), ALU.mult, okey=kn, ikey=[kx, ks_])
            for f in range(NFT):
                pb = self.pp.acquire()
                ps = self.pp.aps[pb]
                pk = f"PS{pb}"
                self.trans(ps[0:128, 0:rows], xn_t[:, f * 128:(f + 1) * 128],
                           self.ctab["iden_f"][0:rows, 0:rows], okey=pk,
                           ikey=[kn, "CONST"])
                self.copy("vector", self.rT[f][:, i * 128:i * 128 + rows],
                          ps[0:128, 0:rows], okey=f"rT{f}", ikey=pk)
                self.pp.release(pb)
        wk.pop()

    def _park_zero(self):
        wk = self.work
        wk.push()
        z = wk.alloc(512)
        self.memset("vector", z, 0.0, okey="zz")
        for f in range(NFT):
            for c in range(4):
                self.dma("sync",
                         self.dram["glapark"][f * 128:(f + 1) * 128,
                                              c * 512:(c + 1) * 512],
                         z, "d_park", ikey="zz", okey="glapark")
        wk.pop()

    def _phase_gla(self):
        dd = self.dram
        wk = self.work
        wk.push()
        wq = [wk.alloc(E, BF16) for _ in range(NFT)]
        wkk = [wk.alloc(E, BF16) for _ in range(NFT)]
        wv = [wk.alloc(E, BF16) for _ in range(NFT)]
        for f in range(NFT):
            self.dma("gpsimd", wkk[f], dd["attn_Wk"][f * 128:(f + 1) * 128, :],
                     "d_w", okey=f"Wk{f}")
            self.dma("gpsimd", wv[f], dd["attn_Wv"][f * 128:(f + 1) * 128, :],
                     "d_w", okey=f"Wv{f}")
            self.dma("gpsimd", wq[f], dd["attn_Wq"][f * 128:(f + 1) * 128, :],
                     "d_w", okey=f"Wq{f}")
        S = wk.alloc(512)
        Sb = wk.alloc(512, BF16)
        Gtot = wk.alloc(NFT)
        self.memset("vector", S, 0.0, okey="S")
        self.memset("vector", Gtot, 0.0, okey="Gtot")
        NB = 256
        bufs = {
            "G": [wk.alloc(NB) for _ in range(NFT)],
            "k": [wk.alloc(NB, BF16) for _ in range(NFT)],
            "sc1": wk.alloc(NB), "sc2": wk.alloc(NB), "sc3": wk.alloc(CHUNK),
            "kdec": [wk.alloc(CHUNK, BF16) for _ in range(NFT)],
            "kh": [wk.alloc(CHUNK, BF16) for _ in range(NFT)],
            "qt": [wk.alloc(CHUNK, BF16) for _ in range(NFT)],
            "sq": [wk.alloc(NB, BF16) for _ in range(NFT)],
            "strm": [wk.alloc(NB, BF16) for _ in range(NFT)],
            "v": [wk.alloc(E, BF16) for _ in range(NB // 128)],
            "kdT": wk.alloc(E, BF16),
            "Ab": wk.alloc(2048, BF16),
            "gcol": wk.alloc(32),
            "sgate": [wk.alloc(E, BF16) for _ in range(NB // 128)],
            "of": wk.alloc(E),
            "sc64": wk.alloc(D),
            "oh": wk.alloc(E, BF16),
            "ohT": [wk.alloc(NB, BF16) for _ in range(NFT)],
            "msq": wk.alloc(32),
            "g1": wk.alloc(D),
            "g1b": wk.alloc(D, BF16),
            "g1T": wk.alloc(128, BF16, parts=64),
            "glaT": wk.alloc(256),
        }
        self._gla_pass(bufs, (wq, wkk, wv), S, Sb, Gtot, pass_b=False)
        self.p.fence()
        if self.cfg.get("exch", True):
            self._gla_exchange(S, Gtot, wk)
        else:
            self.memset("vector", S, 0.0, okey="S")
        self.p.fence()
        if self.cfg.get("passb", True):
            self._gla_pass(bufs, (wq, wkk, wv), S, Sb, Gtot, pass_b=True)
        wk.pop()

    def _gla_logf_chain(self, zps, zkey, f, ncols, bufs):
        G, k = bufs["G"][f], bufs["k"][f]
        sc1, sc2 = bufs["sc1"], bufs["sc2"]
        va = lambda ap: ap[:, 0:ncols]
        kG, kk = f"G{f}", f"k{f}"
        self.ts("vector", va(sc1), zps, self.vcol("delta", f), None, ALU.add,
                okey="sc1", ikey=[zkey, "vecs_t"])
        self.stt("vector", va(sc2), va(sc1), -1.0, ALU.mult, va(sc1), ALU.max,
                 okey="sc2", ikey="sc1")
        self.act(va(sc2), va(sc2), AF.Exp, scale=-1.0, okey="sc2", ikey="sc2")
        self.act(va(sc2), va(sc2), AF.Ln, bias=self.vcol("c_one"), okey="sc2",
                 ikey="sc2")
        self.stt("vector", va(G), va(sc1), 0.0, ALU.min, va(sc2), ALU.subtract,
                 okey=kG, ikey=["sc1", "sc2"])
        self.act(va(sc2), va(G), AF.Exp, okey="sc2", ikey=kG)
        self.ts("vector", va(k), va(sc2), -1.0, 1.0, ALU.mult, ALU.add,
                okey=kk, ikey="sc2")
        for c in range(ncols // CHUNK):
            s = slice(c * CHUNK, (c + 1) * CHUNK)
            self.scan(G[:, s], G[:, s], okey=kG, ikey=kG)

    def _gla_pass(self, bufs, ws, S, Sb, Gtot, pass_b):
        wq, wkk, wv = ws
        dd = self.dram
        tag = "B" if pass_b else "A"
        wk = self.work
        if pass_b:
            wk.push()
            self._wo = [wk.alloc(E, BF16) for _ in range(NFT)]
            self._g1w = [wk.alloc(D, BF16) for _ in range(NFT)]
            self._g2w = wk.alloc(E, BF16, parts=64)
            for f in range(NFT):
                self.dma("gpsimd", self._wo[f], dd["attn_Wo"][f * 128:(f + 1) * 128, :],
                         "d_w", okey=f"Wo{f}")
                self.ts("vector", self._wo[f], self._wo[f], self.vcol("normw", f),
                        None, ALU.mult, okey=f"Wo{f}", ikey=[f"Wo{f}", "vecs_t"])
                self.dma("gpsimd", self._g1w[f],
                         dd["attn_G1"][f * 128:(f + 1) * 128, :], "d_w", okey=f"G1{f}")
            self.dma("gpsimd", self._g2w, dd["attn_G2"][0:64, :], "d_w", okey="G2w")
        NB = 256
        strm = bufs["strm"]
        for blk in range(self.cfg.get("blocks", TC // NB)):
            c0 = HALO + blk * NB
            for f in range(NFT):
                self.copy("vector", strm[f][:, 0:NB], self.rT[f][:, c0:c0 + NB],
                          okey=f"strm{f}", ikey=f"rT{f}")
            qps = []
            if pass_b:
                for f in range(NFT):
                    pb = self.pp.acquire()
                    ps = self.pp.aps[pb]
                    for kt in range(NFT):
                        self.mm(ps[0:128, 0:NB], wq[kt][:, f * 128:(f + 1) * 128],
                                strm[kt][:, 0:NB],
                                start=(kt == 0), stop=(kt == NFT - 1),
                                okey=f"PS{pb}", ikey=[f"Wq{kt}", f"strm{kt}"])
                    qps.append(pb)
                for f in range(NFT):
                    pb = qps[f]
                    self.act(bufs["sq"][f][:, 0:NB], self.pp.aps[pb][0:128, 0:NB],
                             AF.Silu, okey=f"sq{f}", ikey=f"PS{pb}")
                    self.pp.release(pb)
                for mt in range(NB // 128):
                    pb = self.pp.acquire()
                    ps = self.pp.aps[pb]
                    for kt in range(NFT):
                        self.mm(ps[0:128, 0:64], strm[kt][:, mt * 128:(mt + 1) * 128],
                                self._g1w[kt], start=(kt == 0), stop=(kt == NFT - 1),
                                okey=f"PS{pb}", ikey=[f"strm{kt}", f"G1{kt}"])
                    self.copy("vector", bufs["g1"], ps[0:128, 0:64],
                              okey="g1", ikey=f"PS{pb}")
                    self.pp.release(pb)
                    self.copy("vector", bufs["g1b"], bufs["g1"], okey="g1b", ikey="g1")
                    pb = self.pp.acquire()
                    psb = self.pp.aps[pb].bitcast(BF16)
                    self.trans(psb[0:64, 0:128], bufs["g1b"], self.ctab["iden_b"],
                               okey=f"PS{pb}", ikey=["g1b", "CONST"])
                    self.copy("vector", bufs["g1T"], psb[0:64, 0:128],
                              okey="g1T", ikey=f"PS{pb}")
                    self.pp.release(pb)
                    for half in range(2):
                        pb = self.pp.acquire()
                        ps = self.pp.aps[pb]
                        self.mm(ps, bufs["g1T"], self._g2w[0:64, half * 512:half * 512 + 512],
                                okey=f"PS{pb}", ikey=["g1T", "G2w"])
                        self.act(bufs["sgate"][mt][:, half * 512:half * 512 + 512], ps,
                                 AF.Sigmoid, okey=f"sgate{mt}", ikey=f"PS{pb}")
                        self.pp.release(pb)
            if self.cfg.get("glastop", 9) < 2:
                continue
            for f in range(NFT):
                pb = self.pp.acquire()
                ps = self.pp.aps[pb]
                for kt in range(NFT):
                    self.mm(ps[0:128, 0:NB], wkk[kt][:, f * 128:(f + 1) * 128],
                            strm[kt][:, 0:NB],
                            start=(kt == 0), stop=(kt == NFT - 1),
                            okey=f"PS{pb}", ikey=[f"Wk{kt}", f"strm{kt}"])
                self._gla_logf_chain(ps[0:128, 0:NB], f"PS{pb}", f, NB, bufs)
                self.pp.release(pb)
            if self.cfg.get("glastop", 9) < 3:
                continue
            for mt in range(NB // 128):
                for half in range(2):
                    pb = self.pp.acquire()
                    ps = self.pp.aps[pb]
                    for kt in range(NFT):
                        self.mm(ps, strm[kt][:, mt * 128:(mt + 1) * 128],
                                wv[kt][:, half * 512:half * 512 + 512],
                                start=(kt == 0), stop=(kt == NFT - 1),
                                okey=f"PS{pb}", ikey=[f"strm{kt}", f"Wv{kt}"])
                    self.copy("scalar", bufs["v"][mt][:, half * 512:half * 512 + 512],
                              ps, okey=f"v{mt}", ikey=f"PS{pb}")
                    self.pp.release(pb)
            if self.cfg.get("glastop", 9) < 4:
                continue
            for cc in range(NB // CHUNK):
                self._gla_chunk(bufs, S, Sb, Gtot, blk, cc, pass_b)
        if pass_b:
            wk.pop()

    def _gla_chunk(self, bufs, S, Sb, Gtot, blk, cc, pass_b):
        cs = slice(cc * CHUNK, (cc + 1) * CHUNK)
        gcol = bufs["gcol"]
        sc3 = bufs["sc3"]
        for f in range(NFT):
            G = bufs["G"][f]
            GC = gcol[:, f:f + 1]
            self.copy("vector", GC, G[:, cc * CHUNK + 127:cc * CHUNK + 128],
                      okey="gcol", ikey=f"G{f}")
            self.ts("vector", sc3[:, 0:CHUNK], G[:, cs], -1.0, GC, ALU.mult, ALU.add,
                    okey="sc3", ikey=[f"G{f}", "gcol"])
            self.act(sc3[:, 0:CHUNK], sc3[:, 0:CHUNK], AF.Exp, okey="sc3", ikey="sc3")
            self.tt("vector", bufs["kdec"][f][:, 0:CHUNK], bufs["k"][f][:, cs],
                    sc3[:, 0:CHUNK], ALU.mult, okey=f"kdec{f}", ikey=[f"k{f}", "sc3"])
            if pass_b:
                Rc = gcol[:, 8 + f:9 + f]
                self.copy("vector", Rc, G[:, cc * CHUNK + 63:cc * CHUNK + 64],
                          okey="gcol", ikey=f"G{f}")
                self.ts("vector", sc3[:, 0:CHUNK], G[:, cs], Rc, None, ALU.subtract,
                        okey="sc3", ikey=[f"G{f}", "gcol"])
                self.act(sc3[:, 0:CHUNK], sc3[:, 0:CHUNK], AF.Exp, okey="sc3", ikey="sc3")
                self.tt("vector", bufs["qt"][f][:, 0:CHUNK], bufs["sq"][f][:, cs],
                        sc3[:, 0:CHUNK], ALU.mult, okey=f"qt{f}", ikey=[f"sq{f}", "sc3"])
                self.ts("vector", sc3[:, 0:CHUNK], G[:, cs], -1.0, Rc, ALU.mult, ALU.add,
                        okey="sc3", ikey=[f"G{f}", "gcol"])
                self.act(sc3[:, 0:CHUNK], sc3[:, 0:CHUNK], AF.Exp, okey="sc3", ikey="sc3")
                self.tt("vector", bufs["kh"][f][:, 0:CHUNK], bufs["k"][f][:, cs],
                        sc3[:, 0:CHUNK], ALU.mult, okey=f"kh{f}", ikey=[f"k{f}", "sc3"])
                eR = gcol[:, 16 + f:17 + f]
                self.act(eR, Rc, AF.Exp, okey="gcol", ikey="gcol")
                self.ts("vector", Sb[:, f * 64:(f + 1) * 64], S[:, f * 64:(f + 1) * 64],
                        eR, None, ALU.mult, okey="Sb", ikey=["S", "gcol"])
        if self.cfg.get("glastop", 9) < 5:
            return
        vtile = bufs["v"][cc]
        opair = []
        if pass_b:
            po = [self.pp.acquire(), self.pp.acquire()]
            for h in range(H):
                f, half = h // 2, h % 2
                pbase = half * 64
                pa = self.pp.acquire()
                aps = self.pp.aps[pa]
                self.mm(aps[0:128, 0:128], bufs["kh"][f][pbase:pbase + 64, 0:CHUNK],
                        bufs["qt"][f][pbase:pbase + 64, 0:CHUNK],
                        okey=f"PS{pa}", ikey=[f"kh{f}", f"qt{f}"], tp=(pbase, 0))
                ab = bufs["Ab"][:, h * 128:(h + 1) * 128]
                self.tt("vector", ab, aps[0:128, 0:128], self.ctab["triu_b"], ALU.mult,
                        okey="Ab", ikey=[f"PS{pa}", "CONST"])
                self.pp.release(pa)
                od = self.pp.aps[po[h // 8]][0:128, (h % 8) * 64:(h % 8) * 64 + 64]
                self.mm(od, ab, vtile[:, h * 64:(h + 1) * 64], start=True, stop=False,
                        okey=f"PS{po[h//8]}", ikey=["Ab", f"v{cc}"])
                self.mm(od, bufs["qt"][f][pbase:pbase + 64, 0:CHUNK],
                        Sb[pbase:pbase + 64, f * 64:(f + 1) * 64],
                        start=False, stop=True,
                        okey=f"PS{po[h//8]}", ikey=[f"qt{f}", "Sb"], tp=(pbase, 0))
            opair = po
        if self.cfg.get("glastop", 9) < 6:
            return
        pk = self.pp.acquire()
        pkb = self.pp.aps[pk].bitcast(BF16)
        kdT = bufs["kdT"]
        for f in range(NFT):
            slot = (f % 4) * 128
            self.trans(pkb[0:128, slot:slot + 128], bufs["kdec"][f][:, 0:CHUNK],
                       self.ctab["iden_b"], okey=f"PS{pk}",
                       ikey=[f"kdec{f}", "CONST"])
            self.copy("vector", kdT[:, f * 128:(f + 1) * 128], pkb[0:128, slot:slot + 128],
                      okey="kdT", ikey=f"PS{pk}")
        self.pp.release(pk)
        if self.cfg.get("glastop", 9) < 7:
            self.pp.release(pk) if False else None
            return
        pS = self.pp.acquire()
        psS = self.pp.aps[pS]
        for h in range(H):
            f, half = h // 2, h % 2
            self.mm(psS[half * 64:half * 64 + 64, f * 64:(f + 1) * 64],
                    kdT[:, h * 64:(h + 1) * 64], vtile[:, h * 64:(h + 1) * 64],
                    tp=(0, half * 64), okey=f"PS{pS}", ikey=["kdT", f"v{cc}"])
        if self.cfg.get("glastop", 9) < 8:
            self.pp.release(pS)
            return
        for f in range(NFT):
            PcC = gcol[:, 24 + f:25 + f]
            self.act(PcC, gcol[:, f:f + 1], AF.Exp, okey="gcol", ikey="gcol")
            self.stt("vector", S[:, f * 64:(f + 1) * 64], S[:, f * 64:(f + 1) * 64],
                     PcC, ALU.mult, psS[:, f * 64:(f + 1) * 64], ALU.add,
                     okey="S", ikey=["S", "gcol", f"PS{pS}"])
        self.pp.release(pS)
        self.tt("vector", Gtot, Gtot, gcol[:, 0:NFT], ALU.add,
                okey="Gtot", ikey=["Gtot", "gcol"])
        if pass_b:
            self._gla_out(opair, bufs, blk, cc)
            for b in opair:
                self.pp.release(b)

    def _gla_out(self, po, bufs, blk, cc):
        of, oh = bufs["of"], bufs["oh"]
        sg = bufs["sgate"][cc]
        self.tt("vector", of[:, 0:512], self.pp.aps[po[0]], sg[:, 0:512], ALU.mult,
                okey="of", ikey=[f"PS{po[0]}", f"sgate{cc}"])
        self.tt("vector", of[:, 512:1024], self.pp.aps[po[1]], sg[:, 512:1024],
                ALU.mult, okey="of", ikey=[f"PS{po[1]}", f"sgate{cc}"])
        self.act(oh, of, AF.Square, okey="oh", ikey="of")
        msq = bufs["msq"][:, 0:16]
        rstd = bufs["msq"][:, 16:32]
        self.red(msq, oh.rearrange("p (h d) -> p h d", d=D), ALU.add,
                 okey="msq", ikey="oh")
        self.act(msq, msq, AF.Ln, bias=self.vcol("c_epsD"), okey="msq", ikey="msq")
        self.act(rstd, msq, AF.Exp, scale=-0.5, bias=self.vcol("c_hlnD"),
                 okey="msq", ikey="msq")
        self.tt("vector", oh, of,
                rstd.rearrange("p (h o) -> p h o", o=1).broadcast_to((128, H, D)), ALU.mult,
                okey="oh", ikey=["of", "msq"])
        for f in range(NFT):
            pb = self.pp.acquire()
            psb = self.pp.aps[pb].bitcast(BF16)
            self.trans(psb[0:128, 0:128], oh[:, f * 128:(f + 1) * 128],
                       self.ctab["iden_b"], okey=f"PS{pb}", ikey=["oh", "CONST"])
            self.copy("vector", bufs["ohT"][f][:, cc * 128:(cc + 1) * 128],
                      psb[0:128, 0:128], okey=f"ohT{f}", ikey=f"PS{pb}")
            self.pp.release(pb)
        if cc == 1:
            glaT = bufs["glaT"]
            for mt in range(NFT):
                pb = self.pp.acquire()
                ps = self.pp.aps[pb]
                for kt in range(NFT):
                    self.mm(ps[0:128, 0:256], self._wo[kt][:, mt * 128:(mt + 1) * 128],
                            bufs["ohT"][kt][:, 0:256], start=(kt == 0),
                            stop=(kt == NFT - 1),
                            okey=f"PS{pb}", ikey=[f"Wo{kt}", f"ohT{kt}"])
                self.copy("vector", glaT, ps[0:128, 0:256],
                          okey="glaT", ikey=f"PS{pb}")
                self.pp.release(pb)
                self.dma("sync",
                         self.dram["glapark"][mt * 128:(mt + 1) * 128,
                                              blk * 256:(blk + 1) * 256],
                         glaT, "d_park", ikey="glaT", okey="glapark")

    def _gla_exchange(self, S, Gtot, wk):
        dd = self.dram
        wk.push()
        pack = wk.alloc(520)
        self.copy("vector", pack[:, 0:512], S, okey="pack", ikey="S")
        self.act(pack[:, 512:520], Gtot, AF.Exp, okey="pack", ikey="Gtot")
        self.dma("sync", dd["cin_s"], pack, "d_cc2", ikey="pack", okey="cin_s")
        self.cc("AllGather", ALU.bypass, dd["cin_s"], dd["cout_s"],
                okey="cout_s", ikey="cin_s")
        self.memset("vector", S, 0.0, okey="S")
        rbuf = wk.alloc(520)
        pr = wk.alloc(NFT)
        rs = wk.alloc(512)
        for r in range(NCORES):
            self.dma("sync", rbuf, dd["cout_s"][r * 128:(r + 1) * 128, :], "d_cc2",
                     okey="rbuf", ikey="cout_s")
            self.stt("vector", pr, rbuf[:, 512:520], self.vcol("rkM", r), ALU.mult,
                     self.vcol("rkI", r).broadcast_to((128, NFT)), ALU.add,
                     okey="pr", ikey=["rbuf", "vecs_t"])
            self.ts("vector", rs, rbuf[:, 0:512], self.vcol("rkM", r), None, ALU.mult,
                    okey="rs", ikey=["rbuf", "vecs_t"])
            for f in range(NFT):
                self.stt("vector", S[:, f * 64:(f + 1) * 64], S[:, f * 64:(f + 1) * 64],
                         pr[:, f:f + 1], ALU.mult, rs[:, f * 64:(f + 1) * 64], ALU.add,
                         okey="S", ikey=["S", "pr", "rs"])
        wk.pop()

    def _ffn_setup(self):
        if getattr(self, "_ffn_ready", False):
            return
        self._ffn_ready = True
        wk = self.work
        self.fb = {"halo": wk.alloc(HALO)}
        for nm in ("cosq", "sinq", "cosk", "sink"):
            self.fb["t_" + nm] = wk.alloc(512, BF16)
        off = sum(c for _, c, _ in CONST_LAYOUT)
        self.tab_off = {}
        for nm, cols, _ in TAB_LAYOUT:
            self.tab_off[nm] = off
            off += cols

    def _attn_frame(self):
        wk = self.work
        fb = dict(self.fb)
        fb["wqkv"] = [wk.alloc(3 * E, BF16) for _ in range(NFT)]
        fb["wo"] = [wk.alloc(E, BF16) for _ in range(NFT)]
        fb["strm"] = [wk.alloc(257, BF16) for _ in range(NFT)]
        fb["xb"] = wk.alloc(W, BF16)
        fb["q2"] = wk.alloc(W, BF16)
        fb["t2"] = wk.alloc(W, BF16)
        fb["qt"] = [wk.alloc(W, BF16) for _ in range(NFT)]
        fb["kring"] = [[wk.alloc(W, BF16) for _ in range(NFT)] for _ in range(2)]
        fb["vring"] = [[wk.alloc(E, BF16) for _ in range(2)] for _ in range(2)]
        fb["prevk"] = [wk.alloc(W, BF16) for _ in range(NFT)]
        fb["sims"] = wk.alloc(2 * 384)
        fb["P"] = wk.alloc(2 * 384, BF16)
        fb["PTs"] = wk.alloc(1024, BF16)
        fb["attnT"] = [wk.alloc(W, BF16) for _ in range(NFT)]
        fb["rqT"] = [wk.alloc(16) for _ in range(16)]
        fb["rn"] = wk.alloc(256)
        fb["rnb"] = wk.alloc(256, BF16)
        fb["cols"] = wk.alloc(8)
        fb["oldc"] = [wk.alloc(1) for _ in range(NFT)]
        return fb

    GRP = [(0, 6), (6, 12), (12, 18), (18, 22)]

    def _glu_frame(self):
        wk = self.work
        fb = dict(self.fb)
        fb["rsh"] = [wk.alloc(512, BF16) for _ in range(NFT)]
        fb["sq"] = [wk.alloc(512, BF16) for _ in range(NFT)]
        fb["xnT"] = [wk.alloc(512, BF16) for _ in range(NFT)]
        fb["w1g"] = [[[wk.alloc(768, BF16) for _ in range(NFT)] for _ in range(2)]
                     for _ in range(2)]
        fb["w2t"] = [wk.alloc(E, BF16) for _ in range(4)]
        fb["ht"] = [wk.alloc(512, BF16) for _ in range(22)]
        fb["gel"] = wk.alloc(512)
        fb["srow"] = wk.alloc(3 * 512)
        fb["oldg"] = [wk.alloc(3) for _ in range(NFT)]
        return fb

    def _halo_in(self, l, kind):
        fb = self.fbl
        dd = self.dram
        cout = dd["cout_h"] if kind == "a" else dd["cout_g"]
        ncol = HALO if kind == "a" else 1
        dst0 = 0 if kind == "a" else 256
        for f in range(NFT):
            g = self.gather(fb["halo"][:, 0:ncol], cout[:, 0:ncol],
                            self.hidx[:, f:f + 1], "d_gath",
                            okey="HALO", ikey=[f"cout_{kind}", "HIDX"])
            self.ts("vector", self.rT[f][:, dst0:dst0 + ncol], fb["halo"][:, 0:ncol],
                    self.vcol("halomul"), None, ALU.mult,
                    okey=f"rT{f}", ikey=["HALO"])

    def _halo_out(self, l, kind):
        dd = self.dram
        if kind == "a":
            cin, cout, c0, ncol = dd["cin_h"], dd["cout_h"], EXT - HALO, HALO
        else:
            cin, cout, c0, ncol = dd["cin_g"], dd["cout_g"], EXT - 1, 1
        for f in range(NFT):
            self.dma("sync", cin[f * 128:(f + 1) * 128, :],
                     self.rT[f][:, c0:c0 + ncol], "d_cc2",
                     okey=f"cin_{kind}", ikey=f"rT{f}", slow=(ncol < 2))
        self.cc("AllGather", ALU.bypass, cin, cout,
                okey=f"cout_{kind}", ikey=f"cin_{kind}")

    def _attn_proj_block(self, l, c0, n, is_halo):
        fb = self.fbl
        ct = self.ctab
        dd = self.dram
        blkw0 = (c0 - HALO) // W
        tabn = ("cosk", "sink") if is_halo else ("cosk", "sink", "cosq", "sinq")
        for nm in tabn:
            self.dma("gpsimd", fb["t_" + nm][:, 0:n],
                     dd["consts"][:, self.tab_off[nm] + c0:self.tab_off[nm] + c0 + n],
                     "d_tab", okey=f"ftab_{nm}")
        use_oldc = c0 > 257
        for f in range(NFT):
            if use_oldc:
                self.copy("vector", fb["strm"][f][:, 0:1], fb["oldc"][f],
                          okey=f"fstrm{f}", ikey=f"foldc{f}")
                self.copy("vector", fb["strm"][f][:, 1:n + 1],
                          self.rT[f][:, c0:c0 + n], okey=f"fstrm{f}", ikey=f"rT{f}")
            else:
                self.copy("vector", fb["strm"][f][:, 0:n + 1],
                          self.rT[f][:, c0 - 1:c0 + n], okey=f"fstrm{f}", ikey=f"rT{f}")
        def rhs(kt, a, b):
            off = 1 if kt < 4 else 0
            return fb["strm"][kt][:, a + off:b + off]
        for part, tabs, scol in (("k", ("cosk", "sink"), "ks"),
                                 ("q", ("cosq", "sinq"), "qs")):
            if part == "q" and is_halo:
                continue
            sspb = self.pp.acquire()
            ssps = self.pp.aps[sspb]
            for f in range(NFT):
                pb = self.pp.acquire()
                ps = self.pp.aps[pb]
                for kt in range(NFT):
                    self.mm(ps[0:128, 0:n],
                            fb["wqkv"][kt][:, (0 if part == "q" else E) + f * 128:
                                           (0 if part == "q" else E) + f * 128 + 128],
                            rhs(kt, 0, n), start=(kt == 0), stop=(kt == NFT - 1),
                            okey=f"PS{pb}", ikey=[f"fw{kt}", f"fstrm{kt}"])
                self.act(fb["q2"][:, 0:n], ps[0:128, 0:n], AF.Square,
                         okey="fq2", ikey=f"PS{pb}")
                self.act(fb["xb"][:, 0:n], ps[0:128, 0:n], AF.Copy,
                         scale=self.vcol(scol, l), okey="fxb", ikey=f"PS{pb}")
                self.pp.release(pb)
                self.mm(ssps[0:16, 0:n], ct["bdall"][:, f * 16:(f + 1) * 16],
                        fb["q2"][:, 0:n], start=(f == 0), stop=(f == NFT - 1),
                        okey=f"PS{sspb}", ikey=["ca", "fq2"])
                pr = self.pp.acquire()
                prs = self.pp.aps[pr]
                self.mm(prs[0:128, 0:n], ct["RT_b"], fb["xb"][:, 0:n],
                        okey=f"PS{pr}", ikey=["ca", "fxb"])
                self.tt("vector", fb["q2"][:, 0:n], fb["xb"][:, 0:n],
                        fb["t_" + tabs[0]][:, 0:n], ALU.mult, okey="fq2",
                        ikey=["fxb", f"ftab_{tabs[0]}"])
                self.stt("vector", fb["t2"][:, 0:n], prs[0:128, 0:n], 0.0, ALU.add,
                         fb["t_" + tabs[1]][:, 0:n], ALU.mult, okey="ft2",
                         ikey=[f"PS{pr}", f"ftab_{tabs[1]}"])
                self.pp.release(pr)
                if part == "q":
                    self.tt("vector", fb["qt"][f][:, 0:n], fb["q2"][:, 0:n],
                            fb["t2"][:, 0:n], ALU.add, okey=f"fqt{f}",
                            ikey=["fq2", "ft2"])
                else:
                    self.tt("vector", fb["xb"][:, 0:n], fb["q2"][:, 0:n],
                            fb["t2"][:, 0:n], ALU.add, okey="fxb",
                            ikey=["fq2", "ft2"])
                    self._krope_pending = True
                    for i in range(n // W):
                        w = blkw0 + i
                        self.copy("vector", fb["kring"][w % 2][f][:, 0:W],
                                  fb["xb"][:, i * W:(i + 1) * W],
                                  okey=f"fkr{w%2}_{f}", ikey="fxb")
            self.ts("vector", fb["rn"][0:16, 0:n], ssps[0:16, 0:n], 1e-24, None,
                    ALU.max, okey="frn", ikey=f"PS{sspb}")
            self.pp.release(sspb)
            self.act(fb["rn"][0:16, 0:n], fb["rn"][0:16, 0:n], AF.Ln,
                     okey="frn", ikey="frn")
            self.act(fb["rn"][0:16, 0:n], fb["rn"][0:16, 0:n], AF.Exp, scale=-0.5,
                     okey="frn", ikey="frn")
            if part == "q":
                for i in range(n // 128):
                    tt = (c0 - HALO) // 128 + i
                    pb = self.pp.acquire()
                    ps = self.pp.aps[pb]
                    self.trans(ps[0:128, 0:16], fb["rn"][0:16, i * 128:(i + 1) * 128],
                               ct["iden_f"][0:16, 0:16], okey=f"PS{pb}",
                               ikey=["frn", "ca"])
                    self.copy("vector", fb["rqT"][tt], ps[0:128, 0:16],
                              okey=f"frqT{tt}", ikey=f"PS{pb}")
                    self.pp.release(pb)
            else:
                self.copy("vector", fb["rnb"][0:16, 0:n], fb["rn"][0:16, 0:n],
                          okey="frnb", ikey="frn")
                for f in range(NFT):
                    pb = self.pp.acquire()
                    ps = self.pp.aps[pb]
                    self.mm(ps[0:128, 0:n], ct["hmall"][0:16, f * 128:(f + 1) * 128],
                            fb["rnb"][0:16, 0:n], okey=f"PS{pb}", ikey=["ca", "frnb"])
                    for i in range(n // W):
                        w = blkw0 + i
                        ring = fb["kring"][w % 2][f]
                        self.tt("vector", ring, ring,
                                ps[0:128, i * W:(i + 1) * W], ALU.mult,
                                okey=f"fkr{w%2}_{f}", ikey=[f"fkr{w%2}_{f}", f"PS{pb}"])
                    self.pp.release(pb)
        for i in range(n // 128):
            w = blkw0 + (i // 2)
            vt = fb["vring"][w % 2][i % 2]
            for half in range(2):
                pb = self.pp.acquire()
                ps = self.pp.aps[pb]
                for kt in range(NFT):
                    self.mm(ps, rhs(kt, i * 128, (i + 1) * 128),
                            fb["wqkv"][kt][:, 2 * E + half * 512:2 * E + half * 512 + 512],
                            start=(kt == 0), stop=(kt == NFT - 1),
                            okey=f"PS{pb}", ikey=[f"fstrm{kt}", f"fw{kt}"])
                self.copy("scalar", vt[:, half * 512:half * 512 + 512], ps,
                          okey=f"fv{w%2}_{i%2}", ikey=f"PS{pb}")
                self.pp.release(pb)

    def _attn_window(self, l, w):
        fb = self.fbl
        ct = self.ctab
        for f in range(NFT):
            kp = fb["kring"][(w - 1) % 2][f]
            self.ts("vector", fb["prevk"][f], kp, self.vcol("skb2"), None, ALU.mult,
                    okey=f"fpk{f}", ikey=[f"fkr{(w-1)%2}_{f}"])
        qslot = 0
        for h in range(H):
            f, half = h // 2, h % 2
            pbq = half * 64
            simbanks = []
            for rt in range(2):
                pb = self.pp.acquire()
                ps = self.pp.aps[pb]
                qlhs = fb["qt"][f][pbq:pbq + 64,
                                   qslot + rt * 128:qslot + rt * 128 + 128]
                pk = fb["prevk"][f][pbq:pbq + 64, :]
                kc = fb["kring"][w % 2][f][pbq:pbq + 64, :]
                if rt == 0:
                    self.mm(ps[0:128, 0:256], qlhs, pk, okey=f"PS{pb}",
                            ikey=[f"fqt{f}", f"fpk{f}"], tp=(pbq, 0))
                    self.mm(ps[0:128, 256:384], qlhs, kc[:, 0:128], okey=f"PS{pb}",
                            ikey=[f"fqt{f}", f"fkr{w%2}_{f}"], tp=(pbq, 0))
                else:
                    self.mm(ps[0:128, 0:128], qlhs, pk[:, 128:256], okey=f"PS{pb}",
                            ikey=[f"fqt{f}", f"fpk{f}"], tp=(pbq, 0))
                    self.mm(ps[0:128, 128:384], qlhs, kc, okey=f"PS{pb}",
                            ikey=[f"fqt{f}", f"fkr{w%2}_{f}"], tp=(pbq, 0))
                simbanks.append(pb)
            ssc = fb["cols"]
            for rt in range(2):
                pb = simbanks[rt]
                ps = self.pp.aps[pb]
                mk = "maskf" if (w == 0 and l >= 0) else "mask"
                mname = f"{mk}{rt}" if w == 0 else f"mask{rt}"
                self.tt("vector", fb["sims"][:, rt * 384:rt * 384 + 384],
                        ps[0:128, 0:384], ct[mname], ALU.add,
                        okey="fsims", ikey=[f"PS{pb}", "ca"])
                self.pp.release(pb)
                self.red(ssc[:, rt:rt + 1], fb["sims"][:, rt * 384:rt * 384 + 384],
                         ALU.max, negate=True, okey="fcols", ikey="fsims")
                rq = fb["rqT"][w * 2 + rt][:, h:h + 1]
                self.tt("vector", ssc[:, 2 + rt:3 + rt], ssc[:, rt:rt + 1], rq,
                        ALU.mult, okey="fcols", ikey=["fcols", f"frqT{w*2+rt}"])
                self.act(fb["P"][:, rt * 384:rt * 384 + 384],
                         fb["sims"][:, rt * 384:rt * 384 + 384], AF.Exp,
                         bias=ssc[:, 2 + rt:3 + rt], scale=rq,
                         accum=ssc[:, 4 + rt:5 + rt],
                         okey="fP", ikey=["fsims", "fcols", f"frqT{w*2+rt}"])
            self.recip(ssc[:, 6:8], ssc[:, 4:6], okey="fcols", ikey="fcols")
            for rt in range(2):
                self.ts("vector", fb["P"][:, rt * 384:rt * 384 + 384],
                        fb["P"][:, rt * 384:rt * 384 + 384], ssc[:, 6 + rt:7 + rt],
                        None, ALU.mult, okey="fP", ikey=["fP", "fcols"])
            pt = self.pp.acquire()
            ptb = self.pp.aps[pt].bitcast(BF16)
            for rt in range(2):
                for kt in range(rt, rt + 3):
                    self.trans(ptb[0:128, kt * 256 + rt * 128:kt * 256 + rt * 128 + 128],
                               fb["P"][:, rt * 384 + (kt - rt) * 128:
                                       rt * 384 + (kt - rt) * 128 + 128],
                               ct["iden_b"], okey=f"PS{pt}", ikey=["fP", "ca"])
            self.copy("vector", fb["PTs"][:, 0:128], ptb[0:128, 0:128],
                      okey="fPTs", ikey=f"PS{pt}")
            self.copy("vector", fb["PTs"][:, 256:768], ptb[0:128, 256:768],
                      okey="fPTs", ikey=f"PS{pt}")
            self.copy("vector", fb["PTs"][:, 896:1024], ptb[0:128, 896:1024],
                      okey="fPTs", ikey=f"PS{pt}")
            self.pp.release(pt)
            if half == 0:
                self._opair = self.pp.acquire()
            ob = self._opair
            ops = self.pp.aps[ob]
            for qt_i in range(2):
                valid = [kt for kt in range(4) if qt_i <= kt <= qt_i + 2]
                for j, kt in enumerate(valid):
                    if kt < 2:
                        vt = fb["vring"][(w - 1) % 2][kt]
                        vk = f"fv{(w-1)%2}_{kt}"
                    else:
                        vt = fb["vring"][w % 2][kt - 2]
                        vk = f"fv{w%2}_{kt-2}"
                    self.mm(ops[pbq:pbq + 64, qt_i * 128:qt_i * 128 + 128],
                            vt[:, h * 64:(h + 1) * 64],
                            fb["PTs"][:, kt * 256 + qt_i * 128:kt * 256 + qt_i * 128 + 128],
                            start=(j == 0), stop=(j == len(valid) - 1),
                            okey=f"PS{ob}", ikey=[vk, "fPTs"], tp=(0, pbq))
            if half == 1:
                self.copy("vector", fb["attnT"][f][:, 0:W], ops[0:128, 0:W],
                          okey=f"fattnT{f}", ikey=f"PS{ob}")
                self.pp.release(ob)

    def _attn_wo(self, l, wlist):
        fb = self.fbl
        gcols = len(wlist) * W
        assert len(wlist) == 1
        c0 = HALO + wlist[0] * W
        for f in range(NFT):
            self.copy("vector", fb["oldc"][f], self.rT[f][:, c0 + W - 1:c0 + W],
                      okey=f"foldc{f}", ikey=f"rT{f}")
        for mt in range(NFT):
            pb = self.pp.acquire()
            ps = self.pp.aps[pb]
            for kt in range(NFT):
                self.mm(ps[0:128, 0:gcols], fb["wo"][kt][:, mt * 128:(mt + 1) * 128],
                        fb["attnT"][kt][:, 0:gcols], start=(kt == 0),
                        stop=(kt == NFT - 1), okey=f"PS{pb}",
                        ikey=[f"fwo{kt}", f"fattnT{kt}"])
            self.tt("vector", self.rT[mt][:, c0:c0 + gcols], ps[0:128, 0:gcols],
                    self.rT[mt][:, c0:c0 + gcols], ALU.add,
                    okey=f"rT{mt}", ikey=[f"PS{pb}", f"rT{mt}"])
            self.pp.release(pb)

    def _layer_attn(self, l):
        dd = self.dram
        self._ffn_setup()
        self.work.push()
        fb = self.fbl = self._attn_frame()
        for f in range(NFT):
            self.dma("gpsimd", fb["wqkv"][f],
                     dd["ffn_Wqkv"][l, f * 128:(f + 1) * 128, :], "d_w",
                     okey=f"fw{f}")
            self.dma("gpsimd", fb["wo"][f],
                     dd["ffn_Wo"][l, f * 128:(f + 1) * 128, :], "d_w",
                     okey=f"fwo{f}")
        if l > 0:
            self._halo_in(l, "a")
        self._attn_proj_block(l, 1, 256, True)
        for w in range(8):
            self._attn_proj_block(l, 257 + w * W, 256, False)
            self._attn_window(l, w)
            self._attn_wo(l, [w])
        self._halo_out(l, "g")
        self.work.pop()
        self.p.fence()

    def _layer_glu(self, l):
        dd = self.dram
        ct = self.ctab
        self.work.push()
        fb = self.fbl = self._glu_frame()
        NC = 512
        for f in range(NFT):
            for j in range(3):
                self.copy("vector", fb["oldg"][f][:, j:j + 1],
                          self.rT[f][:, HALO + (j + 1) * NC - 1:HALO + (j + 1) * NC],
                          okey=f"goldg{f}", ikey=f"rT{f}")
        for ci, ch in enumerate([1, 2, 3, 0]):
            c0 = HALO + ch * NC
            if ch == 0:
                self._halo_in(l, "g")
            for f in range(NFT):
                if f < 4:
                    self.copy("vector", fb["rsh"][f], self.rT[f][:, c0:c0 + NC],
                              okey=f"grsh{f}", ikey=f"rT{f}")
                else:
                    if ch == 0:
                        self.copy("vector", fb["rsh"][f][:, 0:1],
                                  self.rT[f][:, c0 - 1:c0], okey=f"grsh{f}",
                                  ikey=f"rT{f}")
                    else:
                        self.copy("vector", fb["rsh"][f][:, 0:1],
                                  fb["oldg"][f][:, ch - 1:ch], okey=f"grsh{f}",
                                  ikey=f"goldg{f}")
                    self.copy("vector", fb["rsh"][f][:, 1:NC],
                              self.rT[f][:, c0:c0 + NC - 1], okey=f"grsh{f}",
                              ikey=f"rT{f}")
                self.act(fb["sq"][f], fb["rsh"][f], AF.Square,
                         okey=f"gsq{f}", ikey=f"grsh{f}")
            p1 = self.pp.acquire()
            p2 = self.pp.acquire()
            s1 = self.pp.aps[p1]
            s2 = self.pp.aps[p2]
            for f in range(NFT):
                self.mm(s1[0:1, 0:NC], ct["ones_b"], fb["rsh"][f],
                        start=(f == 0), stop=(f == NFT - 1), okey=f"PS{p1}",
                        ikey=["ca", f"grsh{f}"])
                self.mm(s2[0:1, 0:NC], ct["ones_b"], fb["sq"][f],
                        start=(f == 0), stop=(f == NFT - 1), okey=f"PS{p2}",
                        ikey=["ca", f"gsq{f}"])
            mu = fb["srow"][0:1, 0:NC]
            ms = fb["srow"][0:1, 512:512 + NC]
            rst = fb["srow"][0:1, 1024:1024 + NC]
            self.ts("vector", mu, s1[0:1, 0:NC], 1.0 / E, None, ALU.mult,
                    okey="gmu", ikey=f"PS{p1}")
            self.ts("vector", ms, s2[0:1, 0:NC], 1.0 / E, None, ALU.mult,
                    okey="gms", ikey=f"PS{p2}")
            self.pp.release(p1)
            self.pp.release(p2)
            self.tt("vector", rst, mu, mu, ALU.mult, okey="grst", ikey="gmu")
            self.tt("vector", rst, ms, rst, ALU.subtract, okey="grst",
                    ikey=["gms", "grst"])
            self.act(rst, rst, AF.Ln, bias=self.vcol("c_eps")[0:1, :],
                     okey="grst", ikey="grst")
            self.act(rst, rst, AF.Exp, scale=-0.5, okey="grst", ikey="grst")
            pmu = self.pp.acquire()
            prs = self.pp.acquire()
            self.mm(self.pp.aps[pmu][0:128, 0:NC], ct["ones_row"][0:1, :], mu,
                    okey=f"PS{pmu}", ikey=["ca", "gmu"])
            self.mm(self.pp.aps[prs][0:128, 0:NC], ct["ones_row"][0:1, :], rst,
                    okey=f"PS{prs}", ikey=["ca", "grst"])
            for f in range(NFT):
                self.tt("vector", fb["gel"][:, 0:NC], fb["rsh"][f],
                        self.pp.aps[pmu][0:128, 0:NC], ALU.subtract,
                        okey="ggel", ikey=[f"grsh{f}", f"PS{pmu}"])
                self.stt("vector", fb["gel"][:, 0:NC], fb["gel"][:, 0:NC],
                         self.vcol("lng", l * NFT + f), ALU.mult,
                         self.pp.aps[prs][0:128, 0:NC], ALU.mult,
                         okey="ggel", ikey=["ggel", f"PS{prs}"])
                self.ts("vector", fb["xnT"][f], fb["gel"][:, 0:NC],
                        self.vcol("lnb", l * NFT + f), None, ALU.add,
                        okey=f"gxnT{f}", ikey="ggel")
            self.pp.release(pmu)
            self.pp.release(prs)
            for gi, (t0, t1) in enumerate(self.GRP):
                par = gi % 2
                gc0 = t0 * 128
                gcols = min(t1 * 128, DI) - gc0
                for half in range(2):
                    for f in range(NFT):
                        self.dma("gpsimd", fb["w1g"][par][half][f][:, 0:gcols],
                                 dd["ffn_W1"][l, f * 128:(f + 1) * 128,
                                              half * DI + gc0:half * DI + gc0 + gcols],
                                 "d_w1", okey=f"gw1_{par}_{half}_{f}")
                for ct_i in range(t0, t1):
                    rows = min(128, DI - ct_i * 128)
                    lc = ct_i * 128 - gc0
                    banks = []
                    for half in range(2):
                        pb = self.pp.acquire()
                        ps = self.pp.aps[pb]
                        for kt in range(NFT):
                            self.mm(ps[0:rows, 0:NC],
                                    fb["w1g"][par][half][kt][:, lc:lc + rows],
                                    fb["xnT"][kt], start=(kt == 0),
                                    stop=(kt == NFT - 1), okey=f"PS{pb}",
                                    ikey=[f"gw1_{par}_{half}_{kt}"]
                                    + [f"gxnT{k}" for k in range(NFT)])
                        banks.append(pb)
                    pa, pb = banks
                    self.act(fb["gel"][0:rows, 0:NC], self.pp.aps[pb][0:rows, 0:NC],
                             AF.Gelu, bias=self.vcol("b1g", l * 22 + ct_i)[0:rows, :],
                             okey="ggel", ikey=f"PS{pb}")
                    self.pp.release(pb)
                    self.stt("vector", fb["ht"][ct_i][0:rows, 0:NC],
                             self.pp.aps[pa][0:rows, 0:NC],
                             self.vcol("b1a", l * 22 + ct_i)[0:rows, :],
                             ALU.add, fb["gel"][0:rows, 0:NC], ALU.mult,
                             okey=f"ght{ct_i}", ikey=[f"PS{pa}", "ggel"])
                    self.pp.release(pa)
            mtb = [self.pp.acquire() for _ in range(8)]
            for kt in range(22):
                rows = min(128, DI - kt * 128)
                wb = fb["w2t"][kt % 4]
                self.dma("gpsimd", wb[0:rows, :],
                         dd["ffn_W2"][l, kt * 128:kt * 128 + rows, :], "d_w2",
                         okey=f"gw2_{kt%4}")
                for mt in range(NFT):
                    self.mm(self.pp.aps[mtb[mt]][0:128, 0:NC],
                            fb["w2t"][kt % 4][0:rows, mt * 128:(mt + 1) * 128],
                            fb["ht"][kt][0:rows, 0:NC],
                            start=(kt == 0), stop=(kt == 21),
                            okey=f"PS{mtb[mt]}", ikey=[f"gw2_{kt%4}", f"ght{kt}"])
            for mt in range(NFT):
                self.stt("vector", self.rT[mt][:, c0:c0 + NC],
                         self.pp.aps[mtb[mt]][0:128, 0:NC],
                         self.vcol("b2", l * NFT + mt), ALU.add,
                         self.rT[mt][:, c0:c0 + NC], ALU.add,
                         okey=f"rT{mt}", ikey=[f"PS{mtb[mt]}", f"rT{mt}"])
                self.pp.release(mtb[mt])
            if ch == 3 and l < self.cfg["nlayers"] - 1:
                self._halo_out(l, "a")
        self.work.pop()

    def _phase_final(self):
        dd = self.dram
        wk = self.work
        wk.push()
        gl = [wk.alloc(512) for _ in range(2)]
        xt = [wk.alloc(E) for _ in range(2)]
        ot = [wk.alloc(128) for _ in range(2)]
        s = wk.alloc(512)
        last_y = None
        for mt in range(NFT):
            for c in range(4):
                b = c % 2
                self.dma("sync", gl[b],
                         dd["glapark"][mt * 128:(mt + 1) * 128, c * 512:(c + 1) * 512],
                         "d_park2", okey=f"gl{b}", ikey="glapark")
                self.tt("vector", s, self.rT[mt][:, HALO + c * 512:HALO + (c + 1) * 512],
                        gl[b][:, 0:512], ALU.add, okey="s", ikey=[f"rT{mt}", f"gl{b}"])
                for i in range(4):
                    ti = c * 4 + i
                    b2 = ti % 2
                    pb = self.pp.acquire()
                    ps = self.pp.aps[pb]
                    self.trans(ps[0:128, 0:128], s[:, i * 128:(i + 1) * 128],
                               self.ctab["iden_f"], okey=f"PS{pb}", ikey=["s", "CONST"])
                    self.dma("sync", xt[b2],
                             dd["xs"][HALO + ti * 128:HALO + (ti + 1) * 128, :],
                             "d_x2", okey=f"xtf{b2}")
                    self.tt("vector", ot[b2], ps[0:128, 0:128],
                            xt[b2][:, mt * 128:(mt + 1) * 128], ALU.add,
                            okey=f"otf{b2}", ikey=[f"PS{pb}", f"xtf{b2}"])
                    self.pp.release(pb)
                    last_y = self.dma("sync",
                                      dd["y"][ti * 128:(ti + 1) * 128,
                                              mt * 128:(mt + 1) * 128],
                                      ot[b2], "d_y", ikey=f"otf{b2}", okey="y")
        self.p.wait_tok("sync", last_y)
        wk.pop()


def _host_tables(core):
    t0 = core * TC
    half = np.arange(0, D, 2, dtype=np.float32)
    inv_freq = (1.0 / 10000.0 ** (half.astype(np.float64) / D)).astype(np.float32)
    tpos = np.clip(np.arange(t0 - HALO, t0 + TC), 0, T - 1)
    freqs = (tpos[:, None].astype(np.float32) * inv_freq[None, :]).astype(np.float32)
    freqs = np.concatenate([freqs, freqs], -1)
    cos = np.cos(freqs.astype(np.float64))
    sin = np.sin(freqs.astype(np.float64))
    base = (half.astype(np.float64) + 0.4 * D) / (1.4 * D)
    base = np.concatenate([base, base], -1)
    dq = (tpos % W).astype(np.float64)
    Sq = base[None, :] ** (dq[:, None] / (W // 2))
    cosq = (8.0 * cos * Sq).astype(np.float32)
    sinq = (8.0 * sin * Sq).astype(np.float32)
    SkR = base[None, :] ** (-dq[:, None] / (W // 2))
    cosk = (cos * SkR).astype(np.float32)
    sink = (sin * SkR).astype(np.float32)

    def dup(tab):
        return np.concatenate([tab.T, tab.T], 0).astype(np.float32)
    return dup(cosq), dup(sinq), dup(cosk), dup(sink)


def _host_masks(core):
    def build(rt, first):
        aq = rt * 128 + np.arange(128)[:, None]
        kb = rt * 128 + np.arange(384)[None, :]
        keep = (kb >= aq) & (kb <= aq + 256)
        if first:
            keep = keep & (kb >= 256)
        return np.where(keep, 0.0, NEG).astype(np.float32)
    m0, m1 = build(0, False), build(1, False)
    if core == 0:
        return m0, m1, build(0, True), build(1, True)
    return m0, m1, m0, m1


def _host_consts(core):
    iden = np.eye(128, dtype=np.float32)
    R = np.zeros((128, 128), np.float32)
    for hb in range(2):
        o = hb * 64
        for i in range(32):
            R[o + i, o + 32 + i] = -1.0
            R[o + 32 + i, o + i] = 1.0
    RT = np.ascontiguousarray(R.T)
    bdall = np.zeros((128, 128), np.float32)
    hmall = np.zeros((128, 1024), np.float32)
    for f in range(NFT):
        for m in range(16):
            p0 = 64 * (m - 2 * f) if m in (2 * f, 2 * f + 1) else None
            if p0 is not None:
                bdall[p0:p0 + 64, f * 16 + m] = 1.0
        for m in range(128):
            hmall[2 * f + (m >= 64), f * 128 + m] = 1.0
    ones_b = np.ones((128, 1), np.float32)
    ones_row = np.zeros((128, 128), np.float32)
    ones_row[0, :] = 1.0
    triu = np.triu(np.ones((128, 128), np.float32))
    m0, m1, mf0, mf1 = _host_masks(core)
    cosq, sinq, cosk, sink = _host_tables(core)
    vals = {"iden_f": iden, "iden_b": iden, "RT_b": RT, "bdall": bdall,
            "hmall": hmall, "ones_b": ones_b, "ones_row": ones_row,
            "triu_b": triu, "mask0": m0, "mask1": m1, "maskf0": mf0, "maskf1": mf1,
            "cosq": cosq, "sinq": sinq, "cosk": cosk, "sink": sink}
    cols = []
    for nm, ncols, _ in CONST_LAYOUT + TAB_LAYOUT:
        a = vals[nm].astype(np.float32)
        assert a.shape[1] == ncols, (nm, a.shape, ncols)
        if a.shape[0] < 128:
            a = np.concatenate([a, np.zeros((128 - a.shape[0], ncols), np.float32)], 0)
        cols.append(a)
    return np.ascontiguousarray(np.concatenate(cols, 1))


def _host_vecs(inputs, core):
    v = np.zeros((128, VC_COLS), np.float32)

    def put(name, i, col):
        v[:, VC[name] + i] = col
    put("delta", 0, 0)
    delta = np.asarray(inputs["L0_attn_delta"], np.float32).reshape(NFT, 128)
    for f in range(NFT):
        put("delta", f, delta[f])
    nw = np.asarray(inputs["L0_attn_norm_w"], np.float32).reshape(NFT, 128)
    for f in range(NFT):
        put("normw", f, nw[f])
    put("halomul", 0, 0.0 if core == 0 else 1.0)
    qs = np.asarray(inputs["L0_ffn_q_scale"], np.float32)
    ks = np.asarray(inputs["L0_ffn_k_scale"], np.float32)
    for l in range(DEPTH):
        put("qs", l, np.tile(qs[l], 2))
        put("ks", l, np.tile(ks[l], 2))
        lng = np.asarray(inputs["L0_ffn_ln_g"], np.float32)[l].reshape(NFT, 128)
        lnb = np.asarray(inputs["L0_ffn_ln_b"], np.float32)[l].reshape(NFT, 128)
        b2 = np.asarray(inputs["L0_ffn_b2"], np.float32)[l].reshape(NFT, 128)
        for f in range(NFT):
            put("lng", l * NFT + f, lng[f])
            put("lnb", l * NFT + f, lnb[f])
            put("b2", l * NFT + f, b2[f])
        b1 = np.asarray(inputs["L0_ffn_b1"], np.float32)[l]
        b1a, b1g = b1[:DI], b1[DI:]
        for ct in range(22):
            rows = min(128, DI - ct * 128)
            col = np.zeros(128, np.float32)
            col[:rows] = b1a[ct * 128:ct * 128 + rows]
            put("b1a", l * 22 + ct, col)
            col = np.zeros(128, np.float32)
            col[:rows] = b1g[ct * 128:ct * 128 + rows]
            put("b1g", l * 22 + ct, col)
    for r in range(NCORES):
        m = 1.0 if r < core else 0.0
        put("rkM", r, m)
        put("rkI", r, 1.0 - m)
    half = np.arange(0, 64, 2, dtype=np.float64)
    base = (half + 0.4 * 64) / (1.4 * 64)
    base = np.concatenate([base, base])
    put("skb2", 0, np.tile((base ** 2).astype(np.float32), 2))
    put("c_eps", 0, 1e-5)
    put("c_one", 0, 1.0)
    put("c_epsD", 0, 1e-5 * D)
    put("c_hlnD", 0, 0.5 * math.log(D))
    return v


def _host_hidx(core):
    idx = np.zeros((128, NFT), np.uint32)
    src = core - 1 if core > 0 else 0
    for f in range(NFT):
        idx[:, f] = src * E + f * 128 + np.arange(128)
    return idx


_NC_CACHE = {}


def _get_nc():
    key = tuple(sorted(CFG.items()))
    if key not in _NC_CACHE:
        _NC_CACHE[key] = Build(dict(CFG)).build()
    return _NC_CACHE[key]


def kernel(**inputs):
    nc = _get_nc()
    x = np.asarray(inputs["L0_attn_x"], np.float32)[0]
    in_maps = []
    for c in range(NCORES):
        s0 = c * TC
        xs = np.zeros((EXT, E), np.float32)
        lo = max(0, s0 - HALO)
        xs[HALO - (s0 - lo):HALO] = x[lo:s0]
        xs[HALO:] = x[s0:s0 + TC]
        m = {"xs": np.ascontiguousarray(xs),
             "vecs": _host_vecs(inputs, c),
             "consts": _host_consts(c),
             "hidx": _host_hidx(c)}
        for nm in ["attn_Wq", "attn_Wk", "attn_Wv", "attn_Wo", "attn_G1", "attn_G2",
                   "ffn_Wqkv", "ffn_Wo", "ffn_W1", "ffn_W2"]:
            m[nm] = np.ascontiguousarray(np.asarray(inputs["L0_" + nm], np.float32))
        in_maps.append(m)
    res = run_bass_kernel_spmd(nc, in_maps, core_ids=list(range(NCORES)))
    out = np.concatenate([res.results[c]["y"] for c in range(NCORES)], 0)
    return np.ascontiguousarray(out[None])


# version 02
# speedup vs baseline: 1.0016x; 1.0016x over previous
import contextlib
import math
import numpy as np

import concourse.bass as bass
import concourse.mybir as mybir
from concourse.bass import IndirectOffsetOnAxis
from concourse.bass_utils import run_bass_kernel_spmd

F32 = mybir.dt.float32
BF16 = mybir.dt.bfloat16
U32 = mybir.dt.uint32
AF = mybir.ActivationFunctionType
ALU = mybir.AluOpType
AX = mybir.AxisListType

NCORES = 8
T = 16384
TC = 2048
E = 1024
H = 16
D = 64
HALO = 257
EXT = TC + HALO
W = 256
CHUNK = 128
NFT = 8
DEPTH = 6
DI = 2730
NEG = -1e30
RG = [[0, 1, 2, 3, 4, 5, 6, 7]]
ENG_NAMES = ("tensor", "vector", "scalar", "gpsimd", "sync")

CFG = {"gla": True, "nlayers": DEPTH}


class _Tok:
    __slots__ = ("eng", "pos")
    def __init__(self, eng, pos):
        self.eng = eng
        self.pos = pos


class _DmaTok:
    __slots__ = ("sem", "val")
    def __init__(self, sem, val):
        self.sem = sem
        self.val = val


class _SwTok:
    __slots__ = ("prog", "stream", "slot", "sem", "cell")
    def __init__(self, prog, stream, slot, sem, cell):
        self.prog = prog
        self.stream = stream
        self.slot = slot
        self.sem = sem
        self.cell = cell

    def resolve(self):
        if self.cell[0] is None:
            if self.stream["armed"][self.slot] is self.cell:
                self.prog._resolve_sw(self.stream, self.slot, self.sem)
        return self.cell[0]


class _Eng:
    def __init__(self, name):
        self.name = name
        self.ops = []
        self.incs = []
        self.ninc = 0
        self.waited = {}
        self.last_drain = 0

    def milestone_for(self, pos):
        for p, c in self.incs:
            if p >= pos:
                return c
        lp = len(self.ops) - 1
        assert 0 <= pos <= lp, (self.name, pos, lp)
        if self.ops[lp][2]:
            if self.name in ("vector", "scalar", "gpsimd"):
                self.ops.append([lambda q: q.drain(), [], False])
            else:
                self.ops.append([lambda q: q.nop(), [], False])
            lp = len(self.ops) - 1
        self.ninc += 1
        self.incs.append((lp, self.ninc))
        self.ops[lp][1].append(("self", self.ninc))
        return self.ninc


class Prog:
    DMA_POOL = 4

    def __init__(self, nc):
        self.nc = nc
        self.engs = {n: _Eng(n) for n in ENG_NAMES}
        self.writers = {}
        self.readers = {}
        self.streams = {}
        self.sem_handles = {}
        self.fence_toks = []

    @staticmethod
    def _keys(aps, key):
        if key is not None:
            return list(key) if isinstance(key, (list, tuple)) else [key]
        out = []
        for ap in (aps if isinstance(aps, (list, tuple)) else [aps]):
            if ap is None or not hasattr(ap, "tensor"):
                continue
            out.append(ap.tensor.name)
        return out

    def _emit_wait(self, ename, sem, val):
        e = self.engs[ename]
        if e.waited.get(sem, -1) >= val:
            return
        e.waited[sem] = val
        e.ops.append([lambda q, s=sem, v=val: q.wait_ge(self.sem_handles[s], v), [], False])

    def op(self, eng, fn, outs=(), ins=(), okey=None, ikey=None, dma_sem=None,
           no_track=False):
        e = self.engs[eng]
        ik = self._keys(ins, ikey)
        ok = self._keys(outs, okey)
        deps = []
        if not no_track:
            deps.extend(self.fence_toks)
            for k in ik:
                w = self.writers.get(k)
                if w is not None:
                    deps.append(w)
            for k in ok:
                w = self.writers.get(k)
                if w is not None:
                    deps.append(w)
                deps.extend(self.readers.get(k, ()))
        need_drain = False
        for tok in deps:
            if isinstance(tok, _SwTok):
                tok = tok.resolve()
            if isinstance(tok, _DmaTok):
                self._emit_wait(eng, tok.sem, tok.val)
            elif tok.eng is e:
                if tok.pos >= e.last_drain:
                    need_drain = True
            else:
                cnt = tok.eng.milestone_for(tok.pos)
                self._emit_wait(eng, "M_" + tok.eng.name, cnt)
        if need_drain and eng in ("vector", "scalar", "gpsimd"):
            e.last_drain = len(e.ops)
            e.ops.append([lambda q: q.drain(), [], False])
        if dma_sem is not None:
            name, inc = (dma_sem, 16) if not isinstance(dma_sem, tuple) else dma_sem
            sw = eng == "gpsimd"
            name = name + "@" + eng[0]
            stream = self.streams.setdefault(
                name, {"counts": [0] * self.DMA_POOL, "armed": [None] * self.DMA_POOL,
                       "idx": 0, "inc": inc, "sw": sw})
            assert stream["sw"] == sw
            slot = stream["idx"] % self.DMA_POOL
            stream["idx"] += 1
            slot_sem = f"{name}#{slot}"
            if stream["counts"][slot] > 0:
                self._emit_wait(eng, slot_sem, inc * stream["counts"][slot])
        pos = len(e.ops)
        inc_list = []
        e.ops.append([fn, inc_list, dma_sem is not None])
        if dma_sem is not None:
            stream["counts"][slot] += 1
            inc_list.append((slot_sem, inc))
            tok = _DmaTok(slot_sem, inc * stream["counts"][slot])
        else:
            tok = _Tok(e, pos)
        if not no_track:
            for k in ok:
                self.writers[k] = tok
                self.readers[k] = []
            for k in ik:
                self.readers.setdefault(k, []).append(tok)
        return tok

    def _resolve_sw(self, stream, slot, slot_sem):
        g = self.engs["gpsimd"]
        cell = stream["armed"][slot]
        if cell[0] is None:
            g.ops.append([lambda q, s=slot_sem: q.wait_ge(self.sem_handles[s],
                                                          stream["inc"]), [], False])
            pos = len(g.ops)
            g.ops.append([lambda q, s=slot_sem: q.sem_clear(self.sem_handles[s]), [], False])
            cell[0] = _Tok(g, pos)
        stream["armed"][slot] = None
        return cell[0]

    def fence(self):
        toks = []
        for e in self.engs.values():
            if e.ops:
                toks.append(_Tok(e, len(e.ops) - 1))
        for name, stream in self.streams.items():
            for slot in range(self.DMA_POOL):
                if stream["counts"][slot] > 0:
                    toks.append(_DmaTok(f"{name}#{slot}",
                                        stream["inc"] * stream["counts"][slot]))
        self.fence_toks = toks

    def wait_tok(self, eng, tok):
        if isinstance(tok, _SwTok):
            tok = tok.resolve()
        if isinstance(tok, _DmaTok):
            self._emit_wait(eng, tok.sem, tok.val)
        elif tok.eng is self.engs[eng]:
            e = self.engs[eng]
            if tok.pos >= e.last_drain and eng in ("vector", "scalar", "gpsimd"):
                e.last_drain = len(e.ops)
                e.ops.append([lambda q: q.drain(), [], False])
        else:
            cnt = tok.eng.milestone_for(tok.pos)
            self._emit_wait(eng, "M_" + tok.eng.name, cnt)

    def replay(self, block, stack):
        for n in ENG_NAMES:
            self.sem_handles["M_" + n] = stack.enter_context(
                self.nc.semaphore("M_" + n))
        for name, stream in self.streams.items():
            for slot in range(self.DMA_POOL):
                s = f"{name}#{slot}"
                self.sem_handles[s] = stack.enter_context(
                    self.nc.semaphore(s.replace("#", "_").replace("@", "_")))
        prog = self

        def make_runner(ename):
            def run(q):
                for fn, incs, _dma in prog.engs[ename].ops:
                    inst = fn(q)
                    for kind, val in incs:
                        if kind == "self":
                            inst.then_inc(prog.sem_handles["M_" + ename], 1)
                        else:
                            try:
                                inst.then_inc(prog.sem_handles[kind], val)
                            except BaseException as ex:
                                raise RuntimeError(f"then_inc fail {ename} {kind} {val} {inst}") from ex
            return run
        for ename in ENG_NAMES:
            getattr(block, ename)(make_runner(ename))


class Arena:
    def __init__(self, nc, stack, name, nbytes):
        self.t = stack.enter_context(nc.sbuf_tensor(name, [128, nbytes // 4], F32))
        self.nbytes = nbytes
        self.off = 0
        self.marks = []

    def alloc(self, cols, dtype=F32, parts=128):
        esz = {F32: 4, BF16: 2, U32: 4}[dtype]
        nb = (cols * esz + 3) & ~3
        assert self.off + nb <= self.nbytes, (self.off, nb, self.nbytes)
        off = self.off
        self.off += nb
        base = self.t.ap()
        if dtype == BF16:
            return base.bitcast(BF16)[0:parts, off // 2: off // 2 + cols]
        if dtype == U32:
            return base.bitcast(U32)[0:parts, off // 4: off // 4 + cols]
        return base[0:parts, off // 4: off // 4 + cols]

    def push(self):
        self.marks.append(self.off)

    def pop(self):
        self.off = self.marks.pop()


class PsumPool:
    def __init__(self, nc, stack):
        self.ts = [stack.enter_context(nc.psum_tensor(f"psum{i}", [128, 512], F32))
                   for i in range(8)]
        self.aps = [t.ap() for t in self.ts]
        self.free = list(range(8))
        self.rr = 0

    def acquire(self):
        assert self.free, "psum pool exhausted"
        for _ in range(len(self.free)):
            b = self.free.pop(0)
            break
        return b

    def release(self, b):
        self.free.append(b)


CONST_LAYOUT = [
    ("iden_f", 128, F32),
    ("iden_b", 128, BF16),
    ("RT_b", 128, BF16),
    ("bdall", 128, BF16),
    ("hmall", 1024, BF16),
    ("ones_b", 1, BF16),
    ("ones_row", 128, F32),
    ("triu_b", 128, BF16),
    ("mask0", 384, BF16),
    ("mask1", 384, BF16),
    ("maskf0", 384, BF16),
    ("maskf1", 384, BF16),
]
TAB_LAYOUT = [
    ("cosq", EXT, F32),
    ("sinq", EXT, F32),
    ("cosk", EXT, F32),
    ("sink", EXT, F32),
]
CONST_COLS = sum(c for _, c, _ in CONST_LAYOUT) + sum(c for _, c, _ in TAB_LAYOUT)

def _vc_build():
    vc, c = {}, 0
    def take(name, n):
        nonlocal c
        vc[name] = c
        c += n
    take("delta", NFT)
    take("normw", NFT)
    take("halomul", 1)
    take("qs", DEPTH)
    take("ks", DEPTH)
    take("lng", DEPTH * NFT)
    take("lnb", DEPTH * NFT)
    take("b2", DEPTH * NFT)
    take("rkM", NCORES)
    take("rkI", NCORES)
    take("skb2", 1)
    take("c_eps", 1)
    take("c_one", 1)
    take("c_epsD", 1)
    take("c_hlnD", 1)
    take("b1a", DEPTH * 22)
    take("b1g", DEPTH * 22)
    return vc, c

VC, VC_COLS = _vc_build()


class Build:
    def __init__(self, cfg):
        self.cfg = cfg
        self.nc = bass.Bass("TRN2", target_bir_lowering=False, debug=False)
        self.p = Prog(self.nc)
        self.stack = contextlib.ExitStack()
        self.dram = {}
        self._declare_dram()

    def _declare_dram(self):
        nc, dd = self.nc, self.dram
        dd["xs"] = nc.dram_tensor("xs", [EXT, E], F32, kind="ExternalInput").ap()
        for nm, shp in [
            ("attn_Wq", [E, E]), ("attn_Wk", [E, E]), ("attn_Wv", [E, E]),
            ("attn_Wo", [E, E]), ("attn_G1", [E, D]), ("attn_G2", [D, E]),
            ("ffn_Wqkv", [DEPTH, E, 3 * E]), ("ffn_Wo", [DEPTH, E, E]),
            ("ffn_W1", [DEPTH, E, 2 * DI]), ("ffn_W2", [DEPTH, DI, E]),
        ]:
            dd[nm] = nc.dram_tensor(nm, shp, F32, kind="ExternalInput").ap()
        dd["vecs"] = nc.dram_tensor("vecs", [128, VC_COLS], F32,
                                    kind="ExternalInput").ap()
        dd["consts"] = nc.dram_tensor("consts", [128, CONST_COLS], F32,
                                      kind="ExternalInput").ap()
        dd["hidx"] = nc.dram_tensor("hidx", [128, NFT], U32, kind="ExternalInput").ap()
        dd["y"] = nc.dram_tensor("y", [TC, E], F32, kind="ExternalOutput").ap()
        dd["glapark"] = nc.dram_tensor("glapark", [E, TC], F32).ap()
        dd["cin_s"] = nc.dram_tensor("cin_s", [128, 520], F32).ap()
        dd["cout_s"] = nc.dram_tensor("cout_s", [1024, 520], F32).ap()
        dd["cin_h"] = nc.dram_tensor("cin_h", [E, HALO], F32).ap()
        dd["cout_h"] = nc.dram_tensor("cout_h", [NCORES * E, HALO], F32).ap()
        dd["cin_g"] = nc.dram_tensor("cin_g", [E, 1], F32).ap()
        dd["cout_g"] = nc.dram_tensor("cout_g", [NCORES * E, 1], F32).ap()

    def mm(self, out, lhsT, rhs, start=True, stop=True, okey=None, ikey=None, tp=None):
        return self.p.op("tensor",
                         lambda q: q.matmul(out, lhsT, rhs, start=start, stop=stop,
                                            tile_position=tp),
                         outs=[out], ins=[lhsT, rhs], okey=okey, ikey=ikey)

    def trans(self, out, in_, ident, okey=None, ikey=None):
        return self.p.op("tensor", lambda q: q.transpose(out, in_, ident),
                         outs=[out], ins=[in_, ident], okey=okey, ikey=ikey)

    def tt(self, eng, out, a, b, op, okey=None, ikey=None):
        return self.p.op(eng, lambda q: q.tensor_tensor(out, a, b, op),
                         outs=[out], ins=[a, b], okey=okey, ikey=ikey)

    @staticmethod
    def _addkeys(ikey, aps):
        names = [a.tensor.name for a in aps if hasattr(a, "tensor")]
        if not names:
            return ikey
        if ikey is None:
            return None
        base = list(ikey) if isinstance(ikey, (list, tuple)) else [ikey]
        return base + names

    def ts(self, eng, out, in0, s1, s2, op0, op1=ALU.bypass, accum=None,
           okey=None, ikey=None):
        extra = [x for x in (s1, s2) if hasattr(x, "tensor")]
        outs = [out] + ([accum] if accum is not None else [])
        return self.p.op(eng, lambda q: q.tensor_scalar(out, in0, s1, s2, op0, op1,
                                                        accum_out=accum),
                         outs=outs, ins=[in0] + extra, okey=okey,
                         ikey=self._addkeys(ikey, extra))

    def stt(self, eng, out, in0, s, op0, in1, op1, accum=None, okey=None, ikey=None):
        extra = [s] if hasattr(s, "tensor") else []
        outs = [out] + ([accum] if accum is not None else [])
        return self.p.op(eng,
                         lambda q: q.scalar_tensor_tensor(out, in0, s, in1, op0, op1,
                                                          accum_out=accum),
                         outs=outs, ins=[in0, in1] + extra, okey=okey,
                         ikey=self._addkeys(ikey, extra))

    def act(self, out, in_, func, bias=0.0, scale=1.0, accum=None,
            okey=None, ikey=None):
        extra = [x for x in (bias, scale) if hasattr(x, "tensor")]
        outs = [out] + ([accum] if accum is not None else [])
        return self.p.op("scalar",
                         lambda q: q.activation(out, in_, func, bias=bias, scale=scale,
                                                accum_out=accum),
                         outs=outs, ins=[in_] + extra, okey=okey,
                         ikey=self._addkeys(ikey, extra))

    def copy(self, eng, out, in_, okey=None, ikey=None):
        if eng == "scalar":
            return self.p.op(eng, lambda q: q.copy(out, in_), outs=[out], ins=[in_],
                             okey=okey, ikey=ikey)
        return self.p.op(eng, lambda q: q.tensor_copy(out, in_),
                         outs=[out], ins=[in_], okey=okey, ikey=ikey)

    def red(self, out, in_, op, negate=False, okey=None, ikey=None):
        return self.p.op("vector",
                         lambda q: q.tensor_reduce(out, in_, AX.X, op, negate=negate),
                         outs=[out], ins=[in_], okey=okey, ikey=ikey)

    def scan(self, out, d0, op0=ALU.add, okey=None, ikey=None):
        return self.p.op("vector",
                         lambda q: q.tensor_tensor_scan(out, d0, d0, 0.0, op0,
                                                        ALU.bypass),
                         outs=[out], ins=[d0], okey=okey, ikey=ikey)

    def recip(self, out, in_, okey=None, ikey=None):
        return self.p.op("vector", lambda q: q.reciprocal(out, in_),
                         outs=[out], ins=[in_], okey=okey, ikey=ikey)

    def dma(self, eng, out, in_, sem, okey=None, ikey=None, slow=False):
        return self.p.op(eng,
                         lambda q: q.dma_start(out=out, in_=in_,
                                               allow_slow_non_contiguous=slow),
                         outs=[out], ins=[in_], okey=okey, ikey=ikey, dma_sem=sem)

    def gather(self, out, in_, idx, sem, okey=None, ikey=None):
        return self.p.op("gpsimd",
                         lambda q: q.indirect_dma_start(
                             out=out, out_offset=None, in_=in_,
                             in_offset=IndirectOffsetOnAxis(ap=idx, axis=0)),
                         outs=[out], ins=[in_, idx], okey=okey, ikey=ikey, dma_sem=sem)

    def memset(self, eng, ap, val, okey=None):
        return self.p.op(eng, lambda q: q.memset(ap, val), outs=[ap], okey=okey)

    def cc(self, kind, op, cin, cout, okey=None, ikey=None):
        return self.p.op("gpsimd",
                         lambda q: q.collective_compute(kind, op, RG, ins=[cin.opt()],
                                                        outs=[cout.opt()]),
                         outs=[cout], ins=[cin], okey=okey, ikey=ikey,
                         dma_sem=("d_cc", 1))

    def vcol(self, name, i=0):
        return self.vecs[:, VC[name] + i: VC[name] + i + 1]

    def build(self):
        nc, st = self.nc, self.stack
        cfg = self.cfg
        self.rT = [st.enter_context(nc.sbuf_tensor(f"rT{f}", [128, EXT], F32)).ap()
                   for f in range(NFT)]
        self.pp = PsumPool(nc, st)
        ca = Arena(nc, st, "ca", 4 * VC_COLS + 4 * NFT
                   + sum(c * (2 if d == BF16 else 4) + 4 for _, c, d in CONST_LAYOUT) + 64)
        self.ctab = {}
        for nm, cols, dt in CONST_LAYOUT:
            self.ctab[nm] = ca.alloc(cols, dt)
        self.vecs = st.enter_context(nc.sbuf_tensor("vecs_t", [128, VC_COLS], F32)).ap()
        self.hidx = ca.alloc(NFT, U32)
        self.work = Arena(nc, st, "work", 127744)
        self._load_consts()
        self._phase_top()
        self.p.fence()
        if cfg["gla"]:
            self._phase_gla()
        else:
            self._park_zero()
        self.p.fence()
        for l in range(cfg["nlayers"]):
            self._layer_attn(l)
            self._layer_glu(l)
            self.p.fence()
        self._phase_final()
        with st:
            with nc.Block() as block:
                self.p.replay(block, st)
        return nc

    def _load_consts(self):
        dd = self.dram
        c = dd["consts"]
        off = 0
        for nm, cols, dt in CONST_LAYOUT:
            src = c[:, off:off + cols]
            if dt == BF16:
                self.dma("gpsimd", self.ctab[nm], src, "d_const", okey="CONST",
                         slow=(cols < 2))
            else:
                self.dma("sync", self.ctab[nm], src, "d_const", okey="CONST",
                         slow=(cols < 2))
            off += cols
        self.dma("sync", self.vecs, dd["vecs"], "d_const")
        self.dma("gpsimd", self.hidx, dd["hidx"], "d_const", okey="HIDX")

    def _ffn_setup(self):
        if getattr(self, "_ffn_ready", False):
            return
        self._ffn_ready = True
        wk = self.work
        self.fb = {"halo": wk.alloc(HALO)}
        for nm in ("cosq", "sinq", "cosk", "sink"):
            self.fb["t_" + nm] = wk.alloc(512, BF16)
        off = sum(c for _, c, _ in CONST_LAYOUT)
        self.tab_off = {}
        for nm, cols, _ in TAB_LAYOUT:
            self.tab_off[nm] = off
            off += cols
    def _phase_top(self):
        dd = self.dram
        wk = self.work
        wk.push()
        xt = [wk.alloc(E) for _ in range(2)]
        xsq = [wk.alloc(E) for _ in range(2)]
        xn = [wk.alloc(E) for _ in range(2)]
        stat = [wk.alloc(8) for _ in range(2)]
        ntile = (EXT + 127) // 128
        for i in range(ntile):
            rows = min(128, EXT - i * 128)
            b = i % 2
            kx, ks_, kq, kn = f"xt{b}", f"st{b}", f"xq{b}", f"xn{b}"
            x_t = xt[b][0:rows, :]
            self.dma("sync", x_t, dd["xs"][i * 128:i * 128 + rows, :], "d_x", okey=kx)
            mu = stat[b][0:rows, 0:1]
            sq = stat[b][0:rows, 1:2]
            var = stat[b][0:rows, 2:3]
            rstd = stat[b][0:rows, 3:4]
            self.red(mu, x_t, ALU.add, okey=ks_, ikey=kx)
            self.ts("vector", mu, mu, 1.0 / E, None, ALU.mult, okey=ks_, ikey=ks_)
            self.tt("vector", xsq[b][0:rows, :], x_t, x_t, ALU.mult, okey=kq, ikey=kx)
            self.red(sq, xsq[b][0:rows, :], ALU.add, okey=ks_, ikey=kq)
            self.tt("vector", var, mu, mu, ALU.mult, okey=ks_, ikey=ks_)
            self.stt("vector", var, sq, 1.0 / E, ALU.mult, var, ALU.subtract,
                     okey=ks_, ikey=ks_)
            self.act(var, var, AF.Ln, bias=self.vcol("c_eps")[0:rows, :],
                     okey=ks_, ikey=ks_)
            self.act(rstd, var, AF.Exp, scale=-0.5, okey=ks_, ikey=ks_)
            xn_t = xn[b][0:rows, :]
            self.stt("vector", xn_t, x_t, mu, ALU.subtract,
                     rstd.broadcast_to((rows, E)), ALU.mult, okey=kn, ikey=[kx, ks_])
            for f in range(NFT):
                pb = self.pp.acquire()
                ps = self.pp.aps[pb]
                pk = f"PS{pb}"
                self.trans(ps[0:128, 0:rows], xn_t[:, f * 128:(f + 1) * 128],
                           self.ctab["iden_f"][0:rows, 0:rows], okey=pk,
                           ikey=[kn, "CONST"])
                self.copy("vector", self.rT[f][:, i * 128:i * 128 + rows],
                          ps[0:128, 0:rows], okey=f"rT{f}", ikey=pk)
                self.pp.release(pb)
        wk.pop()

    def _park_zero(self):
        wk = self.work
        wk.push()
        z = wk.alloc(512)
        self.memset("vector", z, 0.0, okey="zz")
        for f in range(NFT):
            for c in range(4):
                self.dma("sync",
                         self.dram["glapark"][f * 128:(f + 1) * 128,
                                              c * 512:(c + 1) * 512],
                         z, "d_park", ikey="zz", okey="glapark")
        wk.pop()

    def _phase_gla(self):
        dd = self.dram
        wk = self.work
        wk.push()
        wq = [wk.alloc(E, BF16) for _ in range(NFT)]
        wkk = [wk.alloc(E, BF16) for _ in range(NFT)]
        wv = [wk.alloc(E, BF16) for _ in range(NFT)]
        for f in range(NFT):
            self.dma("gpsimd", wkk[f], dd["attn_Wk"][f * 128:(f + 1) * 128, :],
                     "d_w", okey=f"Wk{f}")
            self.dma("gpsimd", wv[f], dd["attn_Wv"][f * 128:(f + 1) * 128, :],
                     "d_w", okey=f"Wv{f}")
            self.dma("gpsimd", wq[f], dd["attn_Wq"][f * 128:(f + 1) * 128, :],
                     "d_w", okey=f"Wq{f}")
        S = wk.alloc(512)
        Sb = wk.alloc(512, BF16)
        Gtot = wk.alloc(NFT)
        self.memset("vector", S, 0.0, okey="S")
        self.memset("vector", Gtot, 0.0, okey="Gtot")
        NB = 256
        bufs = {
            "G": [wk.alloc(NB) for _ in range(NFT)],
            "k": [wk.alloc(NB, BF16) for _ in range(NFT)],
            "sc1": wk.alloc(NB), "sc2": wk.alloc(NB), "sc3": wk.alloc(CHUNK),
            "kdec": [wk.alloc(CHUNK, BF16) for _ in range(NFT)],
            "kh": [wk.alloc(CHUNK, BF16) for _ in range(NFT)],
            "qt": [wk.alloc(CHUNK, BF16) for _ in range(NFT)],
            "sq": [wk.alloc(NB, BF16) for _ in range(NFT)],
            "strm": [wk.alloc(NB, BF16) for _ in range(NFT)],
            "v": [wk.alloc(E, BF16) for _ in range(NB // 128)],
            "kdT": wk.alloc(E, BF16),
            "Ab": wk.alloc(2048, BF16),
            "gcol": wk.alloc(32),
            "sgate": [wk.alloc(E, BF16) for _ in range(NB // 128)],
            "of": wk.alloc(E),
            "sc64": wk.alloc(D),
            "oh": wk.alloc(E, BF16),
            "ohT": [wk.alloc(NB, BF16) for _ in range(NFT)],
            "msq": wk.alloc(32),
            "g1": wk.alloc(D),
            "g1b": wk.alloc(D, BF16),
            "g1T": wk.alloc(128, BF16, parts=64),
            "glaT": wk.alloc(256),
        }
        self._gla_pass(bufs, (wq, wkk, wv), S, Sb, Gtot, pass_b=False)
        self.p.fence()
        if self.cfg.get("exch", True):
            self._gla_exchange(S, Gtot, wk)
        else:
            self.memset("vector", S, 0.0, okey="S")
        self.p.fence()
        if self.cfg.get("passb", True):
            self._gla_pass(bufs, (wq, wkk, wv), S, Sb, Gtot, pass_b=True)
        wk.pop()

    def _gla_logf_chain(self, zps, zkey, f, ncols, bufs):
        G, k = bufs["G"][f], bufs["k"][f]
        sc1, sc2 = bufs["sc1"], bufs["sc2"]
        va = lambda ap: ap[:, 0:ncols]
        kG, kk = f"G{f}", f"k{f}"
        self.ts("vector", va(sc1), zps, self.vcol("delta", f), None, ALU.add,
                okey="sc1", ikey=[zkey, "vecs_t"])
        self.stt("vector", va(sc2), va(sc1), -1.0, ALU.mult, va(sc1), ALU.max,
                 okey="sc2", ikey="sc1")
        self.act(va(sc2), va(sc2), AF.Exp, scale=-1.0, okey="sc2", ikey="sc2")
        self.act(va(sc2), va(sc2), AF.Ln, bias=self.vcol("c_one"), okey="sc2",
                 ikey="sc2")
        self.stt("vector", va(G), va(sc1), 0.0, ALU.min, va(sc2), ALU.subtract,
                 okey=kG, ikey=["sc1", "sc2"])
        self.act(va(sc2), va(G), AF.Exp, okey="sc2", ikey=kG)
        self.ts("vector", va(k), va(sc2), -1.0, 1.0, ALU.mult, ALU.add,
                okey=kk, ikey="sc2")
        for c in range(ncols // CHUNK):
            s = slice(c * CHUNK, (c + 1) * CHUNK)
            self.scan(G[:, s], G[:, s], okey=kG, ikey=kG)

    def _gla_pass(self, bufs, ws, S, Sb, Gtot, pass_b):
        wq, wkk, wv = ws
        dd = self.dram
        tag = "B" if pass_b else "A"
        wk = self.work
        if pass_b:
            wk.push()
            self._wo = [wk.alloc(E, BF16) for _ in range(NFT)]
            self._g1w = [wk.alloc(D, BF16) for _ in range(NFT)]
            self._g2w = wk.alloc(E, BF16, parts=64)
            for f in range(NFT):
                self.dma("gpsimd", self._wo[f], dd["attn_Wo"][f * 128:(f + 1) * 128, :],
                         "d_w", okey=f"Wo{f}")
                self.ts("vector", self._wo[f], self._wo[f], self.vcol("normw", f),
                        None, ALU.mult, okey=f"Wo{f}", ikey=[f"Wo{f}", "vecs_t"])
                self.dma("gpsimd", self._g1w[f],
                         dd["attn_G1"][f * 128:(f + 1) * 128, :], "d_w", okey=f"G1{f}")
            self.dma("gpsimd", self._g2w, dd["attn_G2"][0:64, :], "d_w", okey="G2w")
        NB = 256
        strm = bufs["strm"]
        for blk in range(self.cfg.get("blocks", TC // NB)):
            c0 = HALO + blk * NB
            for f in range(NFT):
                self.copy("vector", strm[f][:, 0:NB], self.rT[f][:, c0:c0 + NB],
                          okey=f"strm{f}", ikey=f"rT{f}")
            qps = []
            if pass_b:
                for f in range(NFT):
                    pb = self.pp.acquire()
                    ps = self.pp.aps[pb]
                    for kt in range(NFT):
                        self.mm(ps[0:128, 0:NB], wq[kt][:, f * 128:(f + 1) * 128],
                                strm[kt][:, 0:NB],
                                start=(kt == 0), stop=(kt == NFT - 1),
                                okey=f"PS{pb}", ikey=[f"Wq{kt}", f"strm{kt}"])
                    qps.append(pb)
                for f in range(NFT):
                    pb = qps[f]
                    self.act(bufs["sq"][f][:, 0:NB], self.pp.aps[pb][0:128, 0:NB],
                             AF.Silu, okey=f"sq{f}", ikey=f"PS{pb}")
                    self.pp.release(pb)
                for mt in range(NB // 128):
                    pb = self.pp.acquire()
                    ps = self.pp.aps[pb]
                    for kt in range(NFT):
                        self.mm(ps[0:128, 0:64], strm[kt][:, mt * 128:(mt + 1) * 128],
                                self._g1w[kt], start=(kt == 0), stop=(kt == NFT - 1),
                                okey=f"PS{pb}", ikey=[f"strm{kt}", f"G1{kt}"])
                    self.copy("vector", bufs["g1"], ps[0:128, 0:64],
                              okey="g1", ikey=f"PS{pb}")
                    self.pp.release(pb)
                    self.copy("vector", bufs["g1b"], bufs["g1"], okey="g1b", ikey="g1")
                    pb = self.pp.acquire()
                    psb = self.pp.aps[pb].bitcast(BF16)
                    self.trans(psb[0:64, 0:128], bufs["g1b"], self.ctab["iden_b"],
                               okey=f"PS{pb}", ikey=["g1b", "CONST"])
                    self.copy("vector", bufs["g1T"], psb[0:64, 0:128],
                              okey="g1T", ikey=f"PS{pb}")
                    self.pp.release(pb)
                    for half in range(2):
                        pb = self.pp.acquire()
                        ps = self.pp.aps[pb]
                        self.mm(ps, bufs["g1T"], self._g2w[0:64, half * 512:half * 512 + 512],
                                okey=f"PS{pb}", ikey=["g1T", "G2w"])
                        self.act(bufs["sgate"][mt][:, half * 512:half * 512 + 512], ps,
                                 AF.Sigmoid, okey=f"sgate{mt}", ikey=f"PS{pb}")
                        self.pp.release(pb)
            if self.cfg.get("glastop", 9) < 2:
                continue
            for f in range(NFT):
                pb = self.pp.acquire()
                ps = self.pp.aps[pb]
                for kt in range(NFT):
                    self.mm(ps[0:128, 0:NB], wkk[kt][:, f * 128:(f + 1) * 128],
                            strm[kt][:, 0:NB],
                            start=(kt == 0), stop=(kt == NFT - 1),
                            okey=f"PS{pb}", ikey=[f"Wk{kt}", f"strm{kt}"])
                self._gla_logf_chain(ps[0:128, 0:NB], f"PS{pb}", f, NB, bufs)
                self.pp.release(pb)
            if self.cfg.get("glastop", 9) < 3:
                continue
            for mt in range(NB // 128):
                for half in range(2):
                    pb = self.pp.acquire()
                    ps = self.pp.aps[pb]
                    for kt in range(NFT):
                        self.mm(ps, strm[kt][:, mt * 128:(mt + 1) * 128],
                                wv[kt][:, half * 512:half * 512 + 512],
                                start=(kt == 0), stop=(kt == NFT - 1),
                                okey=f"PS{pb}", ikey=[f"strm{kt}", f"Wv{kt}"])
                    self.copy("scalar", bufs["v"][mt][:, half * 512:half * 512 + 512],
                              ps, okey=f"v{mt}", ikey=f"PS{pb}")
                    self.pp.release(pb)
            if self.cfg.get("glastop", 9) < 4:
                continue
            for cc in range(NB // CHUNK):
                self._gla_chunk(bufs, S, Sb, Gtot, blk, cc, pass_b)
        if pass_b:
            wk.pop()

    def _gla_chunk(self, bufs, S, Sb, Gtot, blk, cc, pass_b):
        cs = slice(cc * CHUNK, (cc + 1) * CHUNK)
        gcol = bufs["gcol"]
        sc3 = bufs["sc3"]
        for f in range(NFT):
            G = bufs["G"][f]
            GC = gcol[:, f:f + 1]
            self.copy("vector", GC, G[:, cc * CHUNK + 127:cc * CHUNK + 128],
                      okey="gcol", ikey=f"G{f}")
            self.ts("vector", sc3[:, 0:CHUNK], G[:, cs], -1.0, GC, ALU.mult, ALU.add,
                    okey="sc3", ikey=[f"G{f}", "gcol"])
            self.act(sc3[:, 0:CHUNK], sc3[:, 0:CHUNK], AF.Exp, okey="sc3", ikey="sc3")
            self.tt("vector", bufs["kdec"][f][:, 0:CHUNK], bufs["k"][f][:, cs],
                    sc3[:, 0:CHUNK], ALU.mult, okey=f"kdec{f}", ikey=[f"k{f}", "sc3"])
            if pass_b:
                Rc = gcol[:, 8 + f:9 + f]
                self.copy("vector", Rc, G[:, cc * CHUNK + 63:cc * CHUNK + 64],
                          okey="gcol", ikey=f"G{f}")
                self.ts("vector", sc3[:, 0:CHUNK], G[:, cs], Rc, None, ALU.subtract,
                        okey="sc3", ikey=[f"G{f}", "gcol"])
                self.act(sc3[:, 0:CHUNK], sc3[:, 0:CHUNK], AF.Exp, okey="sc3", ikey="sc3")
                self.tt("vector", bufs["qt"][f][:, 0:CHUNK], bufs["sq"][f][:, cs],
                        sc3[:, 0:CHUNK], ALU.mult, okey=f"qt{f}", ikey=[f"sq{f}", "sc3"])
                self.ts("vector", sc3[:, 0:CHUNK], G[:, cs], -1.0, Rc, ALU.mult, ALU.add,
                        okey="sc3", ikey=[f"G{f}", "gcol"])
                self.act(sc3[:, 0:CHUNK], sc3[:, 0:CHUNK], AF.Exp, okey="sc3", ikey="sc3")
                self.tt("vector", bufs["kh"][f][:, 0:CHUNK], bufs["k"][f][:, cs],
                        sc3[:, 0:CHUNK], ALU.mult, okey=f"kh{f}", ikey=[f"k{f}", "sc3"])
                eR = gcol[:, 16 + f:17 + f]
                self.act(eR, Rc, AF.Exp, okey="gcol", ikey="gcol")
                self.ts("vector", Sb[:, f * 64:(f + 1) * 64], S[:, f * 64:(f + 1) * 64],
                        eR, None, ALU.mult, okey="Sb", ikey=["S", "gcol"])
        if self.cfg.get("glastop", 9) < 5:
            return
        vtile = bufs["v"][cc]
        opair = []
        if pass_b:
            po = [self.pp.acquire(), self.pp.acquire()]
            for h in range(H):
                f, half = h // 2, h % 2
                pbase = half * 64
                pa = self.pp.acquire()
                aps = self.pp.aps[pa]
                self.mm(aps[0:128, 0:128], bufs["kh"][f][pbase:pbase + 64, 0:CHUNK],
                        bufs["qt"][f][pbase:pbase + 64, 0:CHUNK],
                        okey=f"PS{pa}", ikey=[f"kh{f}", f"qt{f}"], tp=(pbase, 0))
                ab = bufs["Ab"][:, h * 128:(h + 1) * 128]
                self.tt("vector", ab, aps[0:128, 0:128], self.ctab["triu_b"], ALU.mult,
                        okey="Ab", ikey=[f"PS{pa}", "CONST"])
                self.pp.release(pa)
                od = self.pp.aps[po[h // 8]][0:128, (h % 8) * 64:(h % 8) * 64 + 64]
                self.mm(od, ab, vtile[:, h * 64:(h + 1) * 64], start=True, stop=False,
                        okey=f"PS{po[h//8]}", ikey=["Ab", f"v{cc}"])
                self.mm(od, bufs["qt"][f][pbase:pbase + 64, 0:CHUNK],
                        Sb[pbase:pbase + 64, f * 64:(f + 1) * 64],
                        start=False, stop=True,
                        okey=f"PS{po[h//8]}", ikey=[f"qt{f}", "Sb"], tp=(pbase, 0))
            opair = po
        if self.cfg.get("glastop", 9) < 6:
            return
        pk = self.pp.acquire()
        pkb = self.pp.aps[pk].bitcast(BF16)
        kdT = bufs["kdT"]
        for f in range(NFT):
            slot = (f % 4) * 128
            self.trans(pkb[0:128, slot:slot + 128], bufs["kdec"][f][:, 0:CHUNK],
                       self.ctab["iden_b"], okey=f"PS{pk}",
                       ikey=[f"kdec{f}", "CONST"])
            self.copy("vector", kdT[:, f * 128:(f + 1) * 128], pkb[0:128, slot:slot + 128],
                      okey="kdT", ikey=f"PS{pk}")
        self.pp.release(pk)
        if self.cfg.get("glastop", 9) < 7:
            self.pp.release(pk) if False else None
            return
        pS = self.pp.acquire()
        psS = self.pp.aps[pS]
        for h in range(H):
            f, half = h // 2, h % 2
            self.mm(psS[half * 64:half * 64 + 64, f * 64:(f + 1) * 64],
                    kdT[:, h * 64:(h + 1) * 64], vtile[:, h * 64:(h + 1) * 64],
                    tp=(0, half * 64), okey=f"PS{pS}", ikey=["kdT", f"v{cc}"])
        if self.cfg.get("glastop", 9) < 8:
            self.pp.release(pS)
            return
        for f in range(NFT):
            PcC = gcol[:, 24 + f:25 + f]
            self.act(PcC, gcol[:, f:f + 1], AF.Exp, okey="gcol", ikey="gcol")
            self.stt("vector", S[:, f * 64:(f + 1) * 64], S[:, f * 64:(f + 1) * 64],
                     PcC, ALU.mult, psS[:, f * 64:(f + 1) * 64], ALU.add,
                     okey="S", ikey=["S", "gcol", f"PS{pS}"])
        self.pp.release(pS)
        self.tt("vector", Gtot, Gtot, gcol[:, 0:NFT], ALU.add,
                okey="Gtot", ikey=["Gtot", "gcol"])
        if pass_b:
            self._gla_out(opair, bufs, blk, cc)
            for b in opair:
                self.pp.release(b)

    def _gla_out(self, po, bufs, blk, cc):
        of, oh = bufs["of"], bufs["oh"]
        sg = bufs["sgate"][cc]
        self.tt("vector", of[:, 0:512], self.pp.aps[po[0]], sg[:, 0:512], ALU.mult,
                okey="of", ikey=[f"PS{po[0]}", f"sgate{cc}"])
        self.tt("vector", of[:, 512:1024], self.pp.aps[po[1]], sg[:, 512:1024],
                ALU.mult, okey="of", ikey=[f"PS{po[1]}", f"sgate{cc}"])
        self.act(oh, of, AF.Square, okey="oh", ikey="of")
        msq = bufs["msq"][:, 0:16]
        rstd = bufs["msq"][:, 16:32]
        self.red(msq, oh.rearrange("p (h d) -> p h d", d=D), ALU.add,
                 okey="msq", ikey="oh")
        self.act(msq, msq, AF.Ln, bias=self.vcol("c_epsD"), okey="msq", ikey="msq")
        self.act(rstd, msq, AF.Exp, scale=-0.5, bias=self.vcol("c_hlnD"),
                 okey="msq", ikey="msq")
        self.tt("vector", oh, of,
                rstd.rearrange("p (h o) -> p h o", o=1).broadcast_to((128, H, D)), ALU.mult,
                okey="oh", ikey=["of", "msq"])
        for f in range(NFT):
            pb = self.pp.acquire()
            psb = self.pp.aps[pb].bitcast(BF16)
            self.trans(psb[0:128, 0:128], oh[:, f * 128:(f + 1) * 128],
                       self.ctab["iden_b"], okey=f"PS{pb}", ikey=["oh", "CONST"])
            self.copy("vector", bufs["ohT"][f][:, cc * 128:(cc + 1) * 128],
                      psb[0:128, 0:128], okey=f"ohT{f}", ikey=f"PS{pb}")
            self.pp.release(pb)
        if cc == 1:
            glaT = bufs["glaT"]
            for mt in range(NFT):
                pb = self.pp.acquire()
                ps = self.pp.aps[pb]
                for kt in range(NFT):
                    self.mm(ps[0:128, 0:256], self._wo[kt][:, mt * 128:(mt + 1) * 128],
                            bufs["ohT"][kt][:, 0:256], start=(kt == 0),
                            stop=(kt == NFT - 1),
                            okey=f"PS{pb}", ikey=[f"Wo{kt}", f"ohT{kt}"])
                self.copy("vector", glaT, ps[0:128, 0:256],
                          okey="glaT", ikey=f"PS{pb}")
                self.pp.release(pb)
                self.dma("sync",
                         self.dram["glapark"][mt * 128:(mt + 1) * 128,
                                              blk * 256:(blk + 1) * 256],
                         glaT, "d_park", ikey="glaT", okey="glapark")

    def _gla_exchange(self, S, Gtot, wk):
        dd = self.dram
        wk.push()
        pack = wk.alloc(520)
        self.copy("vector", pack[:, 0:512], S, okey="pack", ikey="S")
        self.act(pack[:, 512:520], Gtot, AF.Exp, okey="pack", ikey="Gtot")
        self.dma("sync", dd["cin_s"], pack, "d_cc2", ikey="pack", okey="cin_s")
        self.cc("AllGather", ALU.bypass, dd["cin_s"], dd["cout_s"],
                okey="cout_s", ikey="cin_s")
        self.memset("vector", S, 0.0, okey="S")
        rbuf = wk.alloc(520)
        pr = wk.alloc(NFT)
        rs = wk.alloc(512)
        for r in range(NCORES):
            self.dma("sync", rbuf, dd["cout_s"][r * 128:(r + 1) * 128, :], "d_cc2",
                     okey="rbuf", ikey="cout_s")
            self.stt("vector", pr, rbuf[:, 512:520], self.vcol("rkM", r), ALU.mult,
                     self.vcol("rkI", r).broadcast_to((128, NFT)), ALU.add,
                     okey="pr", ikey=["rbuf", "vecs_t"])
            self.ts("vector", rs, rbuf[:, 0:512], self.vcol("rkM", r), None, ALU.mult,
                    okey="rs", ikey=["rbuf", "vecs_t"])
            for f in range(NFT):
                self.stt("vector", S[:, f * 64:(f + 1) * 64], S[:, f * 64:(f + 1) * 64],
                         pr[:, f:f + 1], ALU.mult, rs[:, f * 64:(f + 1) * 64], ALU.add,
                         okey="S", ikey=["S", "pr", "rs"])
        wk.pop()

    def _ffn_setup(self):
        if getattr(self, "_ffn_ready", False):
            return
        self._ffn_ready = True
        wk = self.work
        self.fb = {"halo": wk.alloc(HALO)}
        for nm in ("cosq", "sinq", "cosk", "sink"):
            self.fb["t_" + nm] = wk.alloc(512, BF16)
        off = sum(c for _, c, _ in CONST_LAYOUT)
        self.tab_off = {}
        for nm, cols, _ in TAB_LAYOUT:
            self.tab_off[nm] = off
            off += cols

    def _attn_frame(self):
        wk = self.work
        fb = dict(self.fb)
        fb["wqkv"] = [wk.alloc(3 * E, BF16) for _ in range(NFT)]
        fb["wo"] = [wk.alloc(E, BF16) for _ in range(NFT)]
        fb["strm"] = [wk.alloc(257, BF16) for _ in range(NFT)]
        fb["xb"] = wk.alloc(W, BF16)
        fb["q2"] = wk.alloc(W, BF16)
        fb["t2"] = wk.alloc(W, BF16)
        fb["qt"] = [[wk.alloc(W, BF16) for _ in range(NFT)] for _ in range(2)]
        fb["kring"] = [[wk.alloc(W, BF16) for _ in range(NFT)] for _ in range(2)]
        fb["vring"] = [[wk.alloc(E, BF16) for _ in range(2)] for _ in range(2)]
        fb["prevk"] = [wk.alloc(W, BF16) for _ in range(NFT)]
        fb["sims"] = [wk.alloc(2 * 384) for _ in range(2)]
        fb["P"] = [wk.alloc(2 * 384, BF16) for _ in range(2)]
        fb["PTs"] = [wk.alloc(1024, BF16) for _ in range(2)]
        fb["attnT"] = [wk.alloc(W, BF16) for _ in range(NFT)]
        fb["rqT"] = [wk.alloc(16) for _ in range(16)]
        fb["rn"] = wk.alloc(256)
        fb["rnb"] = wk.alloc(256, BF16)
        fb["cols"] = [wk.alloc(8) for _ in range(2)]
        fb["oldc"] = [wk.alloc(1) for _ in range(NFT)]
        return fb

    GRP = [(0, 6), (6, 12), (12, 18), (18, 22)]

    def _glu_frame(self):
        wk = self.work
        fb = dict(self.fb)
        fb["rsh"] = [wk.alloc(512, BF16) for _ in range(NFT)]
        fb["sq"] = [wk.alloc(512, BF16) for _ in range(NFT)]
        fb["xnT"] = [[wk.alloc(512, BF16) for _ in range(NFT)] for _ in range(2)]
        fb["w1g"] = [[[wk.alloc(768, BF16) for _ in range(NFT)] for _ in range(2)]
                     for _ in range(2)]
        fb["w2t"] = [wk.alloc(E, BF16) for _ in range(4)]
        fb["ht"] = [wk.alloc(512, BF16) for _ in range(22)]
        fb["gel"] = wk.alloc(512)
        fb["srow"] = wk.alloc(3 * 512)
        fb["oldg"] = [wk.alloc(3) for _ in range(NFT)]
        return fb

    def _halo_in(self, l, kind):
        fb = self.fbl
        dd = self.dram
        cout = dd["cout_h"] if kind == "a" else dd["cout_g"]
        ncol = HALO if kind == "a" else 1
        dst0 = 0 if kind == "a" else 256
        for f in range(NFT):
            g = self.gather(fb["halo"][:, 0:ncol], cout[:, 0:ncol],
                            self.hidx[:, f:f + 1], "d_gath",
                            okey="HALO", ikey=[f"cout_{kind}", "HIDX"])
            self.ts("vector", self.rT[f][:, dst0:dst0 + ncol], fb["halo"][:, 0:ncol],
                    self.vcol("halomul"), None, ALU.mult,
                    okey=f"rT{f}", ikey=["HALO"])

    def _halo_out(self, l, kind):
        dd = self.dram
        if kind == "a":
            cin, cout, c0, ncol = dd["cin_h"], dd["cout_h"], EXT - HALO, HALO
        else:
            cin, cout, c0, ncol = dd["cin_g"], dd["cout_g"], EXT - 1, 1
        for f in range(NFT):
            self.dma("sync", cin[f * 128:(f + 1) * 128, :],
                     self.rT[f][:, c0:c0 + ncol], "d_cc2",
                     okey=f"cin_{kind}", ikey=f"rT{f}", slow=(ncol < 2))
        self.cc("AllGather", ALU.bypass, cin, cout,
                okey=f"cout_{kind}", ikey=f"cin_{kind}")

    def _attn_proj_block(self, l, c0, n, is_halo):
        fb = self.fbl
        ct = self.ctab
        dd = self.dram
        blkw0 = (c0 - HALO) // W
        tabn = ("cosk", "sink") if is_halo else ("cosk", "sink", "cosq", "sinq")
        for nm in tabn:
            self.dma("gpsimd", fb["t_" + nm][:, 0:n],
                     dd["consts"][:, self.tab_off[nm] + c0:self.tab_off[nm] + c0 + n],
                     "d_tab", okey=f"ftab_{nm}")
        use_oldc = c0 > 257
        for f in range(NFT):
            if use_oldc:
                self.copy("vector", fb["strm"][f][:, 0:1], fb["oldc"][f],
                          okey=f"fstrm{f}", ikey=f"foldc{f}")
                self.copy("vector", fb["strm"][f][:, 1:n + 1],
                          self.rT[f][:, c0:c0 + n], okey=f"fstrm{f}", ikey=f"rT{f}")
            else:
                self.copy("vector", fb["strm"][f][:, 0:n + 1],
                          self.rT[f][:, c0 - 1:c0 + n], okey=f"fstrm{f}", ikey=f"rT{f}")
        def rhs(kt, a, b):
            off = 1 if kt < 4 else 0
            return fb["strm"][kt][:, a + off:b + off]
        for part, tabs, scol in (("k", ("cosk", "sink"), "ks"),
                                 ("q", ("cosq", "sinq"), "qs")):
            if part == "q" and is_halo:
                continue
            sspb = self.pp.acquire()
            ssps = self.pp.aps[sspb]
            for f in range(NFT):
                pb = self.pp.acquire()
                ps = self.pp.aps[pb]
                for kt in range(NFT):
                    self.mm(ps[0:128, 0:n],
                            fb["wqkv"][kt][:, (0 if part == "q" else E) + f * 128:
                                           (0 if part == "q" else E) + f * 128 + 128],
                            rhs(kt, 0, n), start=(kt == 0), stop=(kt == NFT - 1),
                            okey=f"PS{pb}", ikey=[f"fw{kt}", f"fstrm{kt}"])
                self.act(fb["q2"][:, 0:n], ps[0:128, 0:n], AF.Square,
                         okey="fq2", ikey=f"PS{pb}")
                self.act(fb["xb"][:, 0:n], ps[0:128, 0:n], AF.Copy,
                         scale=self.vcol(scol, l), okey="fxb", ikey=f"PS{pb}")
                self.pp.release(pb)
                self.mm(ssps[0:16, 0:n], ct["bdall"][:, f * 16:(f + 1) * 16],
                        fb["q2"][:, 0:n], start=(f == 0), stop=(f == NFT - 1),
                        okey=f"PS{sspb}", ikey=["ca", "fq2"])
                pr = self.pp.acquire()
                prs = self.pp.aps[pr]
                self.mm(prs[0:128, 0:n], ct["RT_b"], fb["xb"][:, 0:n],
                        okey=f"PS{pr}", ikey=["ca", "fxb"])
                self.tt("vector", fb["q2"][:, 0:n], fb["xb"][:, 0:n],
                        fb["t_" + tabs[0]][:, 0:n], ALU.mult, okey="fq2",
                        ikey=["fxb", f"ftab_{tabs[0]}"])
                self.stt("vector", fb["t2"][:, 0:n], prs[0:128, 0:n], 0.0, ALU.add,
                         fb["t_" + tabs[1]][:, 0:n], ALU.mult, okey="ft2",
                         ikey=[f"PS{pr}", f"ftab_{tabs[1]}"])
                self.pp.release(pr)
                if part == "q":
                    wpar = blkw0 % 2
                    self.tt("vector", fb["qt"][wpar][f][:, 0:n], fb["q2"][:, 0:n],
                            fb["t2"][:, 0:n], ALU.add, okey=f"fqt{wpar}_{f}",
                            ikey=["fq2", "ft2"])
                else:
                    self.tt("vector", fb["xb"][:, 0:n], fb["q2"][:, 0:n],
                            fb["t2"][:, 0:n], ALU.add, okey="fxb",
                            ikey=["fq2", "ft2"])
                    self._krope_pending = True
                    for i in range(n // W):
                        w = blkw0 + i
                        self.copy("vector", fb["kring"][w % 2][f][:, 0:W],
                                  fb["xb"][:, i * W:(i + 1) * W],
                                  okey=f"fkr{w%2}_{f}", ikey="fxb")
            self.ts("vector", fb["rn"][0:16, 0:n], ssps[0:16, 0:n], 1e-24, None,
                    ALU.max, okey="frn", ikey=f"PS{sspb}")
            self.pp.release(sspb)
            self.act(fb["rn"][0:16, 0:n], fb["rn"][0:16, 0:n], AF.Ln,
                     okey="frn", ikey="frn")
            self.act(fb["rn"][0:16, 0:n], fb["rn"][0:16, 0:n], AF.Exp, scale=-0.5,
                     okey="frn", ikey="frn")
            if part == "q":
                for i in range(n // 128):
                    tt = (c0 - HALO) // 128 + i
                    pb = self.pp.acquire()
                    ps = self.pp.aps[pb]
                    self.trans(ps[0:128, 0:16], fb["rn"][0:16, i * 128:(i + 1) * 128],
                               ct["iden_f"][0:16, 0:16], okey=f"PS{pb}",
                               ikey=["frn", "ca"])
                    self.copy("vector", fb["rqT"][tt], ps[0:128, 0:16],
                              okey=f"frqT{tt}", ikey=f"PS{pb}")
                    self.pp.release(pb)
            else:
                self.copy("vector", fb["rnb"][0:16, 0:n], fb["rn"][0:16, 0:n],
                          okey="frnb", ikey="frn")
                for f in range(NFT):
                    pb = self.pp.acquire()
                    ps = self.pp.aps[pb]
                    self.mm(ps[0:128, 0:n], ct["hmall"][0:16, f * 128:(f + 1) * 128],
                            fb["rnb"][0:16, 0:n], okey=f"PS{pb}", ikey=["ca", "frnb"])
                    for i in range(n // W):
                        w = blkw0 + i
                        ring = fb["kring"][w % 2][f]
                        self.tt("vector", ring, ring,
                                ps[0:128, i * W:(i + 1) * W], ALU.mult,
                                okey=f"fkr{w%2}_{f}", ikey=[f"fkr{w%2}_{f}", f"PS{pb}"])
                    self.pp.release(pb)
        for i in range(n // 128):
            w = blkw0 + (i // 2)
            vt = fb["vring"][w % 2][i % 2]
            for half in range(2):
                pb = self.pp.acquire()
                ps = self.pp.aps[pb]
                for kt in range(NFT):
                    self.mm(ps, rhs(kt, i * 128, (i + 1) * 128),
                            fb["wqkv"][kt][:, 2 * E + half * 512:2 * E + half * 512 + 512],
                            start=(kt == 0), stop=(kt == NFT - 1),
                            okey=f"PS{pb}", ikey=[f"fstrm{kt}", f"fw{kt}"])
                self.copy("scalar", vt[:, half * 512:half * 512 + 512], ps,
                          okey=f"fv{w%2}_{i%2}", ikey=f"PS{pb}")
                self.pp.release(pb)

    def _attn_window(self, l, w):
        fb = self.fbl
        ct = self.ctab
        wp = w % 2
        for f in range(NFT):
            kp = fb["kring"][(w - 1) % 2][f]
            self.ts("vector", fb["prevk"][f], kp, self.vcol("skb2"), None,
                    ALU.mult, okey=f"fpk{f}", ikey=[f"fkr{(w-1)%2}_{f}"])
        for h in range(H):
            f, half = h // 2, h % 2
            hp = h % 2
            pbq = half * 64
            kq = f"fqt{wp}_{f}"
            simbanks = []
            for rt in range(2):
                pb = self.pp.acquire()
                ps = self.pp.aps[pb]
                qlhs = fb["qt"][wp][f][pbq:pbq + 64, rt * 128:rt * 128 + 128]
                pk = fb["prevk"][f][pbq:pbq + 64, :]
                kc = fb["kring"][w % 2][f][pbq:pbq + 64, :]
                if rt == 0:
                    self.mm(ps[0:128, 0:256], qlhs, pk, okey=f"PS{pb}",
                            ikey=[kq, f"fpk{f}"], tp=(pbq, 0))
                    self.mm(ps[0:128, 256:384], qlhs, kc[:, 0:128], okey=f"PS{pb}",
                            ikey=[kq, f"fkr{w%2}_{f}"], tp=(pbq, 0))
                else:
                    self.mm(ps[0:128, 0:128], qlhs, pk[:, 128:256], okey=f"PS{pb}",
                            ikey=[kq, f"fpk{f}"], tp=(pbq, 0))
                    self.mm(ps[0:128, 128:384], qlhs, kc, okey=f"PS{pb}",
                            ikey=[kq, f"fkr{w%2}_{f}"], tp=(pbq, 0))
                simbanks.append(pb)
            ssc = fb["cols"][hp]
            kc_ = f"fcols{hp}"
            ksims = f"fsims{hp}"
            kP = f"fP{hp}"
            kPT = f"fPTs{hp}"
            for rt in range(2):
                pb = simbanks[rt]
                ps = self.pp.aps[pb]
                mname = f"maskf{rt}" if w == 0 else f"mask{rt}"
                self.tt("vector", fb["sims"][hp][:, rt * 384:rt * 384 + 384],
                        ps[0:128, 0:384], ct[mname], ALU.add,
                        okey=ksims, ikey=[f"PS{pb}", "ca"])
                self.pp.release(pb)
                self.red(ssc[:, rt:rt + 1], fb["sims"][hp][:, rt * 384:rt * 384 + 384],
                         ALU.max, negate=True, okey=kc_, ikey=ksims)
                rq = fb["rqT"][w * 2 + rt][:, h:h + 1]
                self.tt("vector", ssc[:, 2 + rt:3 + rt], ssc[:, rt:rt + 1], rq,
                        ALU.mult, okey=kc_, ikey=[kc_, f"frqT{w*2+rt}"])
                self.act(fb["P"][hp][:, rt * 384:rt * 384 + 384],
                         fb["sims"][hp][:, rt * 384:rt * 384 + 384], AF.Exp,
                         bias=ssc[:, 2 + rt:3 + rt], scale=rq,
                         accum=ssc[:, 4 + rt:5 + rt],
                         okey=kP, ikey=[ksims, kc_, f"frqT{w*2+rt}"])
            self.recip(ssc[:, 6:8], ssc[:, 4:6], okey=kc_, ikey=kc_)
            for rt in range(2):
                self.ts("vector", fb["P"][hp][:, rt * 384:rt * 384 + 384],
                        fb["P"][hp][:, rt * 384:rt * 384 + 384],
                        ssc[:, 6 + rt:7 + rt],
                        None, ALU.mult, okey=kP, ikey=[kP, kc_])
            pt = self.pp.acquire()
            ptb = self.pp.aps[pt].bitcast(BF16)
            for rt in range(2):
                for kt in range(rt, rt + 3):
                    self.trans(ptb[0:128, kt * 256 + rt * 128:kt * 256 + rt * 128 + 128],
                               fb["P"][hp][:, rt * 384 + (kt - rt) * 128:
                                           rt * 384 + (kt - rt) * 128 + 128],
                               ct["iden_b"], okey=f"PS{pt}", ikey=[kP, "ca"])
            self.copy("vector", fb["PTs"][hp][:, 0:128], ptb[0:128, 0:128],
                      okey=kPT, ikey=f"PS{pt}")
            self.copy("vector", fb["PTs"][hp][:, 256:768], ptb[0:128, 256:768],
                      okey=kPT, ikey=f"PS{pt}")
            self.copy("vector", fb["PTs"][hp][:, 896:1024], ptb[0:128, 896:1024],
                      okey=kPT, ikey=f"PS{pt}")
            self.pp.release(pt)
            if half == 0:
                self._opair = self.pp.acquire()
            ob = self._opair
            ops = self.pp.aps[ob]
            for qt_i in range(2):
                valid = [kt for kt in range(4) if qt_i <= kt <= qt_i + 2]
                for j, kt in enumerate(valid):
                    if kt < 2:
                        vt = fb["vring"][(w - 1) % 2][kt]
                        vk = f"fv{(w-1)%2}_{kt}"
                    else:
                        vt = fb["vring"][w % 2][kt - 2]
                        vk = f"fv{w%2}_{kt-2}"
                    self.mm(ops[pbq:pbq + 64, qt_i * 128:qt_i * 128 + 128],
                            vt[:, h * 64:(h + 1) * 64],
                            fb["PTs"][hp][:, kt * 256 + qt_i * 128:
                                          kt * 256 + qt_i * 128 + 128],
                            start=(j == 0), stop=(j == len(valid) - 1),
                            okey=f"PS{ob}", ikey=[vk, kPT], tp=(0, pbq))
            if half == 1:
                self.copy("vector", fb["attnT"][f][:, 0:W], ops[0:128, 0:W],
                          okey=f"fattnT{f}", ikey=f"PS{ob}")
                self.pp.release(ob)

    def _attn_wo(self, l, wlist):
        fb = self.fbl
        gcols = len(wlist) * W
        assert len(wlist) == 1
        c0 = HALO + wlist[0] * W
        for f in range(NFT):
            self.copy("vector", fb["oldc"][f], self.rT[f][:, c0 + W - 1:c0 + W],
                      okey=f"foldc{f}", ikey=f"rT{f}")
        for mt in range(NFT):
            pb = self.pp.acquire()
            ps = self.pp.aps[pb]
            for kt in range(NFT):
                self.mm(ps[0:128, 0:gcols], fb["wo"][kt][:, mt * 128:(mt + 1) * 128],
                        fb["attnT"][kt][:, 0:gcols], start=(kt == 0),
                        stop=(kt == NFT - 1), okey=f"PS{pb}",
                        ikey=[f"fwo{kt}", f"fattnT{kt}"])
            self.tt("vector", self.rT[mt][:, c0:c0 + gcols], ps[0:128, 0:gcols],
                    self.rT[mt][:, c0:c0 + gcols], ALU.add,
                    okey=f"rT{mt}", ikey=[f"PS{pb}", f"rT{mt}"])
            self.pp.release(pb)

    def _layer_attn(self, l):
        dd = self.dram
        self._ffn_setup()
        self.work.push()
        fb = self.fbl = self._attn_frame()
        for f in range(NFT):
            self.dma("gpsimd", fb["wqkv"][f],
                     dd["ffn_Wqkv"][l, f * 128:(f + 1) * 128, :], "d_w",
                     okey=f"fw{f}")
            self.dma("gpsimd", fb["wo"][f],
                     dd["ffn_Wo"][l, f * 128:(f + 1) * 128, :], "d_w",
                     okey=f"fwo{f}")
        if l > 0:
            self._halo_in(l, "a")
        self._attn_proj_block(l, 1, 256, True)
        for w in range(8):
            self._attn_proj_block(l, 257 + w * W, 256, False)
            self._attn_window(l, w)
            self._attn_wo(l, [w])
        self._halo_out(l, "g")
        self.work.pop()
        self.p.fence()

    def _layer_glu(self, l):
        dd = self.dram
        ct = self.ctab
        self.work.push()
        fb = self.fbl = self._glu_frame()
        NC = 512
        for f in range(NFT):
            for j in range(3):
                self.copy("vector", fb["oldg"][f][:, j:j + 1],
                          self.rT[f][:, HALO + (j + 1) * NC - 1:HALO + (j + 1) * NC],
                          okey=f"goldg{f}", ikey=f"rT{f}")
        for ci, ch in enumerate([1, 2, 3, 0]):
            c0 = HALO + ch * NC
            if ch == 0:
                self._halo_in(l, "g")
            for f in range(NFT):
                if f < 4:
                    self.copy("vector", fb["rsh"][f], self.rT[f][:, c0:c0 + NC],
                              okey=f"grsh{f}", ikey=f"rT{f}")
                else:
                    if ch == 0:
                        self.copy("vector", fb["rsh"][f][:, 0:1],
                                  self.rT[f][:, c0 - 1:c0], okey=f"grsh{f}",
                                  ikey=f"rT{f}")
                    else:
                        self.copy("vector", fb["rsh"][f][:, 0:1],
                                  fb["oldg"][f][:, ch - 1:ch], okey=f"grsh{f}",
                                  ikey=f"goldg{f}")
                    self.copy("vector", fb["rsh"][f][:, 1:NC],
                              self.rT[f][:, c0:c0 + NC - 1], okey=f"grsh{f}",
                              ikey=f"rT{f}")
                self.act(fb["sq"][f], fb["rsh"][f], AF.Square,
                         okey=f"gsq{f}", ikey=f"grsh{f}")
            p1 = self.pp.acquire()
            p2 = self.pp.acquire()
            s1 = self.pp.aps[p1]
            s2 = self.pp.aps[p2]
            for f in range(NFT):
                self.mm(s1[0:1, 0:NC], ct["ones_b"], fb["rsh"][f],
                        start=(f == 0), stop=(f == NFT - 1), okey=f"PS{p1}",
                        ikey=["ca", f"grsh{f}"])
                self.mm(s2[0:1, 0:NC], ct["ones_b"], fb["sq"][f],
                        start=(f == 0), stop=(f == NFT - 1), okey=f"PS{p2}",
                        ikey=["ca", f"gsq{f}"])
            mu = fb["srow"][0:1, 0:NC]
            ms = fb["srow"][0:1, 512:512 + NC]
            rst = fb["srow"][0:1, 1024:1024 + NC]
            self.ts("vector", mu, s1[0:1, 0:NC], 1.0 / E, None, ALU.mult,
                    okey="gmu", ikey=f"PS{p1}")
            self.ts("vector", ms, s2[0:1, 0:NC], 1.0 / E, None, ALU.mult,
                    okey="gms", ikey=f"PS{p2}")
            self.pp.release(p1)
            self.pp.release(p2)
            self.tt("vector", rst, mu, mu, ALU.mult, okey="grst", ikey="gmu")
            self.tt("vector", rst, ms, rst, ALU.subtract, okey="grst",
                    ikey=["gms", "grst"])
            self.act(rst, rst, AF.Ln, bias=self.vcol("c_eps")[0:1, :],
                     okey="grst", ikey="grst")
            self.act(rst, rst, AF.Exp, scale=-0.5, okey="grst", ikey="grst")
            pmu = self.pp.acquire()
            prs = self.pp.acquire()
            self.mm(self.pp.aps[pmu][0:128, 0:NC], ct["ones_row"][0:1, :], mu,
                    okey=f"PS{pmu}", ikey=["ca", "gmu"])
            self.mm(self.pp.aps[prs][0:128, 0:NC], ct["ones_row"][0:1, :], rst,
                    okey=f"PS{prs}", ikey=["ca", "grst"])
            for f in range(NFT):
                self.tt("vector", fb["gel"][:, 0:NC], fb["rsh"][f],
                        self.pp.aps[pmu][0:128, 0:NC], ALU.subtract,
                        okey="ggel", ikey=[f"grsh{f}", f"PS{pmu}"])
                self.stt("vector", fb["gel"][:, 0:NC], fb["gel"][:, 0:NC],
                         self.vcol("lng", l * NFT + f), ALU.mult,
                         self.pp.aps[prs][0:128, 0:NC], ALU.mult,
                         okey="ggel", ikey=["ggel", f"PS{prs}"])
                self.ts("vector", fb["xnT"][ci % 2][f], fb["gel"][:, 0:NC],
                        self.vcol("lnb", l * NFT + f), None, ALU.add,
                        okey=f"gxnT{ci%2}_{f}", ikey="ggel")
            self.pp.release(pmu)
            self.pp.release(prs)
            for gi, (t0, t1) in enumerate(self.GRP):
                par = gi % 2
                gc0 = t0 * 128
                gcols = min(t1 * 128, DI) - gc0
                for half in range(2):
                    for f in range(NFT):
                        self.dma("gpsimd", fb["w1g"][par][half][f][:, 0:gcols],
                                 dd["ffn_W1"][l, f * 128:(f + 1) * 128,
                                              half * DI + gc0:half * DI + gc0 + gcols],
                                 "d_w1", okey=f"gw1_{par}_{half}_{f}")
                for ct_i in range(t0, t1):
                    rows = min(128, DI - ct_i * 128)
                    lc = ct_i * 128 - gc0
                    banks = []
                    for half in range(2):
                        pb = self.pp.acquire()
                        ps = self.pp.aps[pb]
                        for kt in range(NFT):
                            self.mm(ps[0:rows, 0:NC],
                                    fb["w1g"][par][half][kt][:, lc:lc + rows],
                                    fb["xnT"][ci % 2][kt], start=(kt == 0),
                                    stop=(kt == NFT - 1), okey=f"PS{pb}",
                                    ikey=[f"gw1_{par}_{half}_{kt}"]
                                    + [f"gxnT{ci%2}_{k}" for k in range(NFT)])
                        banks.append(pb)
                    pa, pb = banks
                    self.act(fb["gel"][0:rows, 0:NC], self.pp.aps[pb][0:rows, 0:NC],
                             AF.Gelu, bias=self.vcol("b1g", l * 22 + ct_i)[0:rows, :],
                             okey="ggel", ikey=f"PS{pb}")
                    self.pp.release(pb)
                    self.stt("vector", fb["ht"][ct_i][0:rows, 0:NC],
                             self.pp.aps[pa][0:rows, 0:NC],
                             self.vcol("b1a", l * 22 + ct_i)[0:rows, :],
                             ALU.add, fb["gel"][0:rows, 0:NC], ALU.mult,
                             okey=f"ght{ct_i}", ikey=[f"PS{pa}", "ggel"])
                    self.pp.release(pa)
            mtb = [self.pp.acquire() for _ in range(8)]
            for kt in range(22):
                rows = min(128, DI - kt * 128)
                wb = fb["w2t"][kt % 4]
                self.dma("gpsimd", wb[0:rows, :],
                         dd["ffn_W2"][l, kt * 128:kt * 128 + rows, :], "d_w2",
                         okey=f"gw2_{kt%4}")
                for mt in range(NFT):
                    self.mm(self.pp.aps[mtb[mt]][0:128, 0:NC],
                            fb["w2t"][kt % 4][0:rows, mt * 128:(mt + 1) * 128],
                            fb["ht"][kt][0:rows, 0:NC],
                            start=(kt == 0), stop=(kt == 21),
                            okey=f"PS{mtb[mt]}", ikey=[f"gw2_{kt%4}", f"ght{kt}"])
            for mt in range(NFT):
                self.stt("vector", self.rT[mt][:, c0:c0 + NC],
                         self.pp.aps[mtb[mt]][0:128, 0:NC],
                         self.vcol("b2", l * NFT + mt), ALU.add,
                         self.rT[mt][:, c0:c0 + NC], ALU.add,
                         okey=f"rT{mt}", ikey=[f"PS{mtb[mt]}", f"rT{mt}"])
                self.pp.release(mtb[mt])
            if ch == 3 and l < self.cfg["nlayers"] - 1:
                self._halo_out(l, "a")
        self.work.pop()

    def _phase_final(self):
        dd = self.dram
        wk = self.work
        wk.push()
        gl = [wk.alloc(512) for _ in range(2)]
        xt = [wk.alloc(E) for _ in range(2)]
        ot = [wk.alloc(128) for _ in range(2)]
        s = wk.alloc(512)
        last_y = None
        for mt in range(NFT):
            for c in range(4):
                b = c % 2
                self.dma("sync", gl[b],
                         dd["glapark"][mt * 128:(mt + 1) * 128, c * 512:(c + 1) * 512],
                         "d_park2", okey=f"gl{b}", ikey="glapark")
                self.tt("vector", s, self.rT[mt][:, HALO + c * 512:HALO + (c + 1) * 512],
                        gl[b][:, 0:512], ALU.add, okey="s", ikey=[f"rT{mt}", f"gl{b}"])
                for i in range(4):
                    ti = c * 4 + i
                    b2 = ti % 2
                    pb = self.pp.acquire()
                    ps = self.pp.aps[pb]
                    self.trans(ps[0:128, 0:128], s[:, i * 128:(i + 1) * 128],
                               self.ctab["iden_f"], okey=f"PS{pb}", ikey=["s", "CONST"])
                    self.dma("sync", xt[b2],
                             dd["xs"][HALO + ti * 128:HALO + (ti + 1) * 128, :],
                             "d_x2", okey=f"xtf{b2}")
                    self.tt("vector", ot[b2], ps[0:128, 0:128],
                            xt[b2][:, mt * 128:(mt + 1) * 128], ALU.add,
                            okey=f"otf{b2}", ikey=[f"PS{pb}", f"xtf{b2}"])
                    self.pp.release(pb)
                    last_y = self.dma("sync",
                                      dd["y"][ti * 128:(ti + 1) * 128,
                                              mt * 128:(mt + 1) * 128],
                                      ot[b2], "d_y", ikey=f"otf{b2}", okey="y")
        self.p.wait_tok("sync", last_y)
        wk.pop()


def _host_tables(core):
    t0 = core * TC
    half = np.arange(0, D, 2, dtype=np.float32)
    inv_freq = (1.0 / 10000.0 ** (half.astype(np.float64) / D)).astype(np.float32)
    tpos = np.clip(np.arange(t0 - HALO, t0 + TC), 0, T - 1)
    freqs = (tpos[:, None].astype(np.float32) * inv_freq[None, :]).astype(np.float32)
    freqs = np.concatenate([freqs, freqs], -1)
    cos = np.cos(freqs.astype(np.float64))
    sin = np.sin(freqs.astype(np.float64))
    base = (half.astype(np.float64) + 0.4 * D) / (1.4 * D)
    base = np.concatenate([base, base], -1)
    dq = (tpos % W).astype(np.float64)
    Sq = base[None, :] ** (dq[:, None] / (W // 2))
    cosq = (8.0 * cos * Sq).astype(np.float32)
    sinq = (8.0 * sin * Sq).astype(np.float32)
    SkR = base[None, :] ** (-dq[:, None] / (W // 2))
    cosk = (cos * SkR).astype(np.float32)
    sink = (sin * SkR).astype(np.float32)

    def dup(tab):
        return np.concatenate([tab.T, tab.T], 0).astype(np.float32)
    return dup(cosq), dup(sinq), dup(cosk), dup(sink)


def _host_masks(core):
    def build(rt, first):
        aq = rt * 128 + np.arange(128)[:, None]
        kb = rt * 128 + np.arange(384)[None, :]
        keep = (kb >= aq) & (kb <= aq + 256)
        if first:
            keep = keep & (kb >= 256)
        return np.where(keep, 0.0, NEG).astype(np.float32)
    m0, m1 = build(0, False), build(1, False)
    if core == 0:
        return m0, m1, build(0, True), build(1, True)
    return m0, m1, m0, m1


def _host_consts(core):
    iden = np.eye(128, dtype=np.float32)
    R = np.zeros((128, 128), np.float32)
    for hb in range(2):
        o = hb * 64
        for i in range(32):
            R[o + i, o + 32 + i] = -1.0
            R[o + 32 + i, o + i] = 1.0
    RT = np.ascontiguousarray(R.T)
    bdall = np.zeros((128, 128), np.float32)
    hmall = np.zeros((128, 1024), np.float32)
    for f in range(NFT):
        for m in range(16):
            p0 = 64 * (m - 2 * f) if m in (2 * f, 2 * f + 1) else None
            if p0 is not None:
                bdall[p0:p0 + 64, f * 16 + m] = 1.0
        for m in range(128):
            hmall[2 * f + (m >= 64), f * 128 + m] = 1.0
    ones_b = np.ones((128, 1), np.float32)
    ones_row = np.zeros((128, 128), np.float32)
    ones_row[0, :] = 1.0
    triu = np.triu(np.ones((128, 128), np.float32))
    m0, m1, mf0, mf1 = _host_masks(core)
    cosq, sinq, cosk, sink = _host_tables(core)
    vals = {"iden_f": iden, "iden_b": iden, "RT_b": RT, "bdall": bdall,
            "hmall": hmall, "ones_b": ones_b, "ones_row": ones_row,
            "triu_b": triu, "mask0": m0, "mask1": m1, "maskf0": mf0, "maskf1": mf1,
            "cosq": cosq, "sinq": sinq, "cosk": cosk, "sink": sink}
    cols = []
    for nm, ncols, _ in CONST_LAYOUT + TAB_LAYOUT:
        a = vals[nm].astype(np.float32)
        assert a.shape[1] == ncols, (nm, a.shape, ncols)
        if a.shape[0] < 128:
            a = np.concatenate([a, np.zeros((128 - a.shape[0], ncols), np.float32)], 0)
        cols.append(a)
    return np.ascontiguousarray(np.concatenate(cols, 1))


def _host_vecs(inputs, core):
    v = np.zeros((128, VC_COLS), np.float32)

    def put(name, i, col):
        v[:, VC[name] + i] = col
    put("delta", 0, 0)
    delta = np.asarray(inputs["L0_attn_delta"], np.float32).reshape(NFT, 128)
    for f in range(NFT):
        put("delta", f, delta[f])
    nw = np.asarray(inputs["L0_attn_norm_w"], np.float32).reshape(NFT, 128)
    for f in range(NFT):
        put("normw", f, nw[f])
    put("halomul", 0, 0.0 if core == 0 else 1.0)
    qs = np.asarray(inputs["L0_ffn_q_scale"], np.float32)
    ks = np.asarray(inputs["L0_ffn_k_scale"], np.float32)
    for l in range(DEPTH):
        put("qs", l, np.tile(qs[l], 2))
        put("ks", l, np.tile(ks[l], 2))
        lng = np.asarray(inputs["L0_ffn_ln_g"], np.float32)[l].reshape(NFT, 128)
        lnb = np.asarray(inputs["L0_ffn_ln_b"], np.float32)[l].reshape(NFT, 128)
        b2 = np.asarray(inputs["L0_ffn_b2"], np.float32)[l].reshape(NFT, 128)
        for f in range(NFT):
            put("lng", l * NFT + f, lng[f])
            put("lnb", l * NFT + f, lnb[f])
            put("b2", l * NFT + f, b2[f])
        b1 = np.asarray(inputs["L0_ffn_b1"], np.float32)[l]
        b1a, b1g = b1[:DI], b1[DI:]
        for ct in range(22):
            rows = min(128, DI - ct * 128)
            col = np.zeros(128, np.float32)
            col[:rows] = b1a[ct * 128:ct * 128 + rows]
            put("b1a", l * 22 + ct, col)
            col = np.zeros(128, np.float32)
            col[:rows] = b1g[ct * 128:ct * 128 + rows]
            put("b1g", l * 22 + ct, col)
    for r in range(NCORES):
        m = 1.0 if r < core else 0.0
        put("rkM", r, m)
        put("rkI", r, 1.0 - m)
    half = np.arange(0, 64, 2, dtype=np.float64)
    base = (half + 0.4 * 64) / (1.4 * 64)
    base = np.concatenate([base, base])
    put("skb2", 0, np.tile((base ** 2).astype(np.float32), 2))
    put("c_eps", 0, 1e-5)
    put("c_one", 0, 1.0)
    put("c_epsD", 0, 1e-5 * D)
    put("c_hlnD", 0, 0.5 * math.log(D))
    return v


def _host_hidx(core):
    idx = np.zeros((128, NFT), np.uint32)
    src = core - 1 if core > 0 else 0
    for f in range(NFT):
        idx[:, f] = src * E + f * 128 + np.arange(128)
    return idx


_NC_CACHE = {}


def _get_nc():
    key = tuple(sorted(CFG.items()))
    if key not in _NC_CACHE:
        _NC_CACHE[key] = Build(dict(CFG)).build()
    return _NC_CACHE[key]


def kernel(**inputs):
    nc = _get_nc()
    x = np.asarray(inputs["L0_attn_x"], np.float32)[0]
    in_maps = []
    for c in range(NCORES):
        s0 = c * TC
        xs = np.zeros((EXT, E), np.float32)
        lo = max(0, s0 - HALO)
        xs[HALO - (s0 - lo):HALO] = x[lo:s0]
        xs[HALO:] = x[s0:s0 + TC]
        m = {"xs": np.ascontiguousarray(xs),
             "vecs": _host_vecs(inputs, c),
             "consts": _host_consts(c),
             "hidx": _host_hidx(c)}
        for nm in ["attn_Wq", "attn_Wk", "attn_Wv", "attn_Wo", "attn_G1", "attn_G2",
                   "ffn_Wqkv", "ffn_Wo", "ffn_W1", "ffn_W2"]:
            m[nm] = np.ascontiguousarray(np.asarray(inputs["L0_" + nm], np.float32))
        in_maps.append(m)
    res = run_bass_kernel_spmd(nc, in_maps, core_ids=list(range(NCORES)))
    out = np.concatenate([res.results[c]["y"] for c in range(NCORES)], 0)
    return np.ascontiguousarray(out[None])


# version 04
# speedup vs baseline: 1.0424x; 1.0408x over previous
import contextlib
import math
import numpy as np

import concourse.bass as bass
import concourse.mybir as mybir
from concourse.bass import IndirectOffsetOnAxis
from concourse.bass_utils import run_bass_kernel_spmd

F32 = mybir.dt.float32
BF16 = mybir.dt.bfloat16
U32 = mybir.dt.uint32
AF = mybir.ActivationFunctionType
ALU = mybir.AluOpType
AX = mybir.AxisListType

NCORES = 8
T = 16384
TC = 2048
E = 1024
H = 16
D = 64
HALO = 257
EXT = TC + HALO
W = 256
CHUNK = 128
NFT = 8
DEPTH = 6
DI = 2730
NEG = -1e30
RG = [[0, 1, 2, 3, 4, 5, 6, 7]]
ENG_NAMES = ("tensor", "vector", "scalar", "gpsimd", "sync")

CFG = {"gla": True, "nlayers": DEPTH}


class _Tok:
    __slots__ = ("eng", "pos")
    def __init__(self, eng, pos):
        self.eng = eng
        self.pos = pos


class _DmaTok:
    __slots__ = ("sem", "val")
    def __init__(self, sem, val):
        self.sem = sem
        self.val = val


class _SwTok:
    __slots__ = ("prog", "stream", "slot", "sem", "cell")
    def __init__(self, prog, stream, slot, sem, cell):
        self.prog = prog
        self.stream = stream
        self.slot = slot
        self.sem = sem
        self.cell = cell

    def resolve(self):
        if self.cell[0] is None:
            if self.stream["armed"][self.slot] is self.cell:
                self.prog._resolve_sw(self.stream, self.slot, self.sem)
        return self.cell[0]


class _Eng:
    def __init__(self, name):
        self.name = name
        self.ops = []
        self.incs = []
        self.ninc = 0
        self.waited = {}
        self.last_drain = 0

    def milestone_for(self, pos):
        for p, c in self.incs:
            if p >= pos:
                return c
        lp = len(self.ops) - 1
        assert 0 <= pos <= lp, (self.name, pos, lp)
        if self.ops[lp][2]:
            if self.name in ("vector", "scalar", "gpsimd"):
                self.ops.append([lambda q: q.drain(), [], False])
            else:
                self.ops.append([lambda q: q.nop(), [], False])
            lp = len(self.ops) - 1
        self.ninc += 1
        self.incs.append((lp, self.ninc))
        self.ops[lp][1].append(("self", self.ninc))
        return self.ninc


class Prog:
    DMA_POOL = 4

    def __init__(self, nc):
        self.nc = nc
        self.engs = {n: _Eng(n) for n in ENG_NAMES}
        self.writers = {}
        self.readers = {}
        self.streams = {}
        self.sem_handles = {}
        self.fence_toks = []

    @staticmethod
    def _keys(aps, key):
        if key is not None:
            return list(key) if isinstance(key, (list, tuple)) else [key]
        out = []
        for ap in (aps if isinstance(aps, (list, tuple)) else [aps]):
            if ap is None or not hasattr(ap, "tensor"):
                continue
            out.append(ap.tensor.name)
        return out

    def _emit_wait(self, ename, sem, val):
        e = self.engs[ename]
        if e.waited.get(sem, -1) >= val:
            return
        e.waited[sem] = val
        e.ops.append([lambda q, s=sem, v=val: q.wait_ge(self.sem_handles[s], v), [], False])

    def op(self, eng, fn, outs=(), ins=(), okey=None, ikey=None, dma_sem=None,
           no_track=False):
        e = self.engs[eng]
        ik = self._keys(ins, ikey)
        ok = self._keys(outs, okey)
        deps = []
        if not no_track:
            deps.extend(self.fence_toks)
            for k in ik:
                w = self.writers.get(k)
                if w is not None:
                    deps.append(w)
            for k in ok:
                w = self.writers.get(k)
                if w is not None:
                    deps.append(w)
                deps.extend(self.readers.get(k, ()))
        need_drain = False
        for tok in deps:
            if isinstance(tok, _SwTok):
                tok = tok.resolve()
            if isinstance(tok, _DmaTok):
                self._emit_wait(eng, tok.sem, tok.val)
            elif tok.eng is e:
                if tok.pos >= e.last_drain:
                    need_drain = True
            else:
                cnt = tok.eng.milestone_for(tok.pos)
                self._emit_wait(eng, "M_" + tok.eng.name, cnt)
        if need_drain and eng in ("vector", "scalar", "gpsimd"):
            e.last_drain = len(e.ops)
            e.ops.append([lambda q: q.drain(), [], False])
        if dma_sem is not None:
            name, inc = (dma_sem, 16) if not isinstance(dma_sem, tuple) else dma_sem
            sw = eng == "gpsimd"
            name = name + "@" + eng[0]
            stream = self.streams.setdefault(
                name, {"counts": [0] * self.DMA_POOL, "armed": [None] * self.DMA_POOL,
                       "idx": 0, "inc": inc, "sw": sw})
            assert stream["sw"] == sw
            slot = stream["idx"] % self.DMA_POOL
            stream["idx"] += 1
            slot_sem = f"{name}#{slot}"
            if stream["counts"][slot] > 0:
                self._emit_wait(eng, slot_sem, inc * stream["counts"][slot])
        pos = len(e.ops)
        inc_list = []
        e.ops.append([fn, inc_list, dma_sem is not None])
        if dma_sem is not None:
            stream["counts"][slot] += 1
            inc_list.append((slot_sem, inc))
            tok = _DmaTok(slot_sem, inc * stream["counts"][slot])
        else:
            tok = _Tok(e, pos)
        if not no_track:
            for k in ok:
                self.writers[k] = tok
                self.readers[k] = []
            for k in ik:
                self.readers.setdefault(k, []).append(tok)
        return tok

    def _resolve_sw(self, stream, slot, slot_sem):
        g = self.engs["gpsimd"]
        cell = stream["armed"][slot]
        if cell[0] is None:
            g.ops.append([lambda q, s=slot_sem: q.wait_ge(self.sem_handles[s],
                                                          stream["inc"]), [], False])
            pos = len(g.ops)
            g.ops.append([lambda q, s=slot_sem: q.sem_clear(self.sem_handles[s]), [], False])
            cell[0] = _Tok(g, pos)
        stream["armed"][slot] = None
        return cell[0]

    def fence(self):
        toks = []
        for e in self.engs.values():
            if e.ops:
                toks.append(_Tok(e, len(e.ops) - 1))
        for name, stream in self.streams.items():
            for slot in range(self.DMA_POOL):
                if stream["counts"][slot] > 0:
                    toks.append(_DmaTok(f"{name}#{slot}",
                                        stream["inc"] * stream["counts"][slot]))
        self.fence_toks = toks

    def wait_tok(self, eng, tok):
        if isinstance(tok, _SwTok):
            tok = tok.resolve()
        if isinstance(tok, _DmaTok):
            self._emit_wait(eng, tok.sem, tok.val)
        elif tok.eng is self.engs[eng]:
            e = self.engs[eng]
            if tok.pos >= e.last_drain and eng in ("vector", "scalar", "gpsimd"):
                e.last_drain = len(e.ops)
                e.ops.append([lambda q: q.drain(), [], False])
        else:
            cnt = tok.eng.milestone_for(tok.pos)
            self._emit_wait(eng, "M_" + tok.eng.name, cnt)

    def replay(self, block, stack):
        for n in ENG_NAMES:
            self.sem_handles["M_" + n] = stack.enter_context(
                self.nc.semaphore("M_" + n))
        for name, stream in self.streams.items():
            for slot in range(self.DMA_POOL):
                s = f"{name}#{slot}"
                self.sem_handles[s] = stack.enter_context(
                    self.nc.semaphore(s.replace("#", "_").replace("@", "_")))
        prog = self

        def make_runner(ename):
            def run(q):
                for fn, incs, _dma in prog.engs[ename].ops:
                    inst = fn(q)
                    for kind, val in incs:
                        if kind == "self":
                            inst.then_inc(prog.sem_handles["M_" + ename], 1)
                        else:
                            try:
                                inst.then_inc(prog.sem_handles[kind], val)
                            except BaseException as ex:
                                raise RuntimeError(f"then_inc fail {ename} {kind} {val} {inst}") from ex
            return run
        for ename in ENG_NAMES:
            getattr(block, ename)(make_runner(ename))


class Arena:
    def __init__(self, nc, stack, name, nbytes):
        self.t = stack.enter_context(nc.sbuf_tensor(name, [128, nbytes // 4], F32))
        self.nbytes = nbytes
        self.off = 0
        self.marks = []

    def alloc(self, cols, dtype=F32, parts=128):
        esz = {F32: 4, BF16: 2, U32: 4}[dtype]
        nb = (cols * esz + 3) & ~3
        assert self.off + nb <= self.nbytes, (self.off, nb, self.nbytes)
        off = self.off
        self.off += nb
        base = self.t.ap()
        if dtype == BF16:
            return base.bitcast(BF16)[0:parts, off // 2: off // 2 + cols]
        if dtype == U32:
            return base.bitcast(U32)[0:parts, off // 4: off // 4 + cols]
        return base[0:parts, off // 4: off // 4 + cols]

    def push(self):
        self.marks.append(self.off)

    def pop(self):
        self.off = self.marks.pop()


class PsumPool:
    def __init__(self, nc, stack):
        self.ts = [stack.enter_context(nc.psum_tensor(f"psum{i}", [128, 512], F32))
                   for i in range(8)]
        self.aps = [t.ap() for t in self.ts]
        self.free = list(range(8))
        self.rr = 0

    def acquire(self):
        assert self.free, "psum pool exhausted"
        for _ in range(len(self.free)):
            b = self.free.pop(0)
            break
        return b

    def release(self, b):
        self.free.append(b)


CONST_LAYOUT = [
    ("iden_f", 128, F32),
    ("iden_b", 128, BF16),
    ("RT_b", 128, BF16),
    ("bdall", 128, BF16),
    ("hmall", 1024, BF16),
    ("ones_b", 1, BF16),
    ("ones_row", 128, F32),
    ("triu_b", 128, BF16),
    ("mask0", 384, BF16),
    ("mask1", 384, BF16),
    ("maskf0", 384, BF16),
    ("maskf1", 384, BF16),
]
TAB_LAYOUT = [
    ("cosq", EXT, F32),
    ("sinq", EXT, F32),
    ("cosk", EXT, F32),
    ("sink", EXT, F32),
]
CONST_COLS = sum(c for _, c, _ in CONST_LAYOUT) + sum(c for _, c, _ in TAB_LAYOUT)

def _vc_build():
    vc, c = {}, 0
    def take(name, n):
        nonlocal c
        vc[name] = c
        c += n
    take("delta", NFT)
    take("normw", NFT)
    take("halomul", 1)
    take("qs", DEPTH)
    take("ks", DEPTH)
    take("lng", DEPTH * NFT)
    take("lnb", DEPTH * NFT)
    take("b2", DEPTH * NFT)
    take("rkM", NCORES)
    take("rkI", NCORES)
    take("skb2", 1)
    take("c_eps", 1)
    take("c_one", 1)
    take("c_epsD", 1)
    take("c_hlnD", 1)
    take("b1a", DEPTH * 22)
    take("b1g", DEPTH * 22)
    return vc, c

VC, VC_COLS = _vc_build()


class Build:
    def __init__(self, cfg):
        self.cfg = cfg
        self.nc = bass.Bass("TRN2", target_bir_lowering=False, debug=False)
        self.p = Prog(self.nc)
        self.stack = contextlib.ExitStack()
        self.dram = {}
        self._declare_dram()

    def _declare_dram(self):
        nc, dd = self.nc, self.dram
        dd["xs"] = nc.dram_tensor("xs", [EXT, E], F32, kind="ExternalInput").ap()
        for nm, shp in [
            ("attn_Wq", [E, E]), ("attn_Wk", [E, E]), ("attn_Wv", [E, E]),
            ("attn_Wo", [E, E]), ("attn_G1", [E, D]), ("attn_G2", [D, E]),
            ("ffn_Wqkv", [DEPTH, E, 3 * E]), ("ffn_Wo", [DEPTH, E, E]),
            ("ffn_W1", [DEPTH, E, 2 * DI]), ("ffn_W2", [DEPTH, DI, E]),
        ]:
            dd[nm] = nc.dram_tensor(nm, shp, F32, kind="ExternalInput").ap()
        dd["vecs"] = nc.dram_tensor("vecs", [128, VC_COLS], F32,
                                    kind="ExternalInput").ap()
        dd["consts"] = nc.dram_tensor("consts", [128, CONST_COLS], F32,
                                      kind="ExternalInput").ap()
        dd["hidx"] = nc.dram_tensor("hidx", [128, NFT], U32, kind="ExternalInput").ap()
        dd["y"] = nc.dram_tensor("y", [TC, E], F32, kind="ExternalOutput").ap()
        dd["glapark"] = nc.dram_tensor("glapark", [E, TC], F32).ap()
        dd["cin_s"] = nc.dram_tensor("cin_s", [128, 520], F32).ap()
        dd["cout_s"] = nc.dram_tensor("cout_s", [1024, 520], F32).ap()
        dd["cin_h"] = nc.dram_tensor("cin_h", [E, HALO], F32).ap()
        dd["cout_h"] = nc.dram_tensor("cout_h", [NCORES * E, HALO], F32).ap()
        dd["cin_g"] = nc.dram_tensor("cin_g", [E, 1], F32).ap()
        dd["cout_g"] = nc.dram_tensor("cout_g", [NCORES * E, 1], F32).ap()

    def mm(self, out, lhsT, rhs, start=True, stop=True, okey=None, ikey=None, tp=None):
        return self.p.op("tensor",
                         lambda q: q.matmul(out, lhsT, rhs, start=start, stop=stop,
                                            tile_position=tp),
                         outs=[out], ins=[lhsT, rhs], okey=okey, ikey=ikey)

    def trans(self, out, in_, ident, okey=None, ikey=None):
        return self.p.op("tensor", lambda q: q.transpose(out, in_, ident),
                         outs=[out], ins=[in_, ident], okey=okey, ikey=ikey)

    def tt(self, eng, out, a, b, op, okey=None, ikey=None):
        return self.p.op(eng, lambda q: q.tensor_tensor(out, a, b, op),
                         outs=[out], ins=[a, b], okey=okey, ikey=ikey)

    @staticmethod
    def _addkeys(ikey, aps):
        names = [a.tensor.name for a in aps if hasattr(a, "tensor")]
        if not names:
            return ikey
        if ikey is None:
            return None
        base = list(ikey) if isinstance(ikey, (list, tuple)) else [ikey]
        return base + names

    def ts(self, eng, out, in0, s1, s2, op0, op1=ALU.bypass, accum=None,
           okey=None, ikey=None):
        extra = [x for x in (s1, s2) if hasattr(x, "tensor")]
        outs = [out] + ([accum] if accum is not None else [])
        return self.p.op(eng, lambda q: q.tensor_scalar(out, in0, s1, s2, op0, op1,
                                                        accum_out=accum),
                         outs=outs, ins=[in0] + extra, okey=okey,
                         ikey=self._addkeys(ikey, extra))

    def stt(self, eng, out, in0, s, op0, in1, op1, accum=None, okey=None, ikey=None):
        extra = [s] if hasattr(s, "tensor") else []
        outs = [out] + ([accum] if accum is not None else [])
        return self.p.op(eng,
                         lambda q: q.scalar_tensor_tensor(out, in0, s, in1, op0, op1,
                                                          accum_out=accum),
                         outs=outs, ins=[in0, in1] + extra, okey=okey,
                         ikey=self._addkeys(ikey, extra))

    def act(self, out, in_, func, bias=0.0, scale=1.0, accum=None,
            okey=None, ikey=None):
        extra = [x for x in (bias, scale) if hasattr(x, "tensor")]
        outs = [out] + ([accum] if accum is not None else [])
        return self.p.op("scalar",
                         lambda q: q.activation(out, in_, func, bias=bias, scale=scale,
                                                accum_out=accum),
                         outs=outs, ins=[in_] + extra, okey=okey,
                         ikey=self._addkeys(ikey, extra))

    def copy(self, eng, out, in_, okey=None, ikey=None):
        if eng == "scalar":
            return self.p.op(eng, lambda q: q.copy(out, in_), outs=[out], ins=[in_],
                             okey=okey, ikey=ikey)
        return self.p.op(eng, lambda q: q.tensor_copy(out, in_),
                         outs=[out], ins=[in_], okey=okey, ikey=ikey)

    def red(self, out, in_, op, negate=False, okey=None, ikey=None):
        return self.p.op("vector",
                         lambda q: q.tensor_reduce(out, in_, AX.X, op, negate=negate),
                         outs=[out], ins=[in_], okey=okey, ikey=ikey)

    def scan(self, out, d0, op0=ALU.add, okey=None, ikey=None):
        return self.p.op("vector",
                         lambda q: q.tensor_tensor_scan(out, d0, d0, 0.0, op0,
                                                        ALU.bypass),
                         outs=[out], ins=[d0], okey=okey, ikey=ikey)

    def recip(self, out, in_, okey=None, ikey=None):
        return self.p.op("vector", lambda q: q.reciprocal(out, in_),
                         outs=[out], ins=[in_], okey=okey, ikey=ikey)

    def dma(self, eng, out, in_, sem, okey=None, ikey=None, slow=False):
        return self.p.op(eng,
                         lambda q: q.dma_start(out=out, in_=in_,
                                               allow_slow_non_contiguous=slow),
                         outs=[out], ins=[in_], okey=okey, ikey=ikey, dma_sem=sem)

    def gather(self, out, in_, idx, sem, okey=None, ikey=None):
        return self.p.op("gpsimd",
                         lambda q: q.indirect_dma_start(
                             out=out, out_offset=None, in_=in_,
                             in_offset=IndirectOffsetOnAxis(ap=idx, axis=0)),
                         outs=[out], ins=[in_, idx], okey=okey, ikey=ikey, dma_sem=sem)

    def memset(self, eng, ap, val, okey=None):
        return self.p.op(eng, lambda q: q.memset(ap, val), outs=[ap], okey=okey)

    def cc(self, kind, op, cin, cout, okey=None, ikey=None):
        return self.p.op("gpsimd",
                         lambda q: q.collective_compute(kind, op, RG, ins=[cin.opt()],
                                                        outs=[cout.opt()]),
                         outs=[cout], ins=[cin], okey=okey, ikey=ikey,
                         dma_sem=("d_cc", 1))

    def vcol(self, name, i=0):
        return self.vecs[:, VC[name] + i: VC[name] + i + 1]

    def build(self):
        nc, st = self.nc, self.stack
        cfg = self.cfg
        self.rT = [st.enter_context(nc.sbuf_tensor(f"rT{f}", [128, EXT], F32)).ap()
                   for f in range(NFT)]
        self.pp = PsumPool(nc, st)
        ca = Arena(nc, st, "ca", 4 * VC_COLS + 4 * NFT
                   + sum(c * (2 if d == BF16 else 4) + 4 for _, c, d in CONST_LAYOUT) + 64)
        self.ctab = {}
        for nm, cols, dt in CONST_LAYOUT:
            self.ctab[nm] = ca.alloc(cols, dt)
        self.vecs = st.enter_context(nc.sbuf_tensor("vecs_t", [128, VC_COLS], F32)).ap()
        self.hidx = ca.alloc(NFT, U32)
        self.work = Arena(nc, st, "work", 127744)
        self._load_consts()
        self._phase_top()
        self.p.fence()
        if cfg["gla"]:
            self._phase_gla()
        else:
            self._park_zero()
        self.p.fence()
        for l in range(cfg["nlayers"]):
            self._layer_attn(l)
            self._layer_glu(l)
            self.p.fence()
        self._phase_final()
        with st:
            with nc.Block() as block:
                self.p.replay(block, st)
        return nc

    def _load_consts(self):
        dd = self.dram
        c = dd["consts"]
        off = 0
        for nm, cols, dt in CONST_LAYOUT:
            src = c[:, off:off + cols]
            if dt == BF16:
                self.dma("gpsimd", self.ctab[nm], src, "d_const", okey="CONST",
                         slow=(cols < 2))
            else:
                self.dma("sync", self.ctab[nm], src, "d_const", okey="CONST",
                         slow=(cols < 2))
            off += cols
        self.dma("sync", self.vecs, dd["vecs"], "d_const")
        self.dma("gpsimd", self.hidx, dd["hidx"], "d_const", okey="HIDX")

    def _ffn_setup(self):
        if getattr(self, "_ffn_ready", False):
            return
        self._ffn_ready = True
        wk = self.work
        self.fb = {"halo": wk.alloc(HALO)}
        for nm in ("cosq", "sinq", "cosk", "sink"):
            self.fb["t_" + nm] = wk.alloc(512, BF16)
        off = sum(c for _, c, _ in CONST_LAYOUT)
        self.tab_off = {}
        for nm, cols, _ in TAB_LAYOUT:
            self.tab_off[nm] = off
            off += cols
    def _phase_top(self):
        dd = self.dram
        wk = self.work
        wk.push()
        xt = [wk.alloc(E) for _ in range(2)]
        xsq = [wk.alloc(E) for _ in range(2)]
        xn = [wk.alloc(E) for _ in range(2)]
        stat = [wk.alloc(8) for _ in range(2)]
        ntile = (EXT + 127) // 128
        for i in range(ntile):
            rows = min(128, EXT - i * 128)
            b = i % 2
            kx, ks_, kq, kn = f"xt{b}", f"st{b}", f"xq{b}", f"xn{b}"
            x_t = xt[b][0:rows, :]
            self.dma("sync", x_t, dd["xs"][i * 128:i * 128 + rows, :], "d_x", okey=kx)
            mu = stat[b][0:rows, 0:1]
            sq = stat[b][0:rows, 1:2]
            var = stat[b][0:rows, 2:3]
            rstd = stat[b][0:rows, 3:4]
            self.red(mu, x_t, ALU.add, okey=ks_, ikey=kx)
            self.ts("vector", mu, mu, 1.0 / E, None, ALU.mult, okey=ks_, ikey=ks_)
            self.tt("vector", xsq[b][0:rows, :], x_t, x_t, ALU.mult, okey=kq, ikey=kx)
            self.red(sq, xsq[b][0:rows, :], ALU.add, okey=ks_, ikey=kq)
            self.tt("vector", var, mu, mu, ALU.mult, okey=ks_, ikey=ks_)
            self.stt("vector", var, sq, 1.0 / E, ALU.mult, var, ALU.subtract,
                     okey=ks_, ikey=ks_)
            self.act(var, var, AF.Ln, bias=self.vcol("c_eps")[0:rows, :],
                     okey=ks_, ikey=ks_)
            self.act(rstd, var, AF.Exp, scale=-0.5, okey=ks_, ikey=ks_)
            xn_t = xn[b][0:rows, :]
            self.stt("vector", xn_t, x_t, mu, ALU.subtract,
                     rstd.broadcast_to((rows, E)), ALU.mult, okey=kn, ikey=[kx, ks_])
            for f in range(NFT):
                pb = self.pp.acquire()
                ps = self.pp.aps[pb]
                pk = f"PS{pb}"
                self.trans(ps[0:128, 0:rows], xn_t[:, f * 128:(f + 1) * 128],
                           self.ctab["iden_f"][0:rows, 0:rows], okey=pk,
                           ikey=[kn, "CONST"])
                self.copy("vector", self.rT[f][:, i * 128:i * 128 + rows],
                          ps[0:128, 0:rows], okey=f"rT{f}", ikey=pk)
                self.pp.release(pb)
        wk.pop()

    def _park_zero(self):
        wk = self.work
        wk.push()
        z = wk.alloc(512)
        self.memset("vector", z, 0.0, okey="zz")
        for f in range(NFT):
            for c in range(4):
                self.dma("sync",
                         self.dram["glapark"][f * 128:(f + 1) * 128,
                                              c * 512:(c + 1) * 512],
                         z, "d_park", ikey="zz", okey="glapark")
        wk.pop()

    def _phase_gla(self):
        dd = self.dram
        wk = self.work
        wk.push()
        wq = [wk.alloc(E, BF16) for _ in range(NFT)]
        wkk = [wk.alloc(E, BF16) for _ in range(NFT)]
        wv = [wk.alloc(E, BF16) for _ in range(NFT)]
        for f in range(NFT):
            self.dma("gpsimd", wkk[f], dd["attn_Wk"][f * 128:(f + 1) * 128, :],
                     "d_w", okey=f"Wk{f}")
            self.dma("gpsimd", wv[f], dd["attn_Wv"][f * 128:(f + 1) * 128, :],
                     "d_w", okey=f"Wv{f}")
            self.dma("gpsimd", wq[f], dd["attn_Wq"][f * 128:(f + 1) * 128, :],
                     "d_w", okey=f"Wq{f}")
        S = wk.alloc(512)
        Sb = wk.alloc(512, BF16)
        Gtot = wk.alloc(NFT)
        self.memset("vector", S, 0.0, okey=[f"S{f}" for f in range(NFT)])
        self.memset("vector", Gtot, 0.0, okey="Gtot")
        NB = 256
        bufs = {
            "G": [wk.alloc(NB) for _ in range(NFT)],
            "k": [wk.alloc(NB, BF16) for _ in range(NFT)],
            "sc1": wk.alloc(NB), "sc2": wk.alloc(NB), "sc3": wk.alloc(CHUNK),
            "kdec": [wk.alloc(CHUNK, BF16) for _ in range(NFT)],
            "kh": [wk.alloc(CHUNK, BF16) for _ in range(NFT)],
            "qt": [wk.alloc(CHUNK, BF16) for _ in range(NFT)],
            "sq": [wk.alloc(NB, BF16) for _ in range(NFT)],
            "strm": [wk.alloc(NB, BF16) for _ in range(NFT)],
            "v": [wk.alloc(E, BF16) for _ in range(NB // 128)],
            "kdT": wk.alloc(E, BF16),
            "Ab": wk.alloc(2048, BF16),
            "gcol": wk.alloc(32),
            "sgate": [wk.alloc(E, BF16) for _ in range(NB // 128)],
            "of": wk.alloc(E),
            "sc64": wk.alloc(D),
            "oh": wk.alloc(E, BF16),
            "ohT": [wk.alloc(NB, BF16) for _ in range(NFT)],
            "msq": wk.alloc(32),
            "g1": wk.alloc(D),
            "g1b": wk.alloc(D, BF16),
            "g1T": wk.alloc(128, BF16, parts=64),
            "glaT": wk.alloc(256),
        }
        self._gla_pass(bufs, (wq, wkk, wv), S, Sb, Gtot, pass_b=False)
        self.p.fence()
        if self.cfg.get("exch", True):
            self._gla_exchange(S, Gtot, wk)
        else:
            self.memset("vector", S, 0.0, okey=[f"S{f}" for f in range(NFT)])
        self.p.fence()
        if self.cfg.get("passb", True):
            self._gla_pass(bufs, (wq, wkk, wv), S, Sb, Gtot, pass_b=True)
        wk.pop()

    def _gla_logf_chain(self, zps, zkey, f, ncols, bufs):
        G, k = bufs["G"][f], bufs["k"][f]
        sc1, sc2 = bufs["sc1"], bufs["sc2"]
        va = lambda ap: ap[:, 0:ncols]
        kG, kk = f"G{f}", f"k{f}"
        self.ts("vector", va(sc1), zps, self.vcol("delta", f), None, ALU.add,
                okey="sc1", ikey=[zkey, "vecs_t"])
        self.stt("vector", va(sc2), va(sc1), -1.0, ALU.mult, va(sc1), ALU.max,
                 okey="sc2", ikey="sc1")
        self.act(va(sc2), va(sc2), AF.Exp, scale=-1.0, okey="sc2", ikey="sc2")
        self.act(va(sc2), va(sc2), AF.Ln, bias=self.vcol("c_one"), okey="sc2",
                 ikey="sc2")
        self.stt("vector", va(G), va(sc1), 0.0, ALU.min, va(sc2), ALU.subtract,
                 okey=kG, ikey=["sc1", "sc2"])
        self.act(va(sc2), va(G), AF.Exp, okey="sc2", ikey=kG)
        self.ts("vector", va(k), va(sc2), -1.0, 1.0, ALU.mult, ALU.add,
                okey=kk, ikey="sc2")
        for c in range(ncols // CHUNK):
            s = slice(c * CHUNK, (c + 1) * CHUNK)
            self.scan(G[:, s], G[:, s], okey=kG, ikey=kG)

    def _gla_pass(self, bufs, ws, S, Sb, Gtot, pass_b):
        wq, wkk, wv = ws
        dd = self.dram
        tag = "B" if pass_b else "A"
        wk = self.work
        if pass_b:
            wk.push()
            self._wo = [wk.alloc(E, BF16) for _ in range(NFT)]
            self._g1w = [wk.alloc(D, BF16) for _ in range(NFT)]
            self._g2w = wk.alloc(E, BF16, parts=64)
            for f in range(NFT):
                self.dma("gpsimd", self._wo[f], dd["attn_Wo"][f * 128:(f + 1) * 128, :],
                         "d_w", okey=f"Wo{f}")
                self.ts("vector", self._wo[f], self._wo[f], self.vcol("normw", f),
                        None, ALU.mult, okey=f"Wo{f}", ikey=[f"Wo{f}", "vecs_t"])
                self.dma("gpsimd", self._g1w[f],
                         dd["attn_G1"][f * 128:(f + 1) * 128, :], "d_w", okey=f"G1{f}")
            self.dma("gpsimd", self._g2w, dd["attn_G2"][0:64, :], "d_w", okey="G2w")
        NB = 256
        strm = bufs["strm"]
        for blk in range(self.cfg.get("blocks", TC // NB)):
            c0 = HALO + blk * NB
            for f in range(NFT):
                self.copy("vector", strm[f][:, 0:NB], self.rT[f][:, c0:c0 + NB],
                          okey=f"strm{f}", ikey=f"rT{f}")
            qps = []
            if pass_b:
                for f in range(NFT):
                    pb = self.pp.acquire()
                    ps = self.pp.aps[pb]
                    for kt in range(NFT):
                        self.mm(ps[0:128, 0:NB], wq[kt][:, f * 128:(f + 1) * 128],
                                strm[kt][:, 0:NB],
                                start=(kt == 0), stop=(kt == NFT - 1),
                                okey=f"PS{pb}", ikey=[f"Wq{kt}", f"strm{kt}"])
                    qps.append(pb)
                for f in range(NFT):
                    pb = qps[f]
                    self.act(bufs["sq"][f][:, 0:NB], self.pp.aps[pb][0:128, 0:NB],
                             AF.Silu, okey=f"sq{f}", ikey=f"PS{pb}")
                    self.pp.release(pb)
                for mt in range(NB // 128):
                    pb = self.pp.acquire()
                    ps = self.pp.aps[pb]
                    for kt in range(NFT):
                        self.mm(ps[0:128, 0:64], strm[kt][:, mt * 128:(mt + 1) * 128],
                                self._g1w[kt], start=(kt == 0), stop=(kt == NFT - 1),
                                okey=f"PS{pb}", ikey=[f"strm{kt}", f"G1{kt}"])
                    self.copy("vector", bufs["g1"], ps[0:128, 0:64],
                              okey="g1", ikey=f"PS{pb}")
                    self.pp.release(pb)
                    self.copy("vector", bufs["g1b"], bufs["g1"], okey="g1b", ikey="g1")
                    pb = self.pp.acquire()
                    psb = self.pp.aps[pb].bitcast(BF16)
                    self.trans(psb[0:64, 0:128], bufs["g1b"], self.ctab["iden_b"],
                               okey=f"PS{pb}", ikey=["g1b", "CONST"])
                    self.copy("vector", bufs["g1T"], psb[0:64, 0:128],
                              okey="g1T", ikey=f"PS{pb}")
                    self.pp.release(pb)
                    for half in range(2):
                        pb = self.pp.acquire()
                        ps = self.pp.aps[pb]
                        self.mm(ps, bufs["g1T"], self._g2w[0:64, half * 512:half * 512 + 512],
                                okey=f"PS{pb}", ikey=["g1T", "G2w"])
                        self.act(bufs["sgate"][mt][:, half * 512:half * 512 + 512], ps,
                                 AF.Sigmoid, okey=f"sgate{mt}", ikey=f"PS{pb}")
                        self.pp.release(pb)
            if self.cfg.get("glastop", 9) < 2:
                continue
            for f in range(NFT):
                pb = self.pp.acquire()
                ps = self.pp.aps[pb]
                for kt in range(NFT):
                    self.mm(ps[0:128, 0:NB], wkk[kt][:, f * 128:(f + 1) * 128],
                            strm[kt][:, 0:NB],
                            start=(kt == 0), stop=(kt == NFT - 1),
                            okey=f"PS{pb}", ikey=[f"Wk{kt}", f"strm{kt}"])
                self._gla_logf_chain(ps[0:128, 0:NB], f"PS{pb}", f, NB, bufs)
                self.pp.release(pb)
            if self.cfg.get("glastop", 9) < 3:
                continue
            for mt in range(NB // 128):
                for half in range(2):
                    pb = self.pp.acquire()
                    ps = self.pp.aps[pb]
                    for kt in range(NFT):
                        self.mm(ps, strm[kt][:, mt * 128:(mt + 1) * 128],
                                wv[kt][:, half * 512:half * 512 + 512],
                                start=(kt == 0), stop=(kt == NFT - 1),
                                okey=f"PS{pb}", ikey=[f"strm{kt}", f"Wv{kt}"])
                    self.copy("scalar", bufs["v"][mt][:, half * 512:half * 512 + 512],
                              ps, okey=f"v{mt}", ikey=f"PS{pb}")
                    self.pp.release(pb)
            if self.cfg.get("glastop", 9) < 4:
                continue
            for cc in range(NB // CHUNK):
                self._gla_chunk(bufs, S, Sb, Gtot, blk, cc, pass_b)
        if pass_b:
            wk.pop()

    def _gla_chunk(self, bufs, S, Sb, Gtot, blk, cc, pass_b):
        cs = slice(cc * CHUNK, (cc + 1) * CHUNK)
        gcol = bufs["gcol"]
        sc3 = bufs["sc3"]
        for f in range(NFT):
            G = bufs["G"][f]
            GC = gcol[:, f:f + 1]
            self.copy("vector", GC, G[:, cc * CHUNK + 127:cc * CHUNK + 128],
                      okey=f"gc{f}", ikey=f"G{f}")
            self.ts("vector", sc3[:, 0:CHUNK], G[:, cs], -1.0, GC, ALU.mult, ALU.add,
                    okey="sc3", ikey=[f"G{f}", f"gc{f}"])
            self.act(sc3[:, 0:CHUNK], sc3[:, 0:CHUNK], AF.Exp, okey="sc3", ikey="sc3")
            self.tt("vector", bufs["kdec"][f][:, 0:CHUNK], bufs["k"][f][:, cs],
                    sc3[:, 0:CHUNK], ALU.mult, okey=f"kdec{f}", ikey=[f"k{f}", "sc3"])
            if pass_b:
                Rc = gcol[:, 8 + f:9 + f]
                self.copy("vector", Rc, G[:, cc * CHUNK + 63:cc * CHUNK + 64],
                          okey=f"gc{f}", ikey=f"G{f}")
                self.ts("vector", sc3[:, 0:CHUNK], G[:, cs], Rc, None, ALU.subtract,
                        okey="sc3", ikey=[f"G{f}", f"gc{f}"])
                self.act(sc3[:, 0:CHUNK], sc3[:, 0:CHUNK], AF.Exp, okey="sc3", ikey="sc3")
                self.tt("vector", bufs["qt"][f][:, 0:CHUNK], bufs["sq"][f][:, cs],
                        sc3[:, 0:CHUNK], ALU.mult, okey=f"qt{f}", ikey=[f"sq{f}", "sc3"])
                self.ts("vector", sc3[:, 0:CHUNK], G[:, cs], -1.0, Rc, ALU.mult, ALU.add,
                        okey="sc3", ikey=[f"G{f}", f"gc{f}"])
                self.act(sc3[:, 0:CHUNK], sc3[:, 0:CHUNK], AF.Exp, okey="sc3", ikey="sc3")
                self.tt("vector", bufs["kh"][f][:, 0:CHUNK], bufs["k"][f][:, cs],
                        sc3[:, 0:CHUNK], ALU.mult, okey=f"kh{f}", ikey=[f"k{f}", "sc3"])
                eR = gcol[:, 16 + f:17 + f]
                self.act(eR, Rc, AF.Exp, okey=f"gc{f}", ikey=f"gc{f}")
                self.ts("vector", Sb[:, f * 64:(f + 1) * 64], S[:, f * 64:(f + 1) * 64],
                        eR, None, ALU.mult, okey=f"Sb{f}", ikey=[f"S{f}", f"gc{f}"])
        if self.cfg.get("glastop", 9) < 5:
            return
        vtile = bufs["v"][cc]
        opair = []
        if pass_b:
            po = [self.pp.acquire(), self.pp.acquire()]
            for h in range(H):
                f, half = h // 2, h % 2
                pbase = half * 64
                pa = self.pp.acquire()
                aps = self.pp.aps[pa]
                self.mm(aps[0:128, 0:128], bufs["kh"][f][pbase:pbase + 64, 0:CHUNK],
                        bufs["qt"][f][pbase:pbase + 64, 0:CHUNK],
                        okey=f"PS{pa}", ikey=[f"kh{f}", f"qt{f}"], tp=(pbase, 0))
                ab = bufs["Ab"][:, h * 128:(h + 1) * 128]
                self.tt("vector", ab, aps[0:128, 0:128], self.ctab["triu_b"], ALU.mult,
                        okey="Ab", ikey=[f"PS{pa}", "CONST"])
                self.pp.release(pa)
                od = self.pp.aps[po[h // 8]][0:128, (h % 8) * 64:(h % 8) * 64 + 64]
                self.mm(od, ab, vtile[:, h * 64:(h + 1) * 64], start=True, stop=False,
                        okey=f"PS{po[h//8]}", ikey=["Ab", f"v{cc}"])
                self.mm(od, bufs["qt"][f][pbase:pbase + 64, 0:CHUNK],
                        Sb[pbase:pbase + 64, f * 64:(f + 1) * 64],
                        start=False, stop=True,
                        okey=f"PS{po[h//8]}", ikey=[f"qt{f}", f"Sb{f}"], tp=(pbase, 0))
            opair = po
        if self.cfg.get("glastop", 9) < 6:
            return
        pk = self.pp.acquire()
        pkb = self.pp.aps[pk].bitcast(BF16)
        kdT = bufs["kdT"]
        for f in range(NFT):
            slot = (f % 4) * 128
            self.trans(pkb[0:128, slot:slot + 128], bufs["kdec"][f][:, 0:CHUNK],
                       self.ctab["iden_b"], okey=f"PS{pk}",
                       ikey=[f"kdec{f}", "CONST"])
            self.copy("vector", kdT[:, f * 128:(f + 1) * 128], pkb[0:128, slot:slot + 128],
                      okey=f"kdT{f}", ikey=f"PS{pk}")
        self.pp.release(pk)
        if self.cfg.get("glastop", 9) < 7:
            self.pp.release(pk) if False else None
            return
        pS = self.pp.acquire()
        psS = self.pp.aps[pS]
        for h in range(H):
            f, half = h // 2, h % 2
            self.mm(psS[half * 64:half * 64 + 64, f * 64:(f + 1) * 64],
                    kdT[:, h * 64:(h + 1) * 64], vtile[:, h * 64:(h + 1) * 64],
                    tp=(0, half * 64), okey=f"PS{pS}", ikey=[f"kdT{f}", f"v{cc}"])
        if self.cfg.get("glastop", 9) < 8:
            self.pp.release(pS)
            return
        for f in range(NFT):
            PcC = gcol[:, 24 + f:25 + f]
            self.act(PcC, gcol[:, f:f + 1], AF.Exp, okey=f"gc{f}", ikey=f"gc{f}")
            self.stt("vector", S[:, f * 64:(f + 1) * 64], S[:, f * 64:(f + 1) * 64],
                     PcC, ALU.mult, psS[:, f * 64:(f + 1) * 64], ALU.add,
                     okey=f"S{f}", ikey=[f"S{f}", f"gc{f}", f"PS{pS}"])
        self.pp.release(pS)
        self.tt("vector", Gtot, Gtot, gcol[:, 0:NFT], ALU.add,
                okey="Gtot", ikey=["Gtot"] + [f"gc{f}" for f in range(NFT)])
        if pass_b:
            self._gla_out(opair, bufs, blk, cc)
            for b in opair:
                self.pp.release(b)

    def _gla_out(self, po, bufs, blk, cc):
        of, oh = bufs["of"], bufs["oh"]
        sg = bufs["sgate"][cc]
        self.tt("vector", of[:, 0:512], self.pp.aps[po[0]], sg[:, 0:512], ALU.mult,
                okey="of", ikey=[f"PS{po[0]}", f"sgate{cc}"])
        self.tt("vector", of[:, 512:1024], self.pp.aps[po[1]], sg[:, 512:1024],
                ALU.mult, okey="of", ikey=[f"PS{po[1]}", f"sgate{cc}"])
        self.act(oh, of, AF.Square, okey="oh", ikey="of")
        msq = bufs["msq"][:, 0:16]
        rstd = bufs["msq"][:, 16:32]
        self.red(msq, oh.rearrange("p (h d) -> p h d", d=D), ALU.add,
                 okey="msq", ikey="oh")
        self.act(msq, msq, AF.Ln, bias=self.vcol("c_epsD"), okey="msq", ikey="msq")
        self.act(rstd, msq, AF.Exp, scale=-0.5, bias=self.vcol("c_hlnD"),
                 okey="msq", ikey="msq")
        self.tt("vector", oh, of,
                rstd.rearrange("p (h o) -> p h o", o=1).broadcast_to((128, H, D)), ALU.mult,
                okey="oh", ikey=["of", "msq"])
        for f in range(NFT):
            pb = self.pp.acquire()
            psb = self.pp.aps[pb].bitcast(BF16)
            self.trans(psb[0:128, 0:128], oh[:, f * 128:(f + 1) * 128],
                       self.ctab["iden_b"], okey=f"PS{pb}", ikey=["oh", "CONST"])
            self.copy("vector", bufs["ohT"][f][:, cc * 128:(cc + 1) * 128],
                      psb[0:128, 0:128], okey=f"ohT{f}", ikey=f"PS{pb}")
            self.pp.release(pb)
        if cc == 1:
            glaT = bufs["glaT"]
            for mt in range(NFT):
                pb = self.pp.acquire()
                ps = self.pp.aps[pb]
                for kt in range(NFT):
                    self.mm(ps[0:128, 0:256], self._wo[kt][:, mt * 128:(mt + 1) * 128],
                            bufs["ohT"][kt][:, 0:256], start=(kt == 0),
                            stop=(kt == NFT - 1),
                            okey=f"PS{pb}", ikey=[f"Wo{kt}", f"ohT{kt}"])
                self.copy("vector", glaT, ps[0:128, 0:256],
                          okey="glaT", ikey=f"PS{pb}")
                self.pp.release(pb)
                self.dma("sync",
                         self.dram["glapark"][mt * 128:(mt + 1) * 128,
                                              blk * 256:(blk + 1) * 256],
                         glaT, "d_park", ikey="glaT", okey="glapark")

    def _gla_exchange(self, S, Gtot, wk):
        dd = self.dram
        wk.push()
        pack = wk.alloc(520)
        self.copy("vector", pack[:, 0:512], S, okey="pack",
                  ikey=[f"S{f}" for f in range(NFT)])
        self.act(pack[:, 512:520], Gtot, AF.Exp, okey="pack", ikey="Gtot")
        self.dma("sync", dd["cin_s"], pack, "d_cc2", ikey="pack", okey="cin_s")
        self.cc("AllGather", ALU.bypass, dd["cin_s"], dd["cout_s"],
                okey="cout_s", ikey="cin_s")
        self.memset("vector", S, 0.0, okey=[f"S{f}" for f in range(NFT)])
        rbuf = wk.alloc(520)
        pr = wk.alloc(NFT)
        rs = wk.alloc(512)
        for r in range(NCORES):
            self.dma("sync", rbuf, dd["cout_s"][r * 128:(r + 1) * 128, :], "d_cc2",
                     okey="rbuf", ikey="cout_s")
            self.stt("vector", pr, rbuf[:, 512:520], self.vcol("rkM", r), ALU.mult,
                     self.vcol("rkI", r).broadcast_to((128, NFT)), ALU.add,
                     okey="pr", ikey=["rbuf", "vecs_t"])
            self.ts("vector", rs, rbuf[:, 0:512], self.vcol("rkM", r), None, ALU.mult,
                    okey="rs", ikey=["rbuf", "vecs_t"])
            for f in range(NFT):
                self.stt("vector", S[:, f * 64:(f + 1) * 64], S[:, f * 64:(f + 1) * 64],
                         pr[:, f:f + 1], ALU.mult, rs[:, f * 64:(f + 1) * 64], ALU.add,
                         okey=f"S{f}", ikey=[f"S{f}", "pr", "rs"])
        wk.pop()

    def _ffn_setup(self):
        if getattr(self, "_ffn_ready", False):
            return
        self._ffn_ready = True
        wk = self.work
        self.fb = {"halo": wk.alloc(HALO)}
        for nm in ("cosq", "sinq", "cosk", "sink"):
            self.fb["t_" + nm] = wk.alloc(512, BF16)
        off = sum(c for _, c, _ in CONST_LAYOUT)
        self.tab_off = {}
        for nm, cols, _ in TAB_LAYOUT:
            self.tab_off[nm] = off
            off += cols

    def _attn_frame(self):
        wk = self.work
        fb = dict(self.fb)
        fb["wqkv"] = [wk.alloc(3 * E, BF16) for _ in range(NFT)]
        fb["wo"] = [wk.alloc(E, BF16) for _ in range(NFT)]
        fb["strm"] = [wk.alloc(257, BF16) for _ in range(NFT)]
        fb["xb"] = [wk.alloc(W, BF16) for _ in range(2)]
        fb["q2"] = [wk.alloc(W, BF16) for _ in range(2)]
        fb["t2"] = [wk.alloc(W, BF16) for _ in range(2)]
        fb["qt"] = [[wk.alloc(W, BF16) for _ in range(NFT)] for _ in range(2)]
        fb["kring"] = [[wk.alloc(W, BF16) for _ in range(NFT)] for _ in range(2)]
        fb["vring"] = [[wk.alloc(E, BF16) for _ in range(2)] for _ in range(2)]
        fb["prevk"] = [wk.alloc(W, BF16) for _ in range(NFT)]
        fb["sims"] = [wk.alloc(2 * 384) for _ in range(2)]
        fb["P"] = [wk.alloc(2 * 384, BF16) for _ in range(2)]
        fb["PTs"] = [wk.alloc(1024, BF16) for _ in range(2)]
        fb["attnT"] = [wk.alloc(W, BF16) for _ in range(NFT)]
        fb["rqT"] = [wk.alloc(16) for _ in range(16)]
        fb["rn"] = wk.alloc(256)
        fb["rnb"] = wk.alloc(256, BF16)
        fb["cols"] = [wk.alloc(8) for _ in range(2)]
        fb["oldc"] = [wk.alloc(1) for _ in range(NFT)]
        return fb

    GRP = [(0, 6), (6, 12), (12, 18), (18, 22)]

    def _glu_frame(self):
        wk = self.work
        fb = dict(self.fb)
        fb["rsh"] = [wk.alloc(512, BF16) for _ in range(NFT)]
        fb["sq"] = [wk.alloc(512, BF16) for _ in range(2)]
        fb["xnT"] = [[wk.alloc(512, BF16) for _ in range(NFT)] for _ in range(2)]
        fb["w1g"] = [[[wk.alloc(768, BF16) for _ in range(NFT)] for _ in range(2)]
                     for _ in range(2)]
        fb["w2t"] = [wk.alloc(E, BF16) for _ in range(3)]
        fb["ht"] = [wk.alloc(512, BF16) for _ in range(22)]
        fb["gel"] = wk.alloc(512)
        fb["gel2"] = [wk.alloc(512) for _ in range(2)]
        fb["srow"] = wk.alloc(3 * 512)
        fb["oldg"] = [wk.alloc(3) for _ in range(NFT)]
        return fb

    def _halo_in(self, l, kind):
        fb = self.fbl
        dd = self.dram
        cout = dd["cout_h"] if kind == "a" else dd["cout_g"]
        ncol = HALO if kind == "a" else 1
        dst0 = 0 if kind == "a" else 256
        for f in range(NFT):
            g = self.gather(fb["halo"][:, 0:ncol], cout[:, 0:ncol],
                            self.hidx[:, f:f + 1], "d_gath",
                            okey="HALO", ikey=[f"cout_{kind}", "HIDX"])
            self.ts("vector", self.rT[f][:, dst0:dst0 + ncol], fb["halo"][:, 0:ncol],
                    self.vcol("halomul"), None, ALU.mult,
                    okey=f"rT{f}", ikey=["HALO"])

    def _halo_out(self, l, kind):
        dd = self.dram
        if kind == "a":
            cin, cout, c0, ncol = dd["cin_h"], dd["cout_h"], EXT - HALO, HALO
        else:
            cin, cout, c0, ncol = dd["cin_g"], dd["cout_g"], EXT - 1, 1
        for f in range(NFT):
            self.dma("sync", cin[f * 128:(f + 1) * 128, :],
                     self.rT[f][:, c0:c0 + ncol], "d_cc2",
                     okey=f"cin_{kind}", ikey=f"rT{f}", slow=(ncol < 2))
        self.cc("AllGather", ALU.bypass, cin, cout,
                okey=f"cout_{kind}", ikey=f"cin_{kind}")

    def _attn_proj_block(self, l, c0, n, is_halo):
        fb = self.fbl
        ct = self.ctab
        dd = self.dram
        blkw0 = (c0 - HALO) // W
        tabn = ("cosk", "sink") if is_halo else ("cosk", "sink", "cosq", "sinq")
        for nm in tabn:
            self.dma("gpsimd", fb["t_" + nm][:, 0:n],
                     dd["consts"][:, self.tab_off[nm] + c0:self.tab_off[nm] + c0 + n],
                     "d_tab", okey=f"ftab_{nm}")
        use_oldc = c0 > 257
        for f in range(NFT):
            if use_oldc:
                self.copy("vector", fb["strm"][f][:, 0:1], fb["oldc"][f],
                          okey=f"fstrm{f}", ikey=f"foldc{f}")
                self.copy("vector", fb["strm"][f][:, 1:n + 1],
                          self.rT[f][:, c0:c0 + n], okey=f"fstrm{f}", ikey=f"rT{f}")
            else:
                self.copy("vector", fb["strm"][f][:, 0:n + 1],
                          self.rT[f][:, c0 - 1:c0 + n], okey=f"fstrm{f}", ikey=f"rT{f}")
        def rhs(kt, a, b):
            off = 1 if kt < 4 else 0
            return fb["strm"][kt][:, a + off:b + off]
        for part, tabs, scol in (("k", ("cosk", "sink"), "ks"),
                                 ("q", ("cosq", "sinq"), "qs")):
            if part == "q" and is_halo:
                continue
            sspb = self.pp.acquire()
            ssps = self.pp.aps[sspb]
            for f in range(NFT):
                fp = f % 2
                kxb, kq2, kt2 = f"fxb{fp}", f"fq2{fp}", f"ft2{fp}"
                xb, q2, t2 = fb["xb"][fp], fb["q2"][fp], fb["t2"][fp]
                pb = self.pp.acquire()
                ps = self.pp.aps[pb]
                for kt in range(NFT):
                    self.mm(ps[0:128, 0:n],
                            fb["wqkv"][kt][:, (0 if part == "q" else E) + f * 128:
                                           (0 if part == "q" else E) + f * 128 + 128],
                            rhs(kt, 0, n), start=(kt == 0), stop=(kt == NFT - 1),
                            okey=f"PS{pb}", ikey=[f"fw{kt}", f"fstrm{kt}"])
                self.act(q2[:, 0:n], ps[0:128, 0:n], AF.Square,
                         okey=kq2, ikey=f"PS{pb}")
                self.act(xb[:, 0:n], ps[0:128, 0:n], AF.Copy,
                         scale=self.vcol(scol, l), okey=kxb, ikey=f"PS{pb}")
                self.pp.release(pb)
                self.mm(ssps[0:16, 0:n], ct["bdall"][:, f * 16:(f + 1) * 16],
                        q2[:, 0:n], start=(f == 0), stop=(f == NFT - 1),
                        okey=f"PS{sspb}", ikey=["ca", kq2])
                pr = self.pp.acquire()
                prs = self.pp.aps[pr]
                self.mm(prs[0:128, 0:n], ct["RT_b"], xb[:, 0:n],
                        okey=f"PS{pr}", ikey=["ca", kxb])
                self.tt("vector", q2[:, 0:n], xb[:, 0:n],
                        fb["t_" + tabs[0]][:, 0:n], ALU.mult, okey=kq2,
                        ikey=[kxb, f"ftab_{tabs[0]}"])
                self.stt("vector", t2[:, 0:n], prs[0:128, 0:n], 0.0, ALU.add,
                         fb["t_" + tabs[1]][:, 0:n], ALU.mult, okey=kt2,
                         ikey=[f"PS{pr}", f"ftab_{tabs[1]}"])
                self.pp.release(pr)
                if part == "q":
                    wpar = blkw0 % 2
                    self.tt("vector", fb["qt"][wpar][f][:, 0:n], q2[:, 0:n],
                            t2[:, 0:n], ALU.add, okey=f"fqt{wpar}_{f}",
                            ikey=[kq2, kt2])
                else:
                    self.tt("vector", xb[:, 0:n], q2[:, 0:n],
                            t2[:, 0:n], ALU.add, okey=kxb,
                            ikey=[kq2, kt2])
                    self._krope_pending = True
                    for i in range(n // W):
                        w = blkw0 + i
                        self.copy("vector", fb["kring"][w % 2][f][:, 0:W],
                                  xb[:, i * W:(i + 1) * W],
                                  okey=f"fkr{w%2}_{f}", ikey=kxb)
            self.ts("vector", fb["rn"][0:16, 0:n], ssps[0:16, 0:n], 1e-24, None,
                    ALU.max, okey="frn", ikey=f"PS{sspb}")
            self.pp.release(sspb)
            self.act(fb["rn"][0:16, 0:n], fb["rn"][0:16, 0:n], AF.Ln,
                     okey="frn", ikey="frn")
            self.act(fb["rn"][0:16, 0:n], fb["rn"][0:16, 0:n], AF.Exp, scale=-0.5,
                     okey="frn", ikey="frn")
            if part == "q":
                for i in range(n // 128):
                    tt = (c0 - HALO) // 128 + i
                    pb = self.pp.acquire()
                    ps = self.pp.aps[pb]
                    self.trans(ps[0:128, 0:16], fb["rn"][0:16, i * 128:(i + 1) * 128],
                               ct["iden_f"][0:16, 0:16], okey=f"PS{pb}",
                               ikey=["frn", "ca"])
                    self.copy("vector", fb["rqT"][tt], ps[0:128, 0:16],
                              okey=f"frqT{tt}", ikey=f"PS{pb}")
                    self.pp.release(pb)
            else:
                self.copy("vector", fb["rnb"][0:16, 0:n], fb["rn"][0:16, 0:n],
                          okey="frnb", ikey="frn")
                for f in range(NFT):
                    pb = self.pp.acquire()
                    ps = self.pp.aps[pb]
                    self.mm(ps[0:128, 0:n], ct["hmall"][0:16, f * 128:(f + 1) * 128],
                            fb["rnb"][0:16, 0:n], okey=f"PS{pb}", ikey=["ca", "frnb"])
                    for i in range(n // W):
                        w = blkw0 + i
                        ring = fb["kring"][w % 2][f]
                        self.tt("vector", ring, ring,
                                ps[0:128, i * W:(i + 1) * W], ALU.mult,
                                okey=f"fkr{w%2}_{f}", ikey=[f"fkr{w%2}_{f}", f"PS{pb}"])
                    self.pp.release(pb)
        for i in range(n // 128):
            w = blkw0 + (i // 2)
            vt = fb["vring"][w % 2][i % 2]
            for half in range(2):
                pb = self.pp.acquire()
                ps = self.pp.aps[pb]
                for kt in range(NFT):
                    self.mm(ps, rhs(kt, i * 128, (i + 1) * 128),
                            fb["wqkv"][kt][:, 2 * E + half * 512:2 * E + half * 512 + 512],
                            start=(kt == 0), stop=(kt == NFT - 1),
                            okey=f"PS{pb}", ikey=[f"fstrm{kt}", f"fw{kt}"])
                self.copy("scalar", vt[:, half * 512:half * 512 + 512], ps,
                          okey=f"fv{w%2}_{i%2}", ikey=f"PS{pb}")
                self.pp.release(pb)

    def _attn_window(self, l, w):
        fb = self.fbl
        ct = self.ctab
        wp = w % 2
        for f in range(NFT):
            kp = fb["kring"][(w - 1) % 2][f]
            self.ts("vector", fb["prevk"][f], kp, self.vcol("skb2"), None,
                    ALU.mult, okey=f"fpk{f}", ikey=[f"fkr{(w-1)%2}_{f}"])
        for h in range(H):
            f, half = h // 2, h % 2
            hp = h % 2
            pbq = half * 64
            kq = f"fqt{wp}_{f}"
            simbanks = []
            for rt in range(2):
                pb = self.pp.acquire()
                ps = self.pp.aps[pb]
                qlhs = fb["qt"][wp][f][pbq:pbq + 64, rt * 128:rt * 128 + 128]
                pk = fb["prevk"][f][pbq:pbq + 64, :]
                kc = fb["kring"][w % 2][f][pbq:pbq + 64, :]
                if rt == 0:
                    self.mm(ps[0:128, 0:256], qlhs, pk, okey=f"PS{pb}",
                            ikey=[kq, f"fpk{f}"], tp=(pbq, 0))
                    self.mm(ps[0:128, 256:384], qlhs, kc[:, 0:128], okey=f"PS{pb}",
                            ikey=[kq, f"fkr{w%2}_{f}"], tp=(pbq, 0))
                else:
                    self.mm(ps[0:128, 0:128], qlhs, pk[:, 128:256], okey=f"PS{pb}",
                            ikey=[kq, f"fpk{f}"], tp=(pbq, 0))
                    self.mm(ps[0:128, 128:384], qlhs, kc, okey=f"PS{pb}",
                            ikey=[kq, f"fkr{w%2}_{f}"], tp=(pbq, 0))
                simbanks.append(pb)
            ssc = fb["cols"][hp]
            kPT = f"fPTs{hp}"
            for rt in range(2):
                pb = simbanks[rt]
                ps = self.pp.aps[pb]
                ksims = f"fsims{hp}_{rt}"
                kmx = f"fmx{hp}{rt}"
                kbi = f"fbi{hp}{rt}"
                ksm = f"fsm{hp}{rt}"
                kP = f"fP{hp}_{rt}"
                mname = f"maskf{rt}" if w == 0 else f"mask{rt}"
                self.tt("vector", fb["sims"][hp][:, rt * 384:rt * 384 + 384],
                        ps[0:128, 0:384], ct[mname], ALU.add,
                        okey=ksims, ikey=[f"PS{pb}", "ca"])
                self.pp.release(pb)
                self.red(ssc[:, rt:rt + 1], fb["sims"][hp][:, rt * 384:rt * 384 + 384],
                         ALU.max, negate=True, okey=kmx, ikey=ksims)
                rq = fb["rqT"][w * 2 + rt][:, h:h + 1]
                self.tt("vector", ssc[:, 2 + rt:3 + rt], ssc[:, rt:rt + 1], rq,
                        ALU.mult, okey=kbi, ikey=[kmx, f"frqT{w*2+rt}"])
                self.act(fb["P"][hp][:, rt * 384:rt * 384 + 384],
                         fb["sims"][hp][:, rt * 384:rt * 384 + 384], AF.Exp,
                         bias=ssc[:, 2 + rt:3 + rt], scale=rq,
                         accum=ssc[:, 4 + rt:5 + rt],
                         okey=[kP, ksm], ikey=[ksims, kbi, f"frqT{w*2+rt}"])
            self.recip(ssc[:, 6:8], ssc[:, 4:6],
                       okey=f"frc{hp}", ikey=[f"fsm{hp}0", f"fsm{hp}1"])
            for rt in range(2):
                kP = f"fP{hp}_{rt}"
                self.ts("vector", fb["P"][hp][:, rt * 384:rt * 384 + 384],
                        fb["P"][hp][:, rt * 384:rt * 384 + 384],
                        ssc[:, 6 + rt:7 + rt],
                        None, ALU.mult, okey=kP, ikey=[kP, f"frc{hp}"])
            pt = self.pp.acquire()
            ptb = self.pp.aps[pt].bitcast(BF16)
            for rt in range(2):
                kP = f"fP{hp}_{rt}"
                for kt in range(rt, rt + 3):
                    self.trans(ptb[0:128, kt * 256 + rt * 128:kt * 256 + rt * 128 + 128],
                               fb["P"][hp][:, rt * 384 + (kt - rt) * 128:
                                           rt * 384 + (kt - rt) * 128 + 128],
                               ct["iden_b"], okey=f"PS{pt}", ikey=[kP, "ca"])
            self.copy("scalar", fb["PTs"][hp][:, 0:128], ptb[0:128, 0:128],
                      okey=kPT, ikey=f"PS{pt}")
            self.copy("scalar", fb["PTs"][hp][:, 256:768], ptb[0:128, 256:768],
                      okey=kPT, ikey=f"PS{pt}")
            self.copy("scalar", fb["PTs"][hp][:, 896:1024], ptb[0:128, 896:1024],
                      okey=kPT, ikey=f"PS{pt}")
            self.pp.release(pt)
            if half == 0:
                self._opair = self.pp.acquire()
            ob = self._opair
            ops = self.pp.aps[ob]
            for qt_i in range(2):
                valid = [kt for kt in range(4) if qt_i <= kt <= qt_i + 2]
                for j, kt in enumerate(valid):
                    if kt < 2:
                        vt = fb["vring"][(w - 1) % 2][kt]
                        vk = f"fv{(w-1)%2}_{kt}"
                    else:
                        vt = fb["vring"][w % 2][kt - 2]
                        vk = f"fv{w%2}_{kt-2}"
                    self.mm(ops[pbq:pbq + 64, qt_i * 128:qt_i * 128 + 128],
                            vt[:, h * 64:(h + 1) * 64],
                            fb["PTs"][hp][:, kt * 256 + qt_i * 128:
                                          kt * 256 + qt_i * 128 + 128],
                            start=(j == 0), stop=(j == len(valid) - 1),
                            okey=f"PS{ob}", ikey=[vk, kPT], tp=(0, pbq))
            if half == 1:
                self.copy("vector", fb["attnT"][f][:, 0:W], ops[0:128, 0:W],
                          okey=f"fattnT{f}", ikey=f"PS{ob}")
                self.pp.release(ob)

    def _attn_wo(self, l, wlist):
        fb = self.fbl
        gcols = len(wlist) * W
        assert len(wlist) == 1
        c0 = HALO + wlist[0] * W
        for f in range(NFT):
            self.copy("vector", fb["oldc"][f], self.rT[f][:, c0 + W - 1:c0 + W],
                      okey=f"foldc{f}", ikey=f"rT{f}")
        for mt in range(NFT):
            pb = self.pp.acquire()
            ps = self.pp.aps[pb]
            for kt in range(NFT):
                self.mm(ps[0:128, 0:gcols], fb["wo"][kt][:, mt * 128:(mt + 1) * 128],
                        fb["attnT"][kt][:, 0:gcols], start=(kt == 0),
                        stop=(kt == NFT - 1), okey=f"PS{pb}",
                        ikey=[f"fwo{kt}", f"fattnT{kt}"])
            self.tt("vector", self.rT[mt][:, c0:c0 + gcols], ps[0:128, 0:gcols],
                    self.rT[mt][:, c0:c0 + gcols], ALU.add,
                    okey=f"rT{mt}", ikey=[f"PS{pb}", f"rT{mt}"])
            self.pp.release(pb)

    def _layer_attn(self, l):
        dd = self.dram
        self._ffn_setup()
        self.work.push()
        fb = self.fbl = self._attn_frame()
        for f in range(NFT):
            self.dma("gpsimd", fb["wqkv"][f],
                     dd["ffn_Wqkv"][l, f * 128:(f + 1) * 128, :], "d_w",
                     okey=f"fw{f}")
            self.dma("gpsimd", fb["wo"][f],
                     dd["ffn_Wo"][l, f * 128:(f + 1) * 128, :], "d_w",
                     okey=f"fwo{f}")
        if l > 0:
            self._halo_in(l, "a")
        self._attn_proj_block(l, 1, 256, True)
        for w in range(8):
            self._attn_proj_block(l, 257 + w * W, 256, False)
            self._attn_window(l, w)
            self._attn_wo(l, [w])
        self._halo_out(l, "g")
        self.work.pop()
        self.p.fence()

    def _layer_glu(self, l):
        dd = self.dram
        ct = self.ctab
        self.work.push()
        fb = self.fbl = self._glu_frame()
        NC = 512
        for f in range(NFT):
            for j in range(3):
                self.copy("vector", fb["oldg"][f][:, j:j + 1],
                          self.rT[f][:, HALO + (j + 1) * NC - 1:HALO + (j + 1) * NC],
                          okey=f"goldg{f}", ikey=f"rT{f}")
        for ci, ch in enumerate([1, 2, 3, 0]):
            c0 = HALO + ch * NC
            if ch == 0:
                self._halo_in(l, "g")
            for f in range(NFT):
                if f < 4:
                    self.copy("vector", fb["rsh"][f], self.rT[f][:, c0:c0 + NC],
                              okey=f"grsh{f}", ikey=f"rT{f}")
                else:
                    if ch == 0:
                        self.copy("vector", fb["rsh"][f][:, 0:1],
                                  self.rT[f][:, c0 - 1:c0], okey=f"grsh{f}",
                                  ikey=f"rT{f}")
                    else:
                        self.copy("vector", fb["rsh"][f][:, 0:1],
                                  fb["oldg"][f][:, ch - 1:ch], okey=f"grsh{f}",
                                  ikey=f"goldg{f}")
                    self.copy("vector", fb["rsh"][f][:, 1:NC],
                              self.rT[f][:, c0:c0 + NC - 1], okey=f"grsh{f}",
                              ikey=f"rT{f}")
            p1 = self.pp.acquire()
            p2 = self.pp.acquire()
            s1 = self.pp.aps[p1]
            s2 = self.pp.aps[p2]
            for f in range(NFT):
                self.act(fb["sq"][f % 2], fb["rsh"][f], AF.Square,
                         okey=f"gsq{f%2}", ikey=f"grsh{f}")
                self.mm(s1[0:1, 0:NC], ct["ones_b"], fb["rsh"][f],
                        start=(f == 0), stop=(f == NFT - 1), okey=f"PS{p1}",
                        ikey=["ca", f"grsh{f}"])
                self.mm(s2[0:1, 0:NC], ct["ones_b"], fb["sq"][f % 2],
                        start=(f == 0), stop=(f == NFT - 1), okey=f"PS{p2}",
                        ikey=["ca", f"gsq{f%2}"])
            mu = fb["srow"][0:1, 0:NC]
            ms = fb["srow"][0:1, 512:512 + NC]
            rst = fb["srow"][0:1, 1024:1024 + NC]
            self.ts("vector", mu, s1[0:1, 0:NC], 1.0 / E, None, ALU.mult,
                    okey="gmu", ikey=f"PS{p1}")
            self.ts("vector", ms, s2[0:1, 0:NC], 1.0 / E, None, ALU.mult,
                    okey="gms", ikey=f"PS{p2}")
            self.pp.release(p1)
            self.pp.release(p2)
            self.tt("vector", rst, mu, mu, ALU.mult, okey="grst", ikey="gmu")
            self.tt("vector", rst, ms, rst, ALU.subtract, okey="grst",
                    ikey=["gms", "grst"])
            self.act(rst, rst, AF.Ln, bias=self.vcol("c_eps")[0:1, :],
                     okey="grst", ikey="grst")
            self.act(rst, rst, AF.Exp, scale=-0.5, okey="grst", ikey="grst")
            pmu = self.pp.acquire()
            prs = self.pp.acquire()
            self.mm(self.pp.aps[pmu][0:128, 0:NC], ct["ones_row"][0:1, :], mu,
                    okey=f"PS{pmu}", ikey=["ca", "gmu"])
            self.mm(self.pp.aps[prs][0:128, 0:NC], ct["ones_row"][0:1, :], rst,
                    okey=f"PS{prs}", ikey=["ca", "grst"])
            for f in range(NFT):
                self.tt("vector", fb["gel"][:, 0:NC], fb["rsh"][f],
                        self.pp.aps[pmu][0:128, 0:NC], ALU.subtract,
                        okey="ggel", ikey=[f"grsh{f}", f"PS{pmu}"])
                self.stt("vector", fb["gel"][:, 0:NC], fb["gel"][:, 0:NC],
                         self.vcol("lng", l * NFT + f), ALU.mult,
                         self.pp.aps[prs][0:128, 0:NC], ALU.mult,
                         okey="ggel", ikey=["ggel", f"PS{prs}"])
                self.ts("vector", fb["xnT"][ci % 2][f], fb["gel"][:, 0:NC],
                        self.vcol("lnb", l * NFT + f), None, ALU.add,
                        okey=f"gxnT{ci%2}_{f}", ikey="ggel")
            self.pp.release(pmu)
            self.pp.release(prs)
            for gi, (t0, t1) in enumerate(self.GRP):
                par = gi % 2
                gc0 = t0 * 128
                gcols = min(t1 * 128, DI) - gc0
                for half in range(2):
                    for f in range(NFT):
                        self.dma("gpsimd", fb["w1g"][par][half][f][:, 0:gcols],
                                 dd["ffn_W1"][l, f * 128:(f + 1) * 128,
                                              half * DI + gc0:half * DI + gc0 + gcols],
                                 "d_w1", okey=f"gw1_{par}_{half}_{f}")
                for ct_i in range(t0, t1):
                    rows = min(128, DI - ct_i * 128)
                    lc = ct_i * 128 - gc0
                    banks = []
                    for half in range(2):
                        pb = self.pp.acquire()
                        ps = self.pp.aps[pb]
                        for kt in range(NFT):
                            self.mm(ps[0:rows, 0:NC],
                                    fb["w1g"][par][half][kt][:, lc:lc + rows],
                                    fb["xnT"][ci % 2][kt], start=(kt == 0),
                                    stop=(kt == NFT - 1), okey=f"PS{pb}",
                                    ikey=[f"gw1_{par}_{half}_{kt}"]
                                    + [f"gxnT{ci%2}_{k}" for k in range(NFT)])
                        banks.append(pb)
                    pa, pb = banks
                    gpar = ct_i % 2
                    gel = fb["gel2"][gpar]
                    self.act(gel[0:rows, 0:NC], self.pp.aps[pb][0:rows, 0:NC],
                             AF.Gelu, bias=self.vcol("b1g", l * 22 + ct_i)[0:rows, :],
                             okey=f"ggel{gpar}", ikey=f"PS{pb}")
                    self.pp.release(pb)
                    self.stt("vector", fb["ht"][ct_i][0:rows, 0:NC],
                             self.pp.aps[pa][0:rows, 0:NC],
                             self.vcol("b1a", l * 22 + ct_i)[0:rows, :],
                             ALU.add, gel[0:rows, 0:NC], ALU.mult,
                             okey=f"ght{ct_i}", ikey=[f"PS{pa}", f"ggel{gpar}"])
                    self.pp.release(pa)
            mtb = [self.pp.acquire() for _ in range(8)]
            for kt in range(22):
                rows = min(128, DI - kt * 128)
                wb = fb["w2t"][kt % 3]
                self.dma("gpsimd", wb[0:rows, :],
                         dd["ffn_W2"][l, kt * 128:kt * 128 + rows, :], "d_w2",
                         okey=f"gw2_{kt%3}")
                for mt in range(NFT):
                    self.mm(self.pp.aps[mtb[mt]][0:128, 0:NC],
                            fb["w2t"][kt % 3][0:rows, mt * 128:(mt + 1) * 128],
                            fb["ht"][kt][0:rows, 0:NC],
                            start=(kt == 0), stop=(kt == 21),
                            okey=f"PS{mtb[mt]}", ikey=[f"gw2_{kt%3}", f"ght{kt}"])
            for mt in range(NFT):
                self.stt("vector", self.rT[mt][:, c0:c0 + NC],
                         self.pp.aps[mtb[mt]][0:128, 0:NC],
                         self.vcol("b2", l * NFT + mt), ALU.add,
                         self.rT[mt][:, c0:c0 + NC], ALU.add,
                         okey=f"rT{mt}", ikey=[f"PS{mtb[mt]}", f"rT{mt}"])
                self.pp.release(mtb[mt])
            if ch == 3 and l < self.cfg["nlayers"] - 1:
                self._halo_out(l, "a")
        self.work.pop()

    def _phase_final(self):
        dd = self.dram
        wk = self.work
        wk.push()
        gl = [wk.alloc(512) for _ in range(2)]
        xt = [wk.alloc(E) for _ in range(2)]
        ot = [wk.alloc(128) for _ in range(2)]
        s = wk.alloc(512)
        last_y = None
        for mt in range(NFT):
            for c in range(4):
                b = c % 2
                self.dma("sync", gl[b],
                         dd["glapark"][mt * 128:(mt + 1) * 128, c * 512:(c + 1) * 512],
                         "d_park2", okey=f"gl{b}", ikey="glapark")
                self.tt("vector", s, self.rT[mt][:, HALO + c * 512:HALO + (c + 1) * 512],
                        gl[b][:, 0:512], ALU.add, okey="s", ikey=[f"rT{mt}", f"gl{b}"])
                for i in range(4):
                    ti = c * 4 + i
                    b2 = ti % 2
                    pb = self.pp.acquire()
                    ps = self.pp.aps[pb]
                    self.trans(ps[0:128, 0:128], s[:, i * 128:(i + 1) * 128],
                               self.ctab["iden_f"], okey=f"PS{pb}", ikey=["s", "CONST"])
                    self.dma("sync", xt[b2],
                             dd["xs"][HALO + ti * 128:HALO + (ti + 1) * 128, :],
                             "d_x2", okey=f"xtf{b2}")
                    self.tt("vector", ot[b2], ps[0:128, 0:128],
                            xt[b2][:, mt * 128:(mt + 1) * 128], ALU.add,
                            okey=f"otf{b2}", ikey=[f"PS{pb}", f"xtf{b2}"])
                    self.pp.release(pb)
                    last_y = self.dma("sync",
                                      dd["y"][ti * 128:(ti + 1) * 128,
                                              mt * 128:(mt + 1) * 128],
                                      ot[b2], "d_y", ikey=f"otf{b2}", okey="y")
        self.p.wait_tok("sync", last_y)
        wk.pop()


def _host_tables(core):
    t0 = core * TC
    half = np.arange(0, D, 2, dtype=np.float32)
    inv_freq = (1.0 / 10000.0 ** (half.astype(np.float64) / D)).astype(np.float32)
    tpos = np.clip(np.arange(t0 - HALO, t0 + TC), 0, T - 1)
    freqs = (tpos[:, None].astype(np.float32) * inv_freq[None, :]).astype(np.float32)
    freqs = np.concatenate([freqs, freqs], -1)
    cos = np.cos(freqs.astype(np.float64))
    sin = np.sin(freqs.astype(np.float64))
    base = (half.astype(np.float64) + 0.4 * D) / (1.4 * D)
    base = np.concatenate([base, base], -1)
    dq = (tpos % W).astype(np.float64)
    Sq = base[None, :] ** (dq[:, None] / (W // 2))
    cosq = (8.0 * cos * Sq).astype(np.float32)
    sinq = (8.0 * sin * Sq).astype(np.float32)
    SkR = base[None, :] ** (-dq[:, None] / (W // 2))
    cosk = (cos * SkR).astype(np.float32)
    sink = (sin * SkR).astype(np.float32)

    def dup(tab):
        return np.concatenate([tab.T, tab.T], 0).astype(np.float32)
    return dup(cosq), dup(sinq), dup(cosk), dup(sink)


def _host_masks(core):
    def build(rt, first):
        aq = rt * 128 + np.arange(128)[:, None]
        kb = rt * 128 + np.arange(384)[None, :]
        keep = (kb >= aq) & (kb <= aq + 256)
        if first:
            keep = keep & (kb >= 256)
        return np.where(keep, 0.0, NEG).astype(np.float32)
    m0, m1 = build(0, False), build(1, False)
    if core == 0:
        return m0, m1, build(0, True), build(1, True)
    return m0, m1, m0, m1


def _host_consts(core):
    iden = np.eye(128, dtype=np.float32)
    R = np.zeros((128, 128), np.float32)
    for hb in range(2):
        o = hb * 64
        for i in range(32):
            R[o + i, o + 32 + i] = -1.0
            R[o + 32 + i, o + i] = 1.0
    RT = np.ascontiguousarray(R.T)
    bdall = np.zeros((128, 128), np.float32)
    hmall = np.zeros((128, 1024), np.float32)
    for f in range(NFT):
        for m in range(16):
            p0 = 64 * (m - 2 * f) if m in (2 * f, 2 * f + 1) else None
            if p0 is not None:
                bdall[p0:p0 + 64, f * 16 + m] = 1.0
        for m in range(128):
            hmall[2 * f + (m >= 64), f * 128 + m] = 1.0
    ones_b = np.ones((128, 1), np.float32)
    ones_row = np.zeros((128, 128), np.float32)
    ones_row[0, :] = 1.0
    triu = np.triu(np.ones((128, 128), np.float32))
    m0, m1, mf0, mf1 = _host_masks(core)
    cosq, sinq, cosk, sink = _host_tables(core)
    vals = {"iden_f": iden, "iden_b": iden, "RT_b": RT, "bdall": bdall,
            "hmall": hmall, "ones_b": ones_b, "ones_row": ones_row,
            "triu_b": triu, "mask0": m0, "mask1": m1, "maskf0": mf0, "maskf1": mf1,
            "cosq": cosq, "sinq": sinq, "cosk": cosk, "sink": sink}
    cols = []
    for nm, ncols, _ in CONST_LAYOUT + TAB_LAYOUT:
        a = vals[nm].astype(np.float32)
        assert a.shape[1] == ncols, (nm, a.shape, ncols)
        if a.shape[0] < 128:
            a = np.concatenate([a, np.zeros((128 - a.shape[0], ncols), np.float32)], 0)
        cols.append(a)
    return np.ascontiguousarray(np.concatenate(cols, 1))


def _host_vecs(inputs, core):
    v = np.zeros((128, VC_COLS), np.float32)

    def put(name, i, col):
        v[:, VC[name] + i] = col
    put("delta", 0, 0)
    delta = np.asarray(inputs["L0_attn_delta"], np.float32).reshape(NFT, 128)
    for f in range(NFT):
        put("delta", f, delta[f])
    nw = np.asarray(inputs["L0_attn_norm_w"], np.float32).reshape(NFT, 128)
    for f in range(NFT):
        put("normw", f, nw[f])
    put("halomul", 0, 0.0 if core == 0 else 1.0)
    qs = np.asarray(inputs["L0_ffn_q_scale"], np.float32)
    ks = np.asarray(inputs["L0_ffn_k_scale"], np.float32)
    for l in range(DEPTH):
        put("qs", l, np.tile(qs[l], 2))
        put("ks", l, np.tile(ks[l], 2))
        lng = np.asarray(inputs["L0_ffn_ln_g"], np.float32)[l].reshape(NFT, 128)
        lnb = np.asarray(inputs["L0_ffn_ln_b"], np.float32)[l].reshape(NFT, 128)
        b2 = np.asarray(inputs["L0_ffn_b2"], np.float32)[l].reshape(NFT, 128)
        for f in range(NFT):
            put("lng", l * NFT + f, lng[f])
            put("lnb", l * NFT + f, lnb[f])
            put("b2", l * NFT + f, b2[f])
        b1 = np.asarray(inputs["L0_ffn_b1"], np.float32)[l]
        b1a, b1g = b1[:DI], b1[DI:]
        for ct in range(22):
            rows = min(128, DI - ct * 128)
            col = np.zeros(128, np.float32)
            col[:rows] = b1a[ct * 128:ct * 128 + rows]
            put("b1a", l * 22 + ct, col)
            col = np.zeros(128, np.float32)
            col[:rows] = b1g[ct * 128:ct * 128 + rows]
            put("b1g", l * 22 + ct, col)
    for r in range(NCORES):
        m = 1.0 if r < core else 0.0
        put("rkM", r, m)
        put("rkI", r, 1.0 - m)
    half = np.arange(0, 64, 2, dtype=np.float64)
    base = (half + 0.4 * 64) / (1.4 * 64)
    base = np.concatenate([base, base])
    put("skb2", 0, np.tile((base ** 2).astype(np.float32), 2))
    put("c_eps", 0, 1e-5)
    put("c_one", 0, 1.0)
    put("c_epsD", 0, 1e-5 * D)
    put("c_hlnD", 0, 0.5 * math.log(D))
    return v


def _host_hidx(core):
    idx = np.zeros((128, NFT), np.uint32)
    src = core - 1 if core > 0 else 0
    for f in range(NFT):
        idx[:, f] = src * E + f * 128 + np.arange(128)
    return idx


_NC_CACHE = {}


def _get_nc():
    key = tuple(sorted(CFG.items()))
    if key not in _NC_CACHE:
        _NC_CACHE[key] = Build(dict(CFG)).build()
    return _NC_CACHE[key]


def kernel(**inputs):
    nc = _get_nc()
    x = np.asarray(inputs["L0_attn_x"], np.float32)[0]
    in_maps = []
    for c in range(NCORES):
        s0 = c * TC
        xs = np.zeros((EXT, E), np.float32)
        lo = max(0, s0 - HALO)
        xs[HALO - (s0 - lo):HALO] = x[lo:s0]
        xs[HALO:] = x[s0:s0 + TC]
        m = {"xs": np.ascontiguousarray(xs),
             "vecs": _host_vecs(inputs, c),
             "consts": _host_consts(c),
             "hidx": _host_hidx(c)}
        for nm in ["attn_Wq", "attn_Wk", "attn_Wv", "attn_Wo", "attn_G1", "attn_G2",
                   "ffn_Wqkv", "ffn_Wo", "ffn_W1", "ffn_W2"]:
            m[nm] = np.ascontiguousarray(np.asarray(inputs["L0_" + nm], np.float32))
        in_maps.append(m)
    res = run_bass_kernel_spmd(nc, in_maps, core_ids=list(range(NCORES)))
    out = np.concatenate([res.results[c]["y"] for c in range(NCORES)], 0)
    return np.ascontiguousarray(out[None])


# version 06 (final)
# speedup vs baseline: 1.1606x; 1.1134x over previous
import contextlib
import math
import numpy as np

import concourse.bass as bass
import concourse.mybir as mybir
from concourse.bass import IndirectOffsetOnAxis
from concourse.bass_utils import run_bass_kernel_spmd

F32 = mybir.dt.float32
BF16 = mybir.dt.bfloat16
U32 = mybir.dt.uint32
AF = mybir.ActivationFunctionType
ALU = mybir.AluOpType
AX = mybir.AxisListType

NCORES = 8
T = 16384
TC = 2048
E = 1024
H = 16
D = 64
HALO = 257
EXT = TC + HALO
W = 256
CHUNK = 128
NFT = 8
DEPTH = 6
DI = 2730
NEG = -1e30
RG = [[0, 1, 2, 3, 4, 5, 6, 7]]
ENG_NAMES = ("tensor", "vector", "scalar", "gpsimd", "sync")

CFG = {"gla": True, "nlayers": DEPTH}


class _Tok:
    __slots__ = ("eng", "pos")
    def __init__(self, eng, pos):
        self.eng = eng
        self.pos = pos


class _DmaTok:
    __slots__ = ("sem", "val")
    def __init__(self, sem, val):
        self.sem = sem
        self.val = val


class _SwTok:
    __slots__ = ("prog", "stream", "slot", "sem", "cell")
    def __init__(self, prog, stream, slot, sem, cell):
        self.prog = prog
        self.stream = stream
        self.slot = slot
        self.sem = sem
        self.cell = cell

    def resolve(self):
        if self.cell[0] is None:
            if self.stream["armed"][self.slot] is self.cell:
                self.prog._resolve_sw(self.stream, self.slot, self.sem)
        return self.cell[0]


class _Eng:
    def __init__(self, name):
        self.name = name
        self.ops = []
        self.incs = []
        self.ninc = 0
        self.waited = {}
        self.last_drain = 0

    def milestone_for(self, pos):
        for p, c in self.incs:
            if p >= pos:
                return c
        lp = len(self.ops) - 1
        assert 0 <= pos <= lp, (self.name, pos, lp)
        if self.ops[lp][2]:
            if self.name in ("vector", "scalar", "gpsimd"):
                self.ops.append([lambda q: q.drain(), [], False])
            else:
                self.ops.append([lambda q: q.nop(), [], False])
            lp = len(self.ops) - 1
        self.ninc += 1
        self.incs.append((lp, self.ninc))
        self.ops[lp][1].append(("self", self.ninc))
        return self.ninc


class Prog:
    DMA_POOL = 4

    def __init__(self, nc):
        self.nc = nc
        self.engs = {n: _Eng(n) for n in ENG_NAMES}
        self.writers = {}
        self.readers = {}
        self.streams = {}
        self.sem_handles = {}
        self.fence_toks = []

    @staticmethod
    def _keys(aps, key):
        if key is not None:
            return list(key) if isinstance(key, (list, tuple)) else [key]
        out = []
        for ap in (aps if isinstance(aps, (list, tuple)) else [aps]):
            if ap is None or not hasattr(ap, "tensor"):
                continue
            out.append(ap.tensor.name)
        return out

    def _emit_wait(self, ename, sem, val):
        e = self.engs[ename]
        if e.waited.get(sem, -1) >= val:
            return
        e.waited[sem] = val
        e.ops.append([lambda q, s=sem, v=val: q.wait_ge(self.sem_handles[s], v), [], False])

    def op(self, eng, fn, outs=(), ins=(), okey=None, ikey=None, dma_sem=None,
           no_track=False):
        e = self.engs[eng]
        ik = self._keys(ins, ikey)
        ok = self._keys(outs, okey)
        deps = []
        if not no_track:
            deps.extend(self.fence_toks)
            for k in ik:
                w = self.writers.get(k)
                if w is not None:
                    deps.append(w)
            for k in ok:
                w = self.writers.get(k)
                if w is not None:
                    deps.append(w)
                deps.extend(self.readers.get(k, ()))
        need_drain = False
        for tok in deps:
            if isinstance(tok, _SwTok):
                tok = tok.resolve()
            if isinstance(tok, _DmaTok):
                self._emit_wait(eng, tok.sem, tok.val)
            elif tok.eng is e:
                if tok.pos >= e.last_drain:
                    need_drain = True
            else:
                cnt = tok.eng.milestone_for(tok.pos)
                self._emit_wait(eng, "M_" + tok.eng.name, cnt)
        if need_drain and eng in ("vector", "scalar", "gpsimd"):
            e.last_drain = len(e.ops)
            e.ops.append([lambda q: q.drain(), [], False])
        if dma_sem is not None:
            name, inc = (dma_sem, 16) if not isinstance(dma_sem, tuple) else dma_sem
            sw = eng == "gpsimd"
            name = name + "@" + eng[0]
            stream = self.streams.setdefault(
                name, {"counts": [0] * self.DMA_POOL, "armed": [None] * self.DMA_POOL,
                       "idx": 0, "inc": inc, "sw": sw})
            assert stream["sw"] == sw
            slot = stream["idx"] % self.DMA_POOL
            stream["idx"] += 1
            slot_sem = f"{name}#{slot}"
            if stream["counts"][slot] > 0:
                self._emit_wait(eng, slot_sem, inc * stream["counts"][slot])
        pos = len(e.ops)
        inc_list = []
        e.ops.append([fn, inc_list, dma_sem is not None])
        if dma_sem is not None:
            stream["counts"][slot] += 1
            inc_list.append((slot_sem, inc))
            tok = _DmaTok(slot_sem, inc * stream["counts"][slot])
        else:
            tok = _Tok(e, pos)
        if not no_track:
            for k in ok:
                self.writers[k] = tok
                self.readers[k] = []
            for k in ik:
                self.readers.setdefault(k, []).append(tok)
        return tok

    def _resolve_sw(self, stream, slot, slot_sem):
        g = self.engs["gpsimd"]
        cell = stream["armed"][slot]
        if cell[0] is None:
            g.ops.append([lambda q, s=slot_sem: q.wait_ge(self.sem_handles[s],
                                                          stream["inc"]), [], False])
            pos = len(g.ops)
            g.ops.append([lambda q, s=slot_sem: q.sem_clear(self.sem_handles[s]), [], False])
            cell[0] = _Tok(g, pos)
        stream["armed"][slot] = None
        return cell[0]

    def fence(self):
        toks = []
        for e in self.engs.values():
            if e.ops:
                toks.append(_Tok(e, len(e.ops) - 1))
        for name, stream in self.streams.items():
            for slot in range(self.DMA_POOL):
                if stream["counts"][slot] > 0:
                    toks.append(_DmaTok(f"{name}#{slot}",
                                        stream["inc"] * stream["counts"][slot]))
        self.fence_toks = toks

    def wait_tok(self, eng, tok):
        if isinstance(tok, _SwTok):
            tok = tok.resolve()
        if isinstance(tok, _DmaTok):
            self._emit_wait(eng, tok.sem, tok.val)
        elif tok.eng is self.engs[eng]:
            e = self.engs[eng]
            if tok.pos >= e.last_drain and eng in ("vector", "scalar", "gpsimd"):
                e.last_drain = len(e.ops)
                e.ops.append([lambda q: q.drain(), [], False])
        else:
            cnt = tok.eng.milestone_for(tok.pos)
            self._emit_wait(eng, "M_" + tok.eng.name, cnt)

    def replay(self, block, stack):
        for n in ENG_NAMES:
            self.sem_handles["M_" + n] = stack.enter_context(
                self.nc.semaphore("M_" + n))
        for name, stream in self.streams.items():
            for slot in range(self.DMA_POOL):
                s = f"{name}#{slot}"
                self.sem_handles[s] = stack.enter_context(
                    self.nc.semaphore(s.replace("#", "_").replace("@", "_")))
        prog = self

        def make_runner(ename):
            def run(q):
                for fn, incs, _dma in prog.engs[ename].ops:
                    inst = fn(q)
                    for kind, val in incs:
                        if kind == "self":
                            inst.then_inc(prog.sem_handles["M_" + ename], 1)
                        else:
                            try:
                                inst.then_inc(prog.sem_handles[kind], val)
                            except BaseException as ex:
                                raise RuntimeError(f"then_inc fail {ename} {kind} {val} {inst}") from ex
            return run
        for ename in ENG_NAMES:
            getattr(block, ename)(make_runner(ename))


class Arena:
    def __init__(self, nc, stack, name, nbytes):
        self.t = stack.enter_context(nc.sbuf_tensor(name, [128, nbytes // 4], F32))
        self.nbytes = nbytes
        self.off = 0
        self.marks = []

    def alloc(self, cols, dtype=F32, parts=128):
        esz = {F32: 4, BF16: 2, U32: 4}[dtype]
        nb = (cols * esz + 3) & ~3
        assert self.off + nb <= self.nbytes, (self.off, nb, self.nbytes)
        off = self.off
        self.off += nb
        base = self.t.ap()
        if dtype == BF16:
            return base.bitcast(BF16)[0:parts, off // 2: off // 2 + cols]
        if dtype == U32:
            return base.bitcast(U32)[0:parts, off // 4: off // 4 + cols]
        return base[0:parts, off // 4: off // 4 + cols]

    def push(self):
        self.marks.append(self.off)

    def pop(self):
        self.off = self.marks.pop()


class PsumPool:
    def __init__(self, nc, stack):
        self.ts = [stack.enter_context(nc.psum_tensor(f"psum{i}", [128, 512], F32))
                   for i in range(8)]
        self.aps = [t.ap() for t in self.ts]
        self.free = list(range(8))
        self.rr = 0

    def acquire(self):
        assert self.free, "psum pool exhausted"
        for _ in range(len(self.free)):
            b = self.free.pop(0)
            break
        return b

    def release(self, b):
        self.free.append(b)


CONST_LAYOUT = [
    ("iden_f", 128, F32),
    ("iden_b", 128, BF16),
    ("RT_b", 128, BF16),
    ("bdall", 128, BF16),
    ("hmall", 1024, BF16),
    ("ones_b", 1, BF16),
    ("zrow_b", 128, BF16),
    ("ones_row", 128, F32),
    ("triu_b", 128, BF16),
    ("mask0", 384, BF16),
    ("mask1", 384, BF16),
    ("maskf0", 384, BF16),
    ("maskf1", 384, BF16),
]
TAB_LAYOUT = [
    ("cosq", EXT, F32),
    ("sinq", EXT, F32),
    ("cosk", EXT, F32),
    ("sink", EXT, F32),
]
CONST_COLS = sum(c for _, c, _ in CONST_LAYOUT) + sum(c for _, c, _ in TAB_LAYOUT)

def _vc_build():
    vc, c = {}, 0
    def take(name, n):
        nonlocal c
        vc[name] = c
        c += n
    take("delta", NFT)
    take("normw", NFT)
    take("halomul", 1)
    take("qs", DEPTH)
    take("ks", DEPTH)
    take("lng", DEPTH * NFT)
    take("lnb", DEPTH * NFT)
    take("b2", DEPTH * NFT)
    take("rkM", NCORES)
    take("rkI", NCORES)
    take("skb2", 1)
    take("c_eps", 1)
    take("c_one", 1)
    take("c_epsD", 1)
    take("c_hlnD", 1)
    take("b1a", DEPTH * 22)
    take("b1g", DEPTH * 22)
    return vc, c

VC, VC_COLS = _vc_build()


class Build:
    def __init__(self, cfg):
        self.cfg = cfg
        self.nc = bass.Bass("TRN2", target_bir_lowering=False, debug=False)
        self.p = Prog(self.nc)
        self.stack = contextlib.ExitStack()
        self.dram = {}
        self._declare_dram()

    def _declare_dram(self):
        nc, dd = self.nc, self.dram
        dd["xs"] = nc.dram_tensor("xs", [EXT, E], F32, kind="ExternalInput").ap()
        for nm, shp in [
            ("attn_Wq", [E, E]), ("attn_Wk", [E, E]), ("attn_Wv", [E, E]),
            ("attn_Wo", [E, E]), ("attn_G1", [E, D]), ("attn_G2", [D, E]),
            ("ffn_Wqkv", [DEPTH, E, 3 * E]), ("ffn_Wo", [DEPTH, E, E]),
            ("ffn_W1", [DEPTH, E, 2 * DI]), ("ffn_W2", [DEPTH, DI, E]),
        ]:
            dd[nm] = nc.dram_tensor(nm, shp, F32, kind="ExternalInput").ap()
        dd["vecs"] = nc.dram_tensor("vecs", [128, VC_COLS], F32,
                                    kind="ExternalInput").ap()
        dd["consts"] = nc.dram_tensor("consts", [128, CONST_COLS], F32,
                                      kind="ExternalInput").ap()
        dd["hidx"] = nc.dram_tensor("hidx", [128, NFT], U32, kind="ExternalInput").ap()
        dd["y"] = nc.dram_tensor("y", [TC, E], F32, kind="ExternalOutput").ap()
        dd["glapark"] = nc.dram_tensor("glapark", [E, TC], F32).ap()
        dd["cin_s"] = nc.dram_tensor("cin_s", [128, 520], F32).ap()
        dd["cout_s"] = nc.dram_tensor("cout_s", [1024, 520], F32).ap()
        dd["cin_h"] = nc.dram_tensor("cin_h", [E, HALO], F32).ap()
        dd["cout_h"] = nc.dram_tensor("cout_h", [NCORES * E, HALO], F32).ap()
        dd["cin_g"] = nc.dram_tensor("cin_g", [E, 1], F32).ap()
        dd["cout_g"] = nc.dram_tensor("cout_g", [NCORES * E, 1], F32).ap()

    def mm(self, out, lhsT, rhs, start=True, stop=True, okey=None, ikey=None, tp=None):
        return self.p.op("tensor",
                         lambda q: q.matmul(out, lhsT, rhs, start=start, stop=stop,
                                            tile_position=tp),
                         outs=[out], ins=[lhsT, rhs], okey=okey, ikey=ikey)

    def trans(self, out, in_, ident, okey=None, ikey=None):
        return self.p.op("tensor", lambda q: q.transpose(out, in_, ident),
                         outs=[out], ins=[in_, ident], okey=okey, ikey=ikey)

    def tt(self, eng, out, a, b, op, okey=None, ikey=None):
        return self.p.op(eng, lambda q: q.tensor_tensor(out, a, b, op),
                         outs=[out], ins=[a, b], okey=okey, ikey=ikey)

    @staticmethod
    def _addkeys(ikey, aps):
        names = [a.tensor.name for a in aps if hasattr(a, "tensor")]
        if not names:
            return ikey
        if ikey is None:
            return None
        base = list(ikey) if isinstance(ikey, (list, tuple)) else [ikey]
        return base + names

    def ts(self, eng, out, in0, s1, s2, op0, op1=ALU.bypass, accum=None,
           okey=None, ikey=None):
        extra = [x for x in (s1, s2) if hasattr(x, "tensor")]
        outs = [out] + ([accum] if accum is not None else [])
        return self.p.op(eng, lambda q: q.tensor_scalar(out, in0, s1, s2, op0, op1,
                                                        accum_out=accum),
                         outs=outs, ins=[in0] + extra, okey=okey,
                         ikey=self._addkeys(ikey, extra))

    def stt(self, eng, out, in0, s, op0, in1, op1, accum=None, okey=None, ikey=None):
        extra = [s] if hasattr(s, "tensor") else []
        outs = [out] + ([accum] if accum is not None else [])
        return self.p.op(eng,
                         lambda q: q.scalar_tensor_tensor(out, in0, s, in1, op0, op1,
                                                          accum_out=accum),
                         outs=outs, ins=[in0, in1] + extra, okey=okey,
                         ikey=self._addkeys(ikey, extra))

    def act(self, out, in_, func, bias=0.0, scale=1.0, accum=None,
            okey=None, ikey=None):
        extra = [x for x in (bias, scale) if hasattr(x, "tensor")]
        outs = [out] + ([accum] if accum is not None else [])
        return self.p.op("scalar",
                         lambda q: q.activation(out, in_, func, bias=bias, scale=scale,
                                                accum_out=accum),
                         outs=outs, ins=[in_] + extra, okey=okey,
                         ikey=self._addkeys(ikey, extra))

    def copy(self, eng, out, in_, okey=None, ikey=None):
        if eng == "scalar":
            return self.p.op(eng, lambda q: q.copy(out, in_), outs=[out], ins=[in_],
                             okey=okey, ikey=ikey)
        return self.p.op(eng, lambda q: q.tensor_copy(out, in_),
                         outs=[out], ins=[in_], okey=okey, ikey=ikey)

    def red(self, out, in_, op, negate=False, okey=None, ikey=None):
        return self.p.op("vector",
                         lambda q: q.tensor_reduce(out, in_, AX.X, op, negate=negate),
                         outs=[out], ins=[in_], okey=okey, ikey=ikey)

    def scan(self, out, d0, op0=ALU.add, okey=None, ikey=None):
        return self.p.op("vector",
                         lambda q: q.tensor_tensor_scan(out, d0, d0, 0.0, op0,
                                                        ALU.bypass),
                         outs=[out], ins=[d0], okey=okey, ikey=ikey)

    def recip(self, out, in_, okey=None, ikey=None):
        return self.p.op("vector", lambda q: q.reciprocal(out, in_),
                         outs=[out], ins=[in_], okey=okey, ikey=ikey)

    def dma(self, eng, out, in_, sem, okey=None, ikey=None, slow=False):
        return self.p.op(eng,
                         lambda q: q.dma_start(out=out, in_=in_,
                                               allow_slow_non_contiguous=slow),
                         outs=[out], ins=[in_], okey=okey, ikey=ikey, dma_sem=sem)

    def gather(self, out, in_, idx, sem, okey=None, ikey=None):
        return self.p.op("gpsimd",
                         lambda q: q.indirect_dma_start(
                             out=out, out_offset=None, in_=in_,
                             in_offset=IndirectOffsetOnAxis(ap=idx, axis=0)),
                         outs=[out], ins=[in_, idx], okey=okey, ikey=ikey, dma_sem=sem)

    def memset(self, eng, ap, val, okey=None):
        return self.p.op(eng, lambda q: q.memset(ap, val), outs=[ap], okey=okey)

    def cc(self, kind, op, cin, cout, okey=None, ikey=None):
        return self.p.op("gpsimd",
                         lambda q: q.collective_compute(kind, op, RG, ins=[cin.opt()],
                                                        outs=[cout.opt()]),
                         outs=[cout], ins=[cin], okey=okey, ikey=ikey,
                         dma_sem=("d_cc", 1))

    def vcol(self, name, i=0):
        return self.vecs[:, VC[name] + i: VC[name] + i + 1]

    def build(self):
        nc, st = self.nc, self.stack
        cfg = self.cfg
        self.rT = [st.enter_context(nc.sbuf_tensor(f"rT{f}", [128, EXT], F32)).ap()
                   for f in range(NFT)]
        self.pp = PsumPool(nc, st)
        ca = Arena(nc, st, "ca", 4 * VC_COLS + 4 * NFT
                   + sum(c * (2 if d == BF16 else 4) + 4 for _, c, d in CONST_LAYOUT) + 64)
        self.ctab = {}
        for nm, cols, dt in CONST_LAYOUT:
            self.ctab[nm] = ca.alloc(cols, dt)
        self.vecs = st.enter_context(nc.sbuf_tensor("vecs_t", [128, VC_COLS], F32)).ap()
        self.hidx = ca.alloc(NFT, U32)
        self.work = Arena(nc, st, "work", 127584)
        self._load_consts()
        self._phase_top()
        self.p.fence()
        if cfg["gla"]:
            self._phase_gla()
        else:
            self._park_zero()
        self.p.fence()
        for l in range(cfg["nlayers"]):
            self._layer_attn(l)
            self._layer_glu(l)
            self.p.fence()
        self._phase_final()
        with st:
            with nc.Block() as block:
                self.p.replay(block, st)
        return nc

    def _load_consts(self):
        dd = self.dram
        c = dd["consts"]
        off = 0
        for nm, cols, dt in CONST_LAYOUT:
            src = c[:, off:off + cols]
            if dt == BF16:
                self.dma("gpsimd", self.ctab[nm], src, "d_const", okey="CONST",
                         slow=(cols < 2))
            else:
                self.dma("sync", self.ctab[nm], src, "d_const", okey="CONST",
                         slow=(cols < 2))
            off += cols
        self.dma("sync", self.vecs, dd["vecs"], "d_const")
        self.dma("gpsimd", self.hidx, dd["hidx"], "d_const", okey="HIDX")

    def _ffn_setup(self):
        if getattr(self, "_ffn_ready", False):
            return
        self._ffn_ready = True
        wk = self.work
        self.fb = {"halo": wk.alloc(HALO)}
        for nm in ("cosq", "sinq", "cosk", "sink"):
            self.fb["t_" + nm] = wk.alloc(512, BF16)
        off = sum(c for _, c, _ in CONST_LAYOUT)
        self.tab_off = {}
        for nm, cols, _ in TAB_LAYOUT:
            self.tab_off[nm] = off
            off += cols
    def _phase_top(self):
        dd = self.dram
        wk = self.work
        wk.push()
        xt = [wk.alloc(E) for _ in range(2)]
        xsq = [wk.alloc(E) for _ in range(2)]
        xn = [wk.alloc(E) for _ in range(2)]
        stat = [wk.alloc(8) for _ in range(2)]
        ntile = (EXT + 127) // 128
        for i in range(ntile):
            rows = min(128, EXT - i * 128)
            b = i % 2
            kx, ks_, kq, kn = f"xt{b}", f"st{b}", f"xq{b}", f"xn{b}"
            x_t = xt[b][0:rows, :]
            self.dma("sync", x_t, dd["xs"][i * 128:i * 128 + rows, :], "d_x", okey=kx)
            mu = stat[b][0:rows, 0:1]
            sq = stat[b][0:rows, 1:2]
            var = stat[b][0:rows, 2:3]
            rstd = stat[b][0:rows, 3:4]
            self.red(mu, x_t, ALU.add, okey=ks_, ikey=kx)
            self.ts("vector", mu, mu, 1.0 / E, None, ALU.mult, okey=ks_, ikey=ks_)
            self.tt("vector", xsq[b][0:rows, :], x_t, x_t, ALU.mult, okey=kq, ikey=kx)
            self.red(sq, xsq[b][0:rows, :], ALU.add, okey=ks_, ikey=kq)
            self.tt("vector", var, mu, mu, ALU.mult, okey=ks_, ikey=ks_)
            self.stt("vector", var, sq, 1.0 / E, ALU.mult, var, ALU.subtract,
                     okey=ks_, ikey=ks_)
            self.act(var, var, AF.Ln, bias=self.vcol("c_eps")[0:rows, :],
                     okey=ks_, ikey=ks_)
            self.act(rstd, var, AF.Exp, scale=-0.5, okey=ks_, ikey=ks_)
            xn_t = xn[b][0:rows, :]
            self.stt("vector", xn_t, x_t, mu, ALU.subtract,
                     rstd.broadcast_to((rows, E)), ALU.mult, okey=kn, ikey=[kx, ks_])
            for f in range(NFT):
                pb = self.pp.acquire()
                ps = self.pp.aps[pb]
                pk = f"PS{pb}"
                self.trans(ps[0:128, 0:rows], xn_t[:, f * 128:(f + 1) * 128],
                           self.ctab["iden_f"][0:rows, 0:rows], okey=pk,
                           ikey=[kn, "CONST"])
                self.copy("vector", self.rT[f][:, i * 128:i * 128 + rows],
                          ps[0:128, 0:rows], okey=f"rT{f}", ikey=pk)
                self.pp.release(pb)
        wk.pop()

    def _park_zero(self):
        wk = self.work
        wk.push()
        z = wk.alloc(512)
        self.memset("vector", z, 0.0, okey="zz")
        for f in range(NFT):
            for c in range(4):
                self.dma("sync",
                         self.dram["glapark"][f * 128:(f + 1) * 128,
                                              c * 512:(c + 1) * 512],
                         z, "d_park", ikey="zz", okey="glapark")
        wk.pop()

    def _phase_gla(self):
        dd = self.dram
        wk = self.work
        wk.push()
        wq = [wk.alloc(E, BF16) for _ in range(NFT)]
        wkk = [wk.alloc(E, BF16) for _ in range(NFT)]
        wv = [wk.alloc(E, BF16) for _ in range(NFT)]
        for f in range(NFT):
            self.dma("gpsimd", wkk[f], dd["attn_Wk"][f * 128:(f + 1) * 128, :],
                     "d_w", okey=f"Wk{f}")
            self.dma("gpsimd", wv[f], dd["attn_Wv"][f * 128:(f + 1) * 128, :],
                     "d_w", okey=f"Wv{f}")
            self.dma("gpsimd", wq[f], dd["attn_Wq"][f * 128:(f + 1) * 128, :],
                     "d_w", okey=f"Wq{f}")
        S = wk.alloc(512)
        Sb = wk.alloc(512, BF16)
        Gtot = wk.alloc(NFT)
        self.memset("vector", S, 0.0, okey=[f"S{f}" for f in range(NFT)])
        self.memset("vector", Gtot, 0.0, okey="Gtot")
        NB = 256
        bufs = {
            "G": [wk.alloc(NB) for _ in range(NFT)],
            "k": [wk.alloc(NB, BF16) for _ in range(NFT)],
            "sc1": wk.alloc(NB), "sc2": wk.alloc(NB), "sc3": wk.alloc(CHUNK),
            "kdec": [wk.alloc(CHUNK, BF16) for _ in range(NFT)],
            "kh": [wk.alloc(CHUNK, BF16) for _ in range(NFT)],
            "qt": [wk.alloc(CHUNK, BF16) for _ in range(NFT)],
            "sq": [wk.alloc(NB, BF16) for _ in range(NFT)],
            "strm": [wk.alloc(NB, BF16) for _ in range(NFT)],
            "v": [wk.alloc(E, BF16) for _ in range(NB // 128)],
            "kdT": wk.alloc(E, BF16),
            "Ab": wk.alloc(512, BF16),
            "gcol": wk.alloc(32),
            "sgate": [wk.alloc(E, BF16) for _ in range(NB // 128)],
            "of": wk.alloc(E),
            "sc64": wk.alloc(D),
            "oh": wk.alloc(E, BF16),
            "ohT": [wk.alloc(NB, BF16) for _ in range(NFT)],
            "msq": wk.alloc(32),
            "g1": wk.alloc(D),
            "g1b": wk.alloc(D, BF16),
            "g1T": wk.alloc(128, BF16, parts=64),
            "glaT": wk.alloc(256),
        }
        self._gla_pass(bufs, (wq, wkk, wv), S, Sb, Gtot, pass_b=False)
        self.p.fence()
        if self.cfg.get("exch", True):
            self._gla_exchange(S, Gtot, wk)
        else:
            self.memset("vector", S, 0.0, okey=[f"S{f}" for f in range(NFT)])
        self.p.fence()
        if self.cfg.get("passb", True):
            self._gla_pass(bufs, (wq, wkk, wv), S, Sb, Gtot, pass_b=True)
        wk.pop()

    def _gla_logf_chain(self, zps, zkey, f, ncols, bufs):
        G, k = bufs["G"][f], bufs["k"][f]
        sc1, sc2 = bufs["sc1"], bufs["sc2"]
        va = lambda ap: ap[:, 0:ncols]
        kG, kk = f"G{f}", f"k{f}"
        self.ts("vector", va(sc1), zps, self.vcol("delta", f), None, ALU.add,
                okey="sc1", ikey=[zkey, "vecs_t"])
        self.stt("vector", va(sc2), va(sc1), -1.0, ALU.mult, va(sc1), ALU.max,
                 okey="sc2", ikey="sc1")
        self.act(va(sc2), va(sc2), AF.Exp, scale=-1.0, okey="sc2", ikey="sc2")
        self.act(va(sc2), va(sc2), AF.Ln, bias=self.vcol("c_one"), okey="sc2",
                 ikey="sc2")
        self.stt("vector", va(G), va(sc1), 0.0, ALU.min, va(sc2), ALU.subtract,
                 okey=kG, ikey=["sc1", "sc2"])
        self.act(va(sc2), va(G), AF.Exp, okey="sc2", ikey=kG)
        self.ts("vector", va(k), va(sc2), -1.0, 1.0, ALU.mult, ALU.add,
                okey=kk, ikey="sc2")
        for c in range(ncols // CHUNK):
            s = slice(c * CHUNK, (c + 1) * CHUNK)
            self.scan(G[:, s], G[:, s], okey=kG, ikey=kG)

    def _gla_pass(self, bufs, ws, S, Sb, Gtot, pass_b):
        wq, wkk, wv = ws
        dd = self.dram
        tag = "B" if pass_b else "A"
        wk = self.work
        if pass_b:
            wk.push()
            self._wo = [wk.alloc(E, BF16) for _ in range(NFT)]
            self._g1w = [wk.alloc(D, BF16) for _ in range(NFT)]
            self._g2w = wk.alloc(E, BF16, parts=64)
            for f in range(NFT):
                self.dma("gpsimd", self._wo[f], dd["attn_Wo"][f * 128:(f + 1) * 128, :],
                         "d_w", okey=f"Wo{f}")
                self.ts("vector", self._wo[f], self._wo[f], self.vcol("normw", f),
                        None, ALU.mult, okey=f"Wo{f}", ikey=[f"Wo{f}", "vecs_t"])
                self.dma("gpsimd", self._g1w[f],
                         dd["attn_G1"][f * 128:(f + 1) * 128, :], "d_w", okey=f"G1{f}")
            self.dma("gpsimd", self._g2w, dd["attn_G2"][0:64, :], "d_w", okey="G2w")
        NB = 256
        strm = bufs["strm"]
        for blk in range(self.cfg.get("blocks", TC // NB)):
            c0 = HALO + blk * NB
            for f in range(NFT):
                self.copy("vector", strm[f][:, 0:NB], self.rT[f][:, c0:c0 + NB],
                          okey=f"strm{f}", ikey=f"rT{f}")
            qps = []
            if pass_b:
                for f in range(NFT):
                    pb = self.pp.acquire()
                    ps = self.pp.aps[pb]
                    for kt in range(NFT):
                        self.mm(ps[0:128, 0:NB], wq[kt][:, f * 128:(f + 1) * 128],
                                strm[kt][:, 0:NB],
                                start=(kt == 0), stop=(kt == NFT - 1),
                                okey=f"PS{pb}", ikey=[f"Wq{kt}", f"strm{kt}"])
                    qps.append(pb)
                for f in range(NFT):
                    pb = qps[f]
                    self.act(bufs["sq"][f][:, 0:NB], self.pp.aps[pb][0:128, 0:NB],
                             AF.Silu, okey=f"sq{f}", ikey=f"PS{pb}")
                    self.pp.release(pb)
                for mt in range(NB // 128):
                    pb = self.pp.acquire()
                    ps = self.pp.aps[pb]
                    for kt in range(NFT):
                        self.mm(ps[0:128, 0:64], strm[kt][:, mt * 128:(mt + 1) * 128],
                                self._g1w[kt], start=(kt == 0), stop=(kt == NFT - 1),
                                okey=f"PS{pb}", ikey=[f"strm{kt}", f"G1{kt}"])
                    self.copy("vector", bufs["g1"], ps[0:128, 0:64],
                              okey="g1", ikey=f"PS{pb}")
                    self.pp.release(pb)
                    self.copy("vector", bufs["g1b"], bufs["g1"], okey="g1b", ikey="g1")
                    pb = self.pp.acquire()
                    psb = self.pp.aps[pb].bitcast(BF16)
                    self.trans(psb[0:64, 0:128], bufs["g1b"], self.ctab["iden_b"],
                               okey=f"PS{pb}", ikey=["g1b", "CONST"])
                    self.copy("vector", bufs["g1T"], psb[0:64, 0:128],
                              okey="g1T", ikey=f"PS{pb}")
                    self.pp.release(pb)
                    for half in range(2):
                        pb = self.pp.acquire()
                        ps = self.pp.aps[pb]
                        self.mm(ps, bufs["g1T"], self._g2w[0:64, half * 512:half * 512 + 512],
                                okey=f"PS{pb}", ikey=["g1T", "G2w"])
                        self.act(bufs["sgate"][mt][:, half * 512:half * 512 + 512], ps,
                                 AF.Sigmoid, okey=f"sgate{mt}", ikey=f"PS{pb}")
                        self.pp.release(pb)
            if self.cfg.get("glastop", 9) < 2:
                continue
            for f in range(NFT):
                pb = self.pp.acquire()
                ps = self.pp.aps[pb]
                for kt in range(NFT):
                    self.mm(ps[0:128, 0:NB], wkk[kt][:, f * 128:(f + 1) * 128],
                            strm[kt][:, 0:NB],
                            start=(kt == 0), stop=(kt == NFT - 1),
                            okey=f"PS{pb}", ikey=[f"Wk{kt}", f"strm{kt}"])
                self._gla_logf_chain(ps[0:128, 0:NB], f"PS{pb}", f, NB, bufs)
                self.pp.release(pb)
            if self.cfg.get("glastop", 9) < 3:
                continue
            for mt in range(NB // 128):
                for half in range(2):
                    pb = self.pp.acquire()
                    ps = self.pp.aps[pb]
                    for kt in range(NFT):
                        self.mm(ps, strm[kt][:, mt * 128:(mt + 1) * 128],
                                wv[kt][:, half * 512:half * 512 + 512],
                                start=(kt == 0), stop=(kt == NFT - 1),
                                okey=f"PS{pb}", ikey=[f"strm{kt}", f"Wv{kt}"])
                    self.copy("scalar", bufs["v"][mt][:, half * 512:half * 512 + 512],
                              ps, okey=f"v{mt}", ikey=f"PS{pb}")
                    self.pp.release(pb)
            if self.cfg.get("glastop", 9) < 4:
                continue
            for cc in range(NB // CHUNK):
                self._gla_chunk(bufs, S, Sb, Gtot, blk, cc, pass_b)
        if pass_b:
            wk.pop()

    def _gla_chunk(self, bufs, S, Sb, Gtot, blk, cc, pass_b):
        cs = slice(cc * CHUNK, (cc + 1) * CHUNK)
        gcol = bufs["gcol"]
        sc3 = bufs["sc3"]
        for f in range(NFT):
            G = bufs["G"][f]
            GC = gcol[:, f:f + 1]
            self.copy("vector", GC, G[:, cc * CHUNK + 127:cc * CHUNK + 128],
                      okey=f"gc{f}", ikey=f"G{f}")
            self.ts("vector", sc3[:, 0:CHUNK], G[:, cs], -1.0, GC, ALU.mult, ALU.add,
                    okey="sc3", ikey=[f"G{f}", f"gc{f}"])
            self.act(sc3[:, 0:CHUNK], sc3[:, 0:CHUNK], AF.Exp, okey="sc3", ikey="sc3")
            self.tt("vector", bufs["kdec"][f][:, 0:CHUNK], bufs["k"][f][:, cs],
                    sc3[:, 0:CHUNK], ALU.mult, okey=f"kdec{f}", ikey=[f"k{f}", "sc3"])
            if pass_b:
                Rc = gcol[:, 8 + f:9 + f]
                self.copy("vector", Rc, G[:, cc * CHUNK + 63:cc * CHUNK + 64],
                          okey=f"gc{f}", ikey=f"G{f}")
                self.ts("vector", sc3[:, 0:CHUNK], G[:, cs], Rc, None, ALU.subtract,
                        okey="sc3", ikey=[f"G{f}", f"gc{f}"])
                self.act(sc3[:, 0:CHUNK], sc3[:, 0:CHUNK], AF.Exp, okey="sc3", ikey="sc3")
                self.tt("vector", bufs["qt"][f][:, 0:CHUNK], bufs["sq"][f][:, cs],
                        sc3[:, 0:CHUNK], ALU.mult, okey=f"qt{f}", ikey=[f"sq{f}", "sc3"])
                self.ts("vector", sc3[:, 0:CHUNK], G[:, cs], -1.0, Rc, ALU.mult, ALU.add,
                        okey="sc3", ikey=[f"G{f}", f"gc{f}"])
                self.act(sc3[:, 0:CHUNK], sc3[:, 0:CHUNK], AF.Exp, okey="sc3", ikey="sc3")
                self.tt("vector", bufs["kh"][f][:, 0:CHUNK], bufs["k"][f][:, cs],
                        sc3[:, 0:CHUNK], ALU.mult, okey=f"kh{f}", ikey=[f"k{f}", "sc3"])
                eR = gcol[:, 16 + f:17 + f]
                self.act(eR, Rc, AF.Exp, okey=f"gc{f}", ikey=f"gc{f}")
                self.ts("vector", Sb[:, f * 64:(f + 1) * 64], S[:, f * 64:(f + 1) * 64],
                        eR, None, ALU.mult, okey=f"Sb{f}", ikey=[f"S{f}", f"gc{f}"])
        if self.cfg.get("glastop", 9) < 5:
            return
        vtile = bufs["v"][cc]
        opair = []
        if pass_b:
            po = [self.pp.acquire(), self.pp.acquire()]
            for h in range(H):
                f, half = h // 2, h % 2
                pbase = half * 64
                pa = self.pp.acquire()
                aps = self.pp.aps[pa]
                self.mm(aps[0:128, 0:128], bufs["kh"][f][pbase:pbase + 64, 0:CHUNK],
                        bufs["qt"][f][pbase:pbase + 64, 0:CHUNK],
                        okey=f"PS{pa}", ikey=[f"kh{f}", f"qt{f}"], tp=(pbase, 0))
                ab = bufs["Ab"][:, (h % 4) * 128:(h % 4) * 128 + 128]
                self.tt("vector", ab, aps[0:128, 0:128], self.ctab["triu_b"], ALU.mult,
                        okey=f"Ab{h%4}", ikey=[f"PS{pa}", "CONST"])
                self.pp.release(pa)
                od = self.pp.aps[po[h // 8]][0:128, (h % 8) * 64:(h % 8) * 64 + 64]
                self.mm(od, ab, vtile[:, h * 64:(h + 1) * 64], start=True, stop=False,
                        okey=f"PS{po[h//8]}", ikey=[f"Ab{h%4}", f"v{cc}"])
                self.mm(od, bufs["qt"][f][pbase:pbase + 64, 0:CHUNK],
                        Sb[pbase:pbase + 64, f * 64:(f + 1) * 64],
                        start=False, stop=True,
                        okey=f"PS{po[h//8]}", ikey=[f"qt{f}", f"Sb{f}"], tp=(pbase, 0))
            opair = po
        if self.cfg.get("glastop", 9) < 6:
            return
        pk = self.pp.acquire()
        pkb = self.pp.aps[pk].bitcast(BF16)
        kdT = bufs["kdT"]
        for f in range(NFT):
            slot = (f % 4) * 128
            self.trans(pkb[0:128, slot:slot + 128], bufs["kdec"][f][:, 0:CHUNK],
                       self.ctab["iden_b"], okey=f"PS{pk}",
                       ikey=[f"kdec{f}", "CONST"])
            self.copy("vector", kdT[:, f * 128:(f + 1) * 128], pkb[0:128, slot:slot + 128],
                      okey=f"kdT{f}", ikey=f"PS{pk}")
        self.pp.release(pk)
        if self.cfg.get("glastop", 9) < 7:
            self.pp.release(pk) if False else None
            return
        pS = self.pp.acquire()
        psS = self.pp.aps[pS]
        for h in range(H):
            f, half = h // 2, h % 2
            self.mm(psS[half * 64:half * 64 + 64, f * 64:(f + 1) * 64],
                    kdT[:, h * 64:(h + 1) * 64], vtile[:, h * 64:(h + 1) * 64],
                    tp=(0, half * 64), okey=f"PS{pS}", ikey=[f"kdT{f}", f"v{cc}"])
        if self.cfg.get("glastop", 9) < 8:
            self.pp.release(pS)
            return
        for f in range(NFT):
            PcC = gcol[:, 24 + f:25 + f]
            self.act(PcC, gcol[:, f:f + 1], AF.Exp, okey=f"gc{f}", ikey=f"gc{f}")
            self.stt("vector", S[:, f * 64:(f + 1) * 64], S[:, f * 64:(f + 1) * 64],
                     PcC, ALU.mult, psS[:, f * 64:(f + 1) * 64], ALU.add,
                     okey=f"S{f}", ikey=[f"S{f}", f"gc{f}", f"PS{pS}"])
        self.pp.release(pS)
        self.tt("vector", Gtot, Gtot, gcol[:, 0:NFT], ALU.add,
                okey="Gtot", ikey=["Gtot"] + [f"gc{f}" for f in range(NFT)])
        if pass_b:
            self._gla_out(opair, bufs, blk, cc)
            for b in opair:
                self.pp.release(b)

    def _gla_out(self, po, bufs, blk, cc):
        of, oh = bufs["of"], bufs["oh"]
        sg = bufs["sgate"][cc]
        self.tt("vector", of[:, 0:512], self.pp.aps[po[0]], sg[:, 0:512], ALU.mult,
                okey="of", ikey=[f"PS{po[0]}", f"sgate{cc}"])
        self.tt("vector", of[:, 512:1024], self.pp.aps[po[1]], sg[:, 512:1024],
                ALU.mult, okey="of", ikey=[f"PS{po[1]}", f"sgate{cc}"])
        self.act(oh, of, AF.Square, okey="oh", ikey="of")
        msq = bufs["msq"][:, 0:16]
        rstd = bufs["msq"][:, 16:32]
        self.red(msq, oh.rearrange("p (h d) -> p h d", d=D), ALU.add,
                 okey="msq", ikey="oh")
        self.act(msq, msq, AF.Ln, bias=self.vcol("c_epsD"), okey="msq", ikey="msq")
        self.act(rstd, msq, AF.Exp, scale=-0.5, bias=self.vcol("c_hlnD"),
                 okey="msq", ikey="msq")
        self.tt("vector", oh, of,
                rstd.rearrange("p (h o) -> p h o", o=1).broadcast_to((128, H, D)), ALU.mult,
                okey="oh", ikey=["of", "msq"])
        for f in range(NFT):
            pb = self.pp.acquire()
            psb = self.pp.aps[pb].bitcast(BF16)
            self.trans(psb[0:128, 0:128], oh[:, f * 128:(f + 1) * 128],
                       self.ctab["iden_b"], okey=f"PS{pb}", ikey=["oh", "CONST"])
            self.copy("vector", bufs["ohT"][f][:, cc * 128:(cc + 1) * 128],
                      psb[0:128, 0:128], okey=f"ohT{f}", ikey=f"PS{pb}")
            self.pp.release(pb)
        if cc == 1:
            glaT = bufs["glaT"]
            for mt in range(NFT):
                pb = self.pp.acquire()
                ps = self.pp.aps[pb]
                for kt in range(NFT):
                    self.mm(ps[0:128, 0:256], self._wo[kt][:, mt * 128:(mt + 1) * 128],
                            bufs["ohT"][kt][:, 0:256], start=(kt == 0),
                            stop=(kt == NFT - 1),
                            okey=f"PS{pb}", ikey=[f"Wo{kt}", f"ohT{kt}"])
                self.copy("vector", glaT, ps[0:128, 0:256],
                          okey="glaT", ikey=f"PS{pb}")
                self.pp.release(pb)
                self.dma("sync",
                         self.dram["glapark"][mt * 128:(mt + 1) * 128,
                                              blk * 256:(blk + 1) * 256],
                         glaT, "d_park", ikey="glaT", okey="glapark")

    def _gla_exchange(self, S, Gtot, wk):
        dd = self.dram
        wk.push()
        pack = wk.alloc(520)
        self.copy("vector", pack[:, 0:512], S, okey="pack",
                  ikey=[f"S{f}" for f in range(NFT)])
        self.act(pack[:, 512:520], Gtot, AF.Exp, okey="pack", ikey="Gtot")
        self.dma("sync", dd["cin_s"], pack, "d_cc2", ikey="pack", okey="cin_s")
        self.cc("AllGather", ALU.bypass, dd["cin_s"], dd["cout_s"],
                okey="cout_s", ikey="cin_s")
        self.memset("vector", S, 0.0, okey=[f"S{f}" for f in range(NFT)])
        rbuf = wk.alloc(520)
        pr = wk.alloc(NFT)
        rs = wk.alloc(512)
        for r in range(NCORES):
            self.dma("sync", rbuf, dd["cout_s"][r * 128:(r + 1) * 128, :], "d_cc2",
                     okey="rbuf", ikey="cout_s")
            self.stt("vector", pr, rbuf[:, 512:520], self.vcol("rkM", r), ALU.mult,
                     self.vcol("rkI", r).broadcast_to((128, NFT)), ALU.add,
                     okey="pr", ikey=["rbuf", "vecs_t"])
            self.ts("vector", rs, rbuf[:, 0:512], self.vcol("rkM", r), None, ALU.mult,
                    okey="rs", ikey=["rbuf", "vecs_t"])
            for f in range(NFT):
                self.stt("vector", S[:, f * 64:(f + 1) * 64], S[:, f * 64:(f + 1) * 64],
                         pr[:, f:f + 1], ALU.mult, rs[:, f * 64:(f + 1) * 64], ALU.add,
                         okey=f"S{f}", ikey=[f"S{f}", "pr", "rs"])
        wk.pop()

    def _ffn_setup(self):
        if getattr(self, "_ffn_ready", False):
            return
        self._ffn_ready = True
        wk = self.work
        self.fb = {"halo": wk.alloc(HALO)}
        for nm in ("cosq", "sinq", "cosk", "sink"):
            self.fb["t_" + nm] = wk.alloc(512, BF16)
        off = sum(c for _, c, _ in CONST_LAYOUT)
        self.tab_off = {}
        for nm, cols, _ in TAB_LAYOUT:
            self.tab_off[nm] = off
            off += cols

    def _attn_frame(self):
        wk = self.work
        fb = dict(self.fb)
        fb["wqkv"] = [wk.alloc(3 * E, BF16) for _ in range(NFT)]
        fb["wo"] = [wk.alloc(E, BF16) for _ in range(NFT)]
        fb["strm"] = [wk.alloc(257, BF16) for _ in range(NFT)]
        fb["xb"] = [wk.alloc(W, BF16) for _ in range(2)]
        fb["q2"] = [wk.alloc(W, BF16) for _ in range(2)]
        fb["t2"] = [wk.alloc(W, BF16) for _ in range(2)]
        fb["qt"] = [[wk.alloc(W, BF16) for _ in range(NFT)] for _ in range(2)]
        fb["kring"] = [[wk.alloc(W, BF16) for _ in range(NFT)] for _ in range(2)]
        fb["vring"] = [[wk.alloc(E, BF16) for _ in range(2)] for _ in range(2)]
        fb["ktw"] = [wk.alloc(2 * W, BF16) for _ in range(NFT)]
        fb["P"] = [wk.alloc(2 * 384, BF16) for _ in range(2)]
        fb["PTs"] = [wk.alloc(1024, BF16) for _ in range(2)]
        fb["attnT"] = [wk.alloc(W, BF16) for _ in range(NFT)]
        fb["rqT"] = [wk.alloc(16) for _ in range(16)]
        fb["rn"] = wk.alloc(256)
        fb["rnb"] = wk.alloc(256, BF16)
        fb["cols"] = [wk.alloc(8) for _ in range(2)]
        fb["oldc"] = [wk.alloc(1) for _ in range(NFT)]
        return fb

    GRP = [(0, 6), (6, 12), (12, 18), (18, 22)]

    def _glu_frame(self):
        wk = self.work
        fb = dict(self.fb)
        fb["rsh"] = [wk.alloc(512, BF16) for _ in range(NFT)]
        fb["sq"] = [wk.alloc(512, BF16) for _ in range(2)]
        fb["xnT"] = [[wk.alloc(512, BF16) for _ in range(NFT)] for _ in range(2)]
        fb["w1g"] = [[[wk.alloc(768, BF16) for _ in range(NFT)] for _ in range(2)]
                     for _ in range(2)]
        fb["w2t"] = [wk.alloc(E, BF16) for _ in range(3)]
        fb["ht"] = [wk.alloc(512, BF16) for _ in range(22)]
        fb["gel"] = wk.alloc(512)
        fb["gel2"] = [wk.alloc(512) for _ in range(2)]
        fb["srow"] = wk.alloc(3 * 512)
        fb["oldg"] = [wk.alloc(3) for _ in range(NFT)]
        return fb

    def _halo_in(self, l, kind):
        fb = self.fbl
        dd = self.dram
        cout = dd["cout_h"] if kind == "a" else dd["cout_g"]
        ncol = HALO if kind == "a" else 1
        dst0 = 0 if kind == "a" else 256
        for f in range(NFT):
            g = self.gather(fb["halo"][:, 0:ncol], cout[:, 0:ncol],
                            self.hidx[:, f:f + 1], "d_gath",
                            okey="HALO", ikey=[f"cout_{kind}", "HIDX"])
            self.ts("vector", self.rT[f][:, dst0:dst0 + ncol], fb["halo"][:, 0:ncol],
                    self.vcol("halomul"), None, ALU.mult,
                    okey=f"rT{f}", ikey=["HALO"])

    def _halo_out(self, l, kind):
        dd = self.dram
        if kind == "a":
            cin, cout, c0, ncol = dd["cin_h"], dd["cout_h"], EXT - HALO, HALO
        else:
            cin, cout, c0, ncol = dd["cin_g"], dd["cout_g"], EXT - 1, 1
        for f in range(NFT):
            self.dma("sync", cin[f * 128:(f + 1) * 128, :],
                     self.rT[f][:, c0:c0 + ncol], "d_cc2",
                     okey=f"cin_{kind}", ikey=f"rT{f}", slow=(ncol < 2))
        self.cc("AllGather", ALU.bypass, cin, cout,
                okey=f"cout_{kind}", ikey=f"cin_{kind}")

    def _attn_proj_block(self, l, c0, n, is_halo):
        fb = self.fbl
        ct = self.ctab
        dd = self.dram
        blkw0 = (c0 - HALO) // W
        tabn = ("cosk", "sink") if is_halo else ("cosk", "sink", "cosq", "sinq")
        for nm in tabn:
            self.dma("gpsimd", fb["t_" + nm][:, 0:n],
                     dd["consts"][:, self.tab_off[nm] + c0:self.tab_off[nm] + c0 + n],
                     "d_tab", okey=f"ftab_{nm}")
        use_oldc = c0 > 257
        for f in range(NFT):
            if use_oldc:
                self.copy("vector", fb["strm"][f][:, 0:1], fb["oldc"][f],
                          okey=f"fstrm{f}", ikey=f"foldc{f}")
                self.copy("vector", fb["strm"][f][:, 1:n + 1],
                          self.rT[f][:, c0:c0 + n], okey=f"fstrm{f}", ikey=f"rT{f}")
            else:
                self.copy("vector", fb["strm"][f][:, 0:n + 1],
                          self.rT[f][:, c0 - 1:c0 + n], okey=f"fstrm{f}", ikey=f"rT{f}")
        def rhs(kt, a, b):
            off = 1 if kt < 4 else 0
            return fb["strm"][kt][:, a + off:b + off]
        for part, tabs, scol in (("k", ("cosk", "sink"), "ks"),
                                 ("q", ("cosq", "sinq"), "qs")):
            if part == "q" and is_halo:
                continue
            sspb = self.pp.acquire()
            ssps = self.pp.aps[sspb]
            for f in range(NFT):
                fp = f % 2
                kxb, kq2, kt2 = f"fxb{fp}", f"fq2{fp}", f"ft2{fp}"
                xb, q2, t2 = fb["xb"][fp], fb["q2"][fp], fb["t2"][fp]
                pb = self.pp.acquire()
                ps = self.pp.aps[pb]
                for kt in range(NFT):
                    self.mm(ps[0:128, 0:n],
                            fb["wqkv"][kt][:, (0 if part == "q" else E) + f * 128:
                                           (0 if part == "q" else E) + f * 128 + 128],
                            rhs(kt, 0, n), start=(kt == 0), stop=(kt == NFT - 1),
                            okey=f"PS{pb}", ikey=[f"fw{kt}", f"fstrm{kt}"])
                self.act(q2[:, 0:n], ps[0:128, 0:n], AF.Square,
                         okey=kq2, ikey=f"PS{pb}")
                self.act(xb[:, 0:n], ps[0:128, 0:n], AF.Copy,
                         scale=self.vcol(scol, l), okey=kxb, ikey=f"PS{pb}")
                self.pp.release(pb)
                self.mm(ssps[0:16, 0:n], ct["bdall"][:, f * 16:(f + 1) * 16],
                        q2[:, 0:n], start=(f == 0), stop=(f == NFT - 1),
                        okey=f"PS{sspb}", ikey=["ca", kq2])
                pr = self.pp.acquire()
                prs = self.pp.aps[pr]
                self.mm(prs[0:128, 0:n], ct["RT_b"], xb[:, 0:n],
                        okey=f"PS{pr}", ikey=["ca", kxb])
                self.tt("vector", q2[:, 0:n], xb[:, 0:n],
                        fb["t_" + tabs[0]][:, 0:n], ALU.mult, okey=kq2,
                        ikey=[kxb, f"ftab_{tabs[0]}"])
                self.stt("vector", t2[:, 0:n], prs[0:128, 0:n], 0.0, ALU.add,
                         fb["t_" + tabs[1]][:, 0:n], ALU.mult, okey=kt2,
                         ikey=[f"PS{pr}", f"ftab_{tabs[1]}"])
                self.pp.release(pr)
                if part == "q":
                    wpar = blkw0 % 2
                    self.tt("vector", fb["qt"][wpar][f][:, 0:n], q2[:, 0:n],
                            t2[:, 0:n], ALU.add, okey=f"fqt{wpar}_{f}",
                            ikey=[kq2, kt2])
                else:
                    self.tt("vector", xb[:, 0:n], q2[:, 0:n],
                            t2[:, 0:n], ALU.add, okey=kxb,
                            ikey=[kq2, kt2])
                    self._krope_pending = True
                    for i in range(n // W):
                        w = blkw0 + i
                        self.copy("vector", fb["kring"][w % 2][f][:, 0:W],
                                  xb[:, i * W:(i + 1) * W],
                                  okey=f"fkr{w%2}_{f}", ikey=kxb)
            self.ts("vector", fb["rn"][0:16, 0:n], ssps[0:16, 0:n], 1e-24, None,
                    ALU.max, okey="frn", ikey=f"PS{sspb}")
            self.pp.release(sspb)
            self.act(fb["rn"][0:16, 0:n], fb["rn"][0:16, 0:n], AF.Ln,
                     okey="frn", ikey="frn")
            self.act(fb["rn"][0:16, 0:n], fb["rn"][0:16, 0:n], AF.Exp, scale=-0.5,
                     okey="frn", ikey="frn")
            if part == "q":
                for i in range(n // 128):
                    tt = (c0 - HALO) // 128 + i
                    pb = self.pp.acquire()
                    ps = self.pp.aps[pb]
                    self.trans(ps[0:128, 0:16], fb["rn"][0:16, i * 128:(i + 1) * 128],
                               ct["iden_f"][0:16, 0:16], okey=f"PS{pb}",
                               ikey=["frn", "ca"])
                    self.copy("vector", fb["rqT"][tt], ps[0:128, 0:16],
                              okey=f"frqT{tt}", ikey=f"PS{pb}")
                    self.pp.release(pb)
            else:
                self.copy("vector", fb["rnb"][0:16, 0:n], fb["rn"][0:16, 0:n],
                          okey="frnb", ikey="frn")
                for f in range(NFT):
                    pb = self.pp.acquire()
                    ps = self.pp.aps[pb]
                    self.mm(ps[0:128, 0:n], ct["hmall"][0:16, f * 128:(f + 1) * 128],
                            fb["rnb"][0:16, 0:n], okey=f"PS{pb}", ikey=["ca", "frnb"])
                    for i in range(n // W):
                        w = blkw0 + i
                        ring = fb["kring"][w % 2][f]
                        self.tt("vector", ring, ring,
                                ps[0:128, i * W:(i + 1) * W], ALU.mult,
                                okey=f"fkr{w%2}_{f}", ikey=[f"fkr{w%2}_{f}", f"PS{pb}"])
                    self.pp.release(pb)
        for i in range(n // 128):
            w = blkw0 + (i // 2)
            vt = fb["vring"][w % 2][i % 2]
            for half in range(2):
                pb = self.pp.acquire()
                ps = self.pp.aps[pb]
                for kt in range(NFT):
                    self.mm(ps, rhs(kt, i * 128, (i + 1) * 128),
                            fb["wqkv"][kt][:, 2 * E + half * 512:2 * E + half * 512 + 512],
                            start=(kt == 0), stop=(kt == NFT - 1),
                            okey=f"PS{pb}", ikey=[f"fstrm{kt}", f"fw{kt}"])
                self.copy("scalar", vt[:, half * 512:half * 512 + 512], ps,
                          okey=f"fv{w%2}_{i%2}", ikey=f"PS{pb}")
                self.pp.release(pb)

    def _attn_window(self, l, w):
        fb = self.fbl
        ct = self.ctab
        wp = w % 2
        for f in range(NFT):
            kp = fb["kring"][(w - 1) % 2][f]
            self.ts("vector", fb["ktw"][f][:, 0:W], kp, self.vcol("skb2"), None,
                    ALU.mult, okey=f"fpk{f}", ikey=[f"fkr{(w-1)%2}_{f}"])
            self.copy("vector", fb["ktw"][f][:, W:2 * W], fb["kring"][w % 2][f],
                      okey=f"fpk{f}", ikey=[f"fkr{w%2}_{f}"])

        def simpart(h):
            f, half = h // 2, h % 2
            pbq = half * 64
            kq = f"fqt{wp}_{f}"
            banks = []
            for rt in range(2):
                pb = self.pp.acquire()
                ps = self.pp.aps[pb]
                mname = f"maskf{rt}" if w == 0 else f"mask{rt}"
                self.mm(ps[0:128, 0:384], ct["iden_b"], ct[mname],
                        start=True, stop=False, okey=f"PS{pb}", ikey=["ca"])
                qlhs = fb["qt"][wp][f][pbq:pbq + 64, rt * 128:rt * 128 + 128]
                self.mm(ps[0:128, 0:384], qlhs,
                        fb["ktw"][f][pbq:pbq + 64, rt * 128:rt * 128 + 384],
                        okey=f"PS{pb}", ikey=[kq, f"fpk{f}"], tp=(pbq, 0),
                        start=False, stop=True)
                banks.append(pb)
            return banks

        def tailpart(h, banks):
            f, half = h // 2, h % 2
            pbq = half * 64
            hp = h % 2
            ssc = fb["cols"][hp]
            kPT = f"fPTs{hp}"
            for rt in range(2):
                pb = banks[rt]
                ps = self.pp.aps[pb]
                kmx = f"fmx{hp}{rt}"
                kbi = f"fbi{hp}{rt}"
                ksm = f"fsm{hp}{rt}"
                kP = f"fP{hp}_{rt}"
                self.red(ssc[:, rt:rt + 1], ps[0:128, 0:384],
                         ALU.max, negate=True, okey=kmx, ikey=f"PS{pb}")
                rq = fb["rqT"][w * 2 + rt][:, h:h + 1]
                self.tt("vector", ssc[:, 2 + rt:3 + rt], ssc[:, rt:rt + 1], rq,
                        ALU.mult, okey=kbi, ikey=[kmx, f"frqT{w*2+rt}"])
                self.act(fb["P"][hp][:, rt * 384:rt * 384 + 384],
                         ps[0:128, 0:384], AF.Exp,
                         bias=ssc[:, 2 + rt:3 + rt], scale=rq,
                         accum=ssc[:, 4 + rt:5 + rt],
                         okey=[kP, ksm], ikey=[f"PS{pb}", kbi, f"frqT{w*2+rt}"])
                self.pp.release(pb)
            self.recip(ssc[:, 6:8], ssc[:, 4:6],
                       okey=f"frc{hp}", ikey=[f"fsm{hp}0", f"fsm{hp}1"])
            for rt in range(2):
                kP = f"fP{hp}_{rt}"
                self.ts("vector", fb["P"][hp][:, rt * 384:rt * 384 + 384],
                        fb["P"][hp][:, rt * 384:rt * 384 + 384],
                        ssc[:, 6 + rt:7 + rt],
                        None, ALU.mult, okey=kP, ikey=[kP, f"frc{hp}"])
            pt = self.pp.acquire()
            ptb = self.pp.aps[pt].bitcast(BF16)
            for rt in range(2):
                kP = f"fP{hp}_{rt}"
                for kt in range(rt, rt + 3):
                    self.trans(ptb[0:128, kt * 256 + rt * 128:kt * 256 + rt * 128 + 128],
                               fb["P"][hp][:, rt * 384 + (kt - rt) * 128:
                                           rt * 384 + (kt - rt) * 128 + 128],
                               ct["iden_b"], okey=f"PS{pt}", ikey=[kP, "ca"])
            self.copy("scalar", fb["PTs"][hp][:, 0:128], ptb[0:128, 0:128],
                      okey=kPT, ikey=f"PS{pt}")
            self.copy("scalar", fb["PTs"][hp][:, 256:768], ptb[0:128, 256:768],
                      okey=kPT, ikey=f"PS{pt}")
            self.copy("scalar", fb["PTs"][hp][:, 896:1024], ptb[0:128, 896:1024],
                      okey=kPT, ikey=f"PS{pt}")
            self.pp.release(pt)
            if half == 0:
                self._opair = self.pp.acquire()
            ob = self._opair
            ops = self.pp.aps[ob]
            for qt_i in range(2):
                valid = [kt for kt in range(4) if qt_i <= kt <= qt_i + 2]
                for j, kt in enumerate(valid):
                    if kt < 2:
                        vt = fb["vring"][(w - 1) % 2][kt]
                        vk = f"fv{(w-1)%2}_{kt}"
                    else:
                        vt = fb["vring"][w % 2][kt - 2]
                        vk = f"fv{w%2}_{kt-2}"
                    self.mm(ops[pbq:pbq + 64, qt_i * 128:qt_i * 128 + 128],
                            vt[:, h * 64:(h + 1) * 64],
                            fb["PTs"][hp][:, kt * 256 + qt_i * 128:
                                          kt * 256 + qt_i * 128 + 128],
                            start=(j == 0), stop=(j == len(valid) - 1),
                            okey=f"PS{ob}", ikey=[vk, kPT], tp=(0, pbq))
            if half == 1:
                self.copy("vector", fb["attnT"][f][:, 0:W], ops[0:128, 0:W],
                          okey=f"fattnT{f}", ikey=f"PS{ob}")
                self.pp.release(ob)

        pending = simpart(0)
        for h in range(H):
            nxt = simpart(h + 1) if h + 1 < H else None
            tailpart(h, pending)
            pending = nxt

    def _attn_wo(self, l, wlist):
        fb = self.fbl
        gcols = len(wlist) * W
        assert len(wlist) == 1
        c0 = HALO + wlist[0] * W
        for f in range(NFT):
            self.copy("vector", fb["oldc"][f], self.rT[f][:, c0 + W - 1:c0 + W],
                      okey=f"foldc{f}", ikey=f"rT{f}")
        for mt in range(NFT):
            pb = self.pp.acquire()
            ps = self.pp.aps[pb]
            for kt in range(NFT):
                self.mm(ps[0:128, 0:gcols], fb["wo"][kt][:, mt * 128:(mt + 1) * 128],
                        fb["attnT"][kt][:, 0:gcols], start=(kt == 0),
                        stop=(kt == NFT - 1), okey=f"PS{pb}",
                        ikey=[f"fwo{kt}", f"fattnT{kt}"])
            self.tt("vector", self.rT[mt][:, c0:c0 + gcols], ps[0:128, 0:gcols],
                    self.rT[mt][:, c0:c0 + gcols], ALU.add,
                    okey=f"rT{mt}", ikey=[f"PS{pb}", f"rT{mt}"])
            self.pp.release(pb)

    def _layer_attn(self, l):
        dd = self.dram
        self._ffn_setup()
        self.work.push()
        fb = self.fbl = self._attn_frame()
        for f in range(NFT):
            self.dma("gpsimd", fb["wqkv"][f],
                     dd["ffn_Wqkv"][l, f * 128:(f + 1) * 128, :], "d_w",
                     okey=f"fw{f}")
            self.dma("gpsimd", fb["wo"][f],
                     dd["ffn_Wo"][l, f * 128:(f + 1) * 128, :], "d_w",
                     okey=f"fwo{f}")
        if l > 0:
            self._halo_in(l, "a")
        self._attn_proj_block(l, 1, 256, True)
        for w in range(8):
            self._attn_proj_block(l, 257 + w * W, 256, False)
            self._attn_window(l, w)
            self._attn_wo(l, [w])
        self._halo_out(l, "g")
        self.work.pop()
        self.p.fence()

    def _layer_glu(self, l):
        dd = self.dram
        ct = self.ctab
        self.work.push()
        fb = self.fbl = self._glu_frame()
        NC = 512
        for f in range(NFT):
            for j in range(3):
                self.copy("vector", fb["oldg"][f][:, j:j + 1],
                          self.rT[f][:, HALO + (j + 1) * NC - 1:HALO + (j + 1) * NC],
                          okey=f"goldg{f}", ikey=f"rT{f}")
        for ci, ch in enumerate([1, 2, 3, 0]):
            c0 = HALO + ch * NC
            if ch == 0:
                self._halo_in(l, "g")
            for f in range(NFT):
                if f < 4:
                    self.copy("vector", fb["rsh"][f], self.rT[f][:, c0:c0 + NC],
                              okey=f"grsh{f}", ikey=f"rT{f}")
                else:
                    if ch == 0:
                        self.copy("vector", fb["rsh"][f][:, 0:1],
                                  self.rT[f][:, c0 - 1:c0], okey=f"grsh{f}",
                                  ikey=f"rT{f}")
                    else:
                        self.copy("vector", fb["rsh"][f][:, 0:1],
                                  fb["oldg"][f][:, ch - 1:ch], okey=f"grsh{f}",
                                  ikey=f"goldg{f}")
                    self.copy("vector", fb["rsh"][f][:, 1:NC],
                              self.rT[f][:, c0:c0 + NC - 1], okey=f"grsh{f}",
                              ikey=f"rT{f}")
            p1 = self.pp.acquire()
            p2 = self.pp.acquire()
            s1 = self.pp.aps[p1]
            s2 = self.pp.aps[p2]
            for f in range(NFT):
                self.act(fb["sq"][f % 2], fb["rsh"][f], AF.Square,
                         okey=f"gsq{f%2}", ikey=f"grsh{f}")
                self.mm(s1[0:1, 0:NC], ct["ones_b"], fb["rsh"][f],
                        start=(f == 0), stop=(f == NFT - 1), okey=f"PS{p1}",
                        ikey=["ca", f"grsh{f}"])
                self.mm(s2[0:1, 0:NC], ct["ones_b"], fb["sq"][f % 2],
                        start=(f == 0), stop=(f == NFT - 1), okey=f"PS{p2}",
                        ikey=["ca", f"gsq{f%2}"])
            mu = fb["srow"][0:1, 0:NC]
            ms = fb["srow"][0:1, 512:512 + NC]
            rst = fb["srow"][0:1, 1024:1024 + NC]
            self.ts("vector", mu, s1[0:1, 0:NC], 1.0 / E, None, ALU.mult,
                    okey="gmu", ikey=f"PS{p1}")
            self.ts("vector", ms, s2[0:1, 0:NC], 1.0 / E, None, ALU.mult,
                    okey="gms", ikey=f"PS{p2}")
            self.pp.release(p1)
            self.pp.release(p2)
            self.tt("vector", rst, mu, mu, ALU.mult, okey="grst", ikey="gmu")
            self.tt("vector", rst, ms, rst, ALU.subtract, okey="grst",
                    ikey=["gms", "grst"])
            self.act(rst, rst, AF.Ln, bias=self.vcol("c_eps")[0:1, :],
                     okey="grst", ikey="grst")
            self.act(rst, rst, AF.Exp, scale=-0.5, okey="grst", ikey="grst")
            pmu = self.pp.acquire()
            prs = self.pp.acquire()
            self.mm(self.pp.aps[pmu][0:128, 0:NC], ct["ones_row"][0:1, :], mu,
                    okey=f"PS{pmu}", ikey=["ca", "gmu"])
            self.mm(self.pp.aps[prs][0:128, 0:NC], ct["ones_row"][0:1, :], rst,
                    okey=f"PS{prs}", ikey=["ca", "grst"])
            for f in range(NFT):
                self.tt("vector", fb["gel"][:, 0:NC], fb["rsh"][f],
                        self.pp.aps[pmu][0:128, 0:NC], ALU.subtract,
                        okey="ggel", ikey=[f"grsh{f}", f"PS{pmu}"])
                self.stt("vector", fb["gel"][:, 0:NC], fb["gel"][:, 0:NC],
                         self.vcol("lng", l * NFT + f), ALU.mult,
                         self.pp.aps[prs][0:128, 0:NC], ALU.mult,
                         okey="ggel", ikey=["ggel", f"PS{prs}"])
                self.ts("vector", fb["xnT"][ci % 2][f], fb["gel"][:, 0:NC],
                        self.vcol("lnb", l * NFT + f), None, ALU.add,
                        okey=f"gxnT{ci%2}_{f}", ikey="ggel")
            self.pp.release(pmu)
            self.pp.release(prs)
            for gi, (t0, t1) in enumerate(self.GRP):
                par = gi % 2
                gc0 = t0 * 128
                gcols = min(t1 * 128, DI) - gc0
                for half in range(2):
                    for f in range(NFT):
                        self.dma("gpsimd", fb["w1g"][par][half][f][:, 0:gcols],
                                 dd["ffn_W1"][l, f * 128:(f + 1) * 128,
                                              half * DI + gc0:half * DI + gc0 + gcols],
                                 "d_w1", okey=f"gw1_{par}_{half}_{f}")
                for ct_i in range(t0, t1):
                    rows = min(128, DI - ct_i * 128)
                    lc = ct_i * 128 - gc0
                    banks = []
                    for half in range(2):
                        pb = self.pp.acquire()
                        ps = self.pp.aps[pb]
                        for kt in range(NFT):
                            self.mm(ps[0:rows, 0:NC],
                                    fb["w1g"][par][half][kt][:, lc:lc + rows],
                                    fb["xnT"][ci % 2][kt], start=(kt == 0),
                                    stop=(kt == NFT - 1), okey=f"PS{pb}",
                                    ikey=[f"gw1_{par}_{half}_{kt}"]
                                    + [f"gxnT{ci%2}_{k}" for k in range(NFT)])
                        banks.append(pb)
                    pa, pb = banks
                    gpar = ct_i % 2
                    gel = fb["gel2"][gpar]
                    self.act(gel[0:rows, 0:NC], self.pp.aps[pb][0:rows, 0:NC],
                             AF.Gelu, bias=self.vcol("b1g", l * 22 + ct_i)[0:rows, :],
                             okey=f"ggel{gpar}", ikey=f"PS{pb}")
                    self.pp.release(pb)
                    self.stt("vector", fb["ht"][ct_i][0:rows, 0:NC],
                             self.pp.aps[pa][0:rows, 0:NC],
                             self.vcol("b1a", l * 22 + ct_i)[0:rows, :],
                             ALU.add, gel[0:rows, 0:NC], ALU.mult,
                             okey=f"ght{ct_i}", ikey=[f"PS{pa}", f"ggel{gpar}"])
                    self.pp.release(pa)
            mtb = [self.pp.acquire() for _ in range(8)]
            for kt in range(22):
                rows = min(128, DI - kt * 128)
                wb = fb["w2t"][kt % 3]
                self.dma("gpsimd", wb[0:rows, :],
                         dd["ffn_W2"][l, kt * 128:kt * 128 + rows, :], "d_w2",
                         okey=f"gw2_{kt%3}")
                for mt in range(NFT):
                    self.mm(self.pp.aps[mtb[mt]][0:128, 0:NC],
                            fb["w2t"][kt % 3][0:rows, mt * 128:(mt + 1) * 128],
                            fb["ht"][kt][0:rows, 0:NC],
                            start=(kt == 0), stop=(kt == 21),
                            okey=f"PS{mtb[mt]}", ikey=[f"gw2_{kt%3}", f"ght{kt}"])
            for mt in range(NFT):
                self.stt("vector", self.rT[mt][:, c0:c0 + NC],
                         self.pp.aps[mtb[mt]][0:128, 0:NC],
                         self.vcol("b2", l * NFT + mt), ALU.add,
                         self.rT[mt][:, c0:c0 + NC], ALU.add,
                         okey=f"rT{mt}", ikey=[f"PS{mtb[mt]}", f"rT{mt}"])
                self.pp.release(mtb[mt])
            if ch == 3 and l < self.cfg["nlayers"] - 1:
                self._halo_out(l, "a")
        self.work.pop()

    def _phase_final(self):
        dd = self.dram
        wk = self.work
        wk.push()
        gl = [wk.alloc(512) for _ in range(2)]
        xt = [wk.alloc(E) for _ in range(2)]
        ot = [wk.alloc(128) for _ in range(2)]
        s = wk.alloc(512)
        last_y = None
        for mt in range(NFT):
            for c in range(4):
                b = c % 2
                self.dma("sync", gl[b],
                         dd["glapark"][mt * 128:(mt + 1) * 128, c * 512:(c + 1) * 512],
                         "d_park2", okey=f"gl{b}", ikey="glapark")
                self.tt("vector", s, self.rT[mt][:, HALO + c * 512:HALO + (c + 1) * 512],
                        gl[b][:, 0:512], ALU.add, okey="s", ikey=[f"rT{mt}", f"gl{b}"])
                for i in range(4):
                    ti = c * 4 + i
                    b2 = ti % 2
                    pb = self.pp.acquire()
                    ps = self.pp.aps[pb]
                    self.trans(ps[0:128, 0:128], s[:, i * 128:(i + 1) * 128],
                               self.ctab["iden_f"], okey=f"PS{pb}", ikey=["s", "CONST"])
                    self.dma("sync", xt[b2],
                             dd["xs"][HALO + ti * 128:HALO + (ti + 1) * 128, :],
                             "d_x2", okey=f"xtf{b2}")
                    self.tt("vector", ot[b2], ps[0:128, 0:128],
                            xt[b2][:, mt * 128:(mt + 1) * 128], ALU.add,
                            okey=f"otf{b2}", ikey=[f"PS{pb}", f"xtf{b2}"])
                    self.pp.release(pb)
                    last_y = self.dma("sync",
                                      dd["y"][ti * 128:(ti + 1) * 128,
                                              mt * 128:(mt + 1) * 128],
                                      ot[b2], "d_y", ikey=f"otf{b2}", okey="y")
        self.p.wait_tok("sync", last_y)
        wk.pop()


def _host_tables(core):
    t0 = core * TC
    half = np.arange(0, D, 2, dtype=np.float32)
    inv_freq = (1.0 / 10000.0 ** (half.astype(np.float64) / D)).astype(np.float32)
    tpos = np.clip(np.arange(t0 - HALO, t0 + TC), 0, T - 1)
    freqs = (tpos[:, None].astype(np.float32) * inv_freq[None, :]).astype(np.float32)
    freqs = np.concatenate([freqs, freqs], -1)
    cos = np.cos(freqs.astype(np.float64))
    sin = np.sin(freqs.astype(np.float64))
    base = (half.astype(np.float64) + 0.4 * D) / (1.4 * D)
    base = np.concatenate([base, base], -1)
    dq = (tpos % W).astype(np.float64)
    Sq = base[None, :] ** (dq[:, None] / (W // 2))
    cosq = (8.0 * cos * Sq).astype(np.float32)
    sinq = (8.0 * sin * Sq).astype(np.float32)
    SkR = base[None, :] ** (-dq[:, None] / (W // 2))
    cosk = (cos * SkR).astype(np.float32)
    sink = (sin * SkR).astype(np.float32)

    def dup(tab):
        return np.concatenate([tab.T, tab.T], 0).astype(np.float32)
    return dup(cosq), dup(sinq), dup(cosk), dup(sink)


def _host_masks(core):
    def build(rt, first):
        aq = rt * 128 + np.arange(128)[:, None]
        kb = rt * 128 + np.arange(384)[None, :]
        keep = (kb >= aq) & (kb <= aq + 256)
        if first:
            keep = keep & (kb >= 256)
        return np.where(keep, 0.0, NEG).astype(np.float32)
    m0, m1 = build(0, False), build(1, False)
    if core == 0:
        return m0, m1, build(0, True), build(1, True)
    return m0, m1, m0, m1


def _host_consts(core):
    iden = np.eye(128, dtype=np.float32)
    R = np.zeros((128, 128), np.float32)
    for hb in range(2):
        o = hb * 64
        for i in range(32):
            R[o + i, o + 32 + i] = -1.0
            R[o + 32 + i, o + i] = 1.0
    RT = np.ascontiguousarray(R.T)
    bdall = np.zeros((128, 128), np.float32)
    hmall = np.zeros((128, 1024), np.float32)
    for f in range(NFT):
        for m in range(16):
            p0 = 64 * (m - 2 * f) if m in (2 * f, 2 * f + 1) else None
            if p0 is not None:
                bdall[p0:p0 + 64, f * 16 + m] = 1.0
        for m in range(128):
            hmall[2 * f + (m >= 64), f * 128 + m] = 1.0
    ones_b = np.ones((128, 1), np.float32)
    zrow_b = np.zeros((128, 128), np.float32)
    ones_row = np.zeros((128, 128), np.float32)
    ones_row[0, :] = 1.0
    triu = np.triu(np.ones((128, 128), np.float32))
    m0, m1, mf0, mf1 = _host_masks(core)
    cosq, sinq, cosk, sink = _host_tables(core)
    vals = {"iden_f": iden, "iden_b": iden, "RT_b": RT, "bdall": bdall,
            "hmall": hmall, "ones_b": ones_b, "ones_row": ones_row,
            "triu_b": triu, "zrow_b": zrow_b,
            "mask0": m0, "mask1": m1, "maskf0": mf0, "maskf1": mf1,
            "cosq": cosq, "sinq": sinq, "cosk": cosk, "sink": sink}
    cols = []
    for nm, ncols, _ in CONST_LAYOUT + TAB_LAYOUT:
        a = vals[nm].astype(np.float32)
        assert a.shape[1] == ncols, (nm, a.shape, ncols)
        if a.shape[0] < 128:
            a = np.concatenate([a, np.zeros((128 - a.shape[0], ncols), np.float32)], 0)
        cols.append(a)
    return np.ascontiguousarray(np.concatenate(cols, 1))


def _host_vecs(inputs, core):
    v = np.zeros((128, VC_COLS), np.float32)

    def put(name, i, col):
        v[:, VC[name] + i] = col
    put("delta", 0, 0)
    delta = np.asarray(inputs["L0_attn_delta"], np.float32).reshape(NFT, 128)
    for f in range(NFT):
        put("delta", f, delta[f])
    nw = np.asarray(inputs["L0_attn_norm_w"], np.float32).reshape(NFT, 128)
    for f in range(NFT):
        put("normw", f, nw[f])
    put("halomul", 0, 0.0 if core == 0 else 1.0)
    qs = np.asarray(inputs["L0_ffn_q_scale"], np.float32)
    ks = np.asarray(inputs["L0_ffn_k_scale"], np.float32)
    for l in range(DEPTH):
        put("qs", l, np.tile(qs[l], 2))
        put("ks", l, np.tile(ks[l], 2))
        lng = np.asarray(inputs["L0_ffn_ln_g"], np.float32)[l].reshape(NFT, 128)
        lnb = np.asarray(inputs["L0_ffn_ln_b"], np.float32)[l].reshape(NFT, 128)
        b2 = np.asarray(inputs["L0_ffn_b2"], np.float32)[l].reshape(NFT, 128)
        for f in range(NFT):
            put("lng", l * NFT + f, lng[f])
            put("lnb", l * NFT + f, lnb[f])
            put("b2", l * NFT + f, b2[f])
        b1 = np.asarray(inputs["L0_ffn_b1"], np.float32)[l]
        b1a, b1g = b1[:DI], b1[DI:]
        for ct in range(22):
            rows = min(128, DI - ct * 128)
            col = np.zeros(128, np.float32)
            col[:rows] = b1a[ct * 128:ct * 128 + rows]
            put("b1a", l * 22 + ct, col)
            col = np.zeros(128, np.float32)
            col[:rows] = b1g[ct * 128:ct * 128 + rows]
            put("b1g", l * 22 + ct, col)
    for r in range(NCORES):
        m = 1.0 if r < core else 0.0
        put("rkM", r, m)
        put("rkI", r, 1.0 - m)
    half = np.arange(0, 64, 2, dtype=np.float64)
    base = (half + 0.4 * 64) / (1.4 * 64)
    base = np.concatenate([base, base])
    put("skb2", 0, np.tile((base ** 2).astype(np.float32), 2))
    put("c_eps", 0, 1e-5)
    put("c_one", 0, 1.0)
    put("c_epsD", 0, 1e-5 * D)
    put("c_hlnD", 0, 0.5 * math.log(D))
    return v


def _host_hidx(core):
    idx = np.zeros((128, NFT), np.uint32)
    src = core - 1 if core > 0 else 0
    for f in range(NFT):
        idx[:, f] = src * E + f * 128 + np.arange(128)
    return idx


_NC_CACHE = {}


def _get_nc():
    key = tuple(sorted(CFG.items()))
    if key not in _NC_CACHE:
        _NC_CACHE[key] = Build(dict(CFG)).build()
    return _NC_CACHE[key]


def kernel(**inputs):
    nc = _get_nc()
    x = np.asarray(inputs["L0_attn_x"], np.float32)[0]
    in_maps = []
    for c in range(NCORES):
        s0 = c * TC
        xs = np.zeros((EXT, E), np.float32)
        lo = max(0, s0 - HALO)
        xs[HALO - (s0 - lo):HALO] = x[lo:s0]
        xs[HALO:] = x[s0:s0 + TC]
        m = {"xs": np.ascontiguousarray(xs),
             "vecs": _host_vecs(inputs, c),
             "consts": _host_consts(c),
             "hidx": _host_hidx(c)}
        for nm in ["attn_Wq", "attn_Wk", "attn_Wv", "attn_Wo", "attn_G1", "attn_G2",
                   "ffn_Wqkv", "ffn_Wo", "ffn_W1", "ffn_W2"]:
            m[nm] = np.ascontiguousarray(np.asarray(inputs["L0_" + nm], np.float32))
        in_maps.append(m)
    res = run_bass_kernel_spmd(nc, in_maps, core_ids=list(range(NCORES)))
    out = np.concatenate([res.results[c]["y"] for c in range(NCORES)], 0)
    return np.ascontiguousarray(out[None])
